# version 04
# baseline (speedup 1.0000x reference)
import numpy as np
from contextlib import ExitStack
import concourse.bass as bass
import concourse.mybir as mybir
from concourse.bass_utils import run_bass_kernel_spmd

F32, BF16 = mybir.dt.float32, mybir.dt.bfloat16
AF, ALU, AX = mybir.ActivationFunctionType, mybir.AluOpType, mybir.AxisListType
ENGS = ("sync", "tensor", "vector", "scalar", "gpsimd")


def bc(tile_ap, n):
    a = tile_ap
    return bass.AP(a.tensor, a.offset, [list(a.ap[0]), [0, n]])


class Sem:
    __slots__ = ("h", "n")
    def __init__(self, h):
        self.h = h
        self.n = 0


class P:
    def __init__(self, nc, ctx):
        self.nc = nc
        self.ctx = ctx
        self.q = {e: [] for e in ENGS}
        self.all_sems = []
        self._scopes = []

    def push_scope(self):
        from contextlib import ExitStack
        self._scopes.append(ExitStack())

    def pop_scope(self):
        self._scopes.pop().close()

    def _actx(self):
        return self._scopes[-1] if self._scopes else self.ctx

    def sem(self, name):
        s = Sem(self.ctx.enter_context(self.nc.semaphore(name)))
        self.all_sems.append(s)
        return s

    def sb(self, name, shape, dt=F32):
        return self._actx().enter_context(self.nc.sbuf_tensor(name, shape, dt))

    def ps(self, name, shape, dt=F32):
        return self._actx().enter_context(self.nc.psum_tensor(name, shape, dt))

    def barrier(self):
        snap = [(s, s.n) for s in self.all_sems if s.n > 0]
        for eng in ENGS:
            self.wait(eng, *snap)

    @staticmethod
    def _flat(waits):
        out = []
        for x in waits:
            if x is None:
                continue
            if isinstance(x, list):
                out.extend((s.h, v) for (s, v) in x if x is not None)
            else:
                s, v = x
                out.append((s.h, v))
        return out

    def op(self, eng, fn, waits=(), inc=None, inck=1):
        w = self._flat(waits)
        if inc is not None and inck >= 16 and inc.n > 0:
            w.append((inc.h, inc.n))
        tgt = None if inc is None else inc.h
        if inc is not None:
            inc.n += inck
        post = None if inc is None else inc.n
        def g(e, fn=fn, w=w, tgt=tgt, inck=inck):
            for h, v in w:
                e.wait_ge(h, v)
            ins = fn(e)
            if tgt is not None:
                ins.then_inc(tgt, inck)
        self.q[eng].append(g)
        return post

    def wait(self, eng, *waits):
        w = self._flat(waits)
        def g(e, w=w):
            for h, v in w:
                e.wait_ge(h, v)
        self.q[eng].append(g)

    def emit(self):
        with self.nc.Block() as block:
            for eng in ENGS:
                if self.q[eng]:
                    getattr(block, eng)(lambda e, eng=eng: [f(e) for f in self.q[eng]])


D = 2048
NCH = D // 128


def plan_ln_router(p, x_hbm, gwT_hbm, xn_hbm, dwT_hbm, ident_hbm, mode, E, T=2048,
                   name="", do_tr=True, do_router=True):
    n = name
    NT = T // 128
    x_tm = [p.sb(f"{n}x{i}", [128, D]) for i in range(2)]
    xn_tm = [p.sb(f"{n}xn{i}", [128, D]) for i in range(2)]
    xnf = p.sb(f"{n}xnf", [128, D])
    stage = [p.sb(f"{n}stg{i}", [128, D], BF16) for i in range(2)]
    junk = p.sb(f"{n}junk", [128, D], BF16)
    s1 = [p.sb(f"{n}s1{i}", [128, 1]) for i in range(2)]
    mu = [p.sb(f"{n}mu{i}", [128, 1]) for i in range(2)]
    mu2 = [p.sb(f"{n}mu2{i}", [128, 1]) for i in range(2)]
    var = [p.sb(f"{n}var{i}", [128, 1]) for i in range(2)]
    rstd = [p.sb(f"{n}rstd{i}", [128, 1]) for i in range(2)]
    ssq = [p.sb(f"{n}ssq{i}", [128, 1]) for i in range(2)]
    eps = p.sb(f"{n}eps", [128, 1])
    gwT = p.sb(f"{n}gwT", [128, NCH * E])
    ident = p.sb(f"{n}id", [128, 128])
    dwT_sb = p.sb(f"{n}dwT", [E, T])
    z_sb = p.sb(f"{n}z", [128, E])
    m12 = p.sb(f"{n}m12", [128, 2])
    eq1 = p.sb(f"{n}eq1", [128, E])
    eq2 = p.sb(f"{n}eq2", [128, E])
    vz = p.sb(f"{n}vz", [128, E])
    den = p.sb(f"{n}den", [128, 1])
    rden = p.sb(f"{n}rden", [128, 1])
    w12 = p.sb(f"{n}w12", [128, 2])
    t1 = p.sb(f"{n}t1", [128, E])
    ez = p.sb(f"{n}ez", [128, E])
    psm = p.sb(f"{n}psm", [128, 1])
    pst = [p.ps(f"{n}pst{i}", [128, 512]) for i in range(4)]
    z_ps = p.ps(f"{n}zps", [128, 512])
    dw_ps = p.ps(f"{n}dwps", [128, 512])
    ld = p.sem(f"{n}ld")
    va = p.sem(f"{n}va")
    sa = p.sem(f"{n}sa")
    mm = p.sem(f"{n}mm")
    mz = p.sem(f"{n}mz")
    so = p.sem(f"{n}so")

    p.op("sync", lambda e: e.dma_start(
        out=gwT.ap().rearrange("p (c e) -> p c e", e=E),
        in_=gwT_hbm.rearrange("(c p) e -> p c e", p=128)), inc=ld, inck=16)
    p.op("sync", lambda e: e.dma_start(out=ident[:], in_=ident_hbm), inc=ld, inck=16)
    p.op("vector", lambda e: e.memset(eps[:], 1e-5), inc=va)
    ld_base = ld.n
    consts_v = (ld, ld_base)
    eps_v = (va, va.n)

    x_free = [None, None]
    xn_free = [None, None]
    gate = None
    stage_free = [None, None]
    bank_free = [None] * 4
    xnf_free = [None] * NCH
    stage_av = [None] * NCH
    zps_free = None
    dwps_free = None

    for tc in range(NT):
        bi = tc % 2
        w = [consts_v] if tc < 2 else []
        if x_free[bi] is not None:
            w.append(x_free[bi])
        ldv = p.op("sync", lambda e, tc=tc, bi=bi: e.dma_start(
            out=x_tm[bi][:], in_=x_hbm[tc * 128:(tc + 1) * 128, :]), waits=w,
            inc=ld, inck=16)
        xv = (ld, ldv)
        sqv = p.op("scalar", lambda e, bi=bi: e.activation(
            junk[:], x_tm[bi][:], AF.Square, accum_out=ssq[bi][:]),
            waits=[xv, gate], inc=sa)
        s1v = p.op("vector", lambda e, bi=bi: e.reduce_sum(
            s1[bi][:], x_tm[bi][:], axis=AX.X), waits=[xv], inc=va)
        muv = p.op("vector", lambda e, bi=bi: e.tensor_scalar_mul(
            mu[bi][:], s1[bi][:], 1.0 / D), waits=[(va, s1v)], inc=va)
        mu2v = p.op("vector", lambda e, bi=bi: e.tensor_mul(
            mu2[bi][:], mu[bi][:], mu[bi][:]), waits=[(va, muv)], inc=va)
        varv = p.op("vector", lambda e, bi=bi: e.scalar_tensor_tensor(
            var[bi][:], ssq[bi][:], 1.0 / D, mu2[bi][:], ALU.mult, ALU.subtract),
            waits=[(sa, sqv), (va, mu2v)], inc=va)
        lnv = p.op("scalar", lambda e, bi=bi: e.activation(
            rstd[bi][:], var[bi][:], AF.Ln, bias=eps[:]),
            waits=[(va, varv), eps_v, gate], inc=sa)
        rsv = p.op("scalar", lambda e, bi=bi: e.activation(
            rstd[bi][:], rstd[bi][:], AF.Exp, scale=-0.5), waits=[(sa, lnv)], inc=sa)
        w = [(sa, rsv), (va, muv)]
        if xn_free[bi] is not None:
            w.append(xn_free[bi])
        xnv = p.op("vector", lambda e, bi=bi: e.tensor_scalar(
            xn_tm[bi][:], x_tm[bi][:], mu[bi][:], rstd[bi][:], ALU.subtract, ALU.mult),
            waits=w, inc=va)
        x_free[bi] = (va, xnv)
        if not do_tr:
            gate = None
            continue
        ev_d = [None] * NCH
        for k in range(NCH):
            b = k % 4
            w = [(va, xnv)] if k < 4 else [(va, xnv)]
            if bank_free[b] is not None:
                w.append(bank_free[b])
            tv = p.op("tensor", lambda e, k=k, bi=bi, b=b: e.transpose(
                pst[b][:, 0:128], xn_tm[bi][:, k * 128:(k + 1) * 128], ident[:]),
                waits=w, inc=mm)
            w = [(mm, tv)]
            if xnf_free[k] is not None:
                w.append(xnf_free[k])
            dv = p.op("vector", lambda e, k=k, b=b: e.tensor_copy(
                xnf[:, k * 128:(k + 1) * 128], pst[b][:, 0:128]), waits=w, inc=va)
            w2 = [(va, dv)]
            if k == 0 and stage_free[bi] is not None:
                w2.append(stage_free[bi])
            av = p.op("scalar", lambda e, k=k, b=b, bi=bi: e.copy(
                stage[bi][:, k * 128:(k + 1) * 128], xnf[:, k * 128:(k + 1) * 128]),
                waits=w2, inc=sa)
            stage_av[k] = (sa, av)
            bank_free[b] = [(va, dv)]
            ev_d[k] = (va, dv)
        xn_free[bi] = (mm, mm.n)
        sv = p.op("gpsimd", lambda e, bi=bi, tc=tc: e.dma_start(
            out=xn_hbm.rearrange("(c p) t -> p c t", p=128)[:, :, tc * 128:(tc + 1) * 128],
            in_=stage[bi].ap().rearrange("p (c t) -> p c t", t=128)),
            waits=[(sa, sa.n)], inc=so, inck=16)
        stage_free[bi] = (so, sv)
        if not do_router:
            gate = None
            continue
        for k in range(NCH):
            w = [ev_d[k]]
            if k == 0 and zps_free is not None:
                w.append(zps_free)
            if k == 0:
                w.append(consts_v)
            zv = p.op("tensor", lambda e, k=k: e.matmul(
                z_ps[:, 0:E], xnf[:, k * 128:(k + 1) * 128], gwT[:, k * E:(k + 1) * E],
                start=(k == 0), stop=(k == NCH - 1)), waits=w, inc=mz)
            xnf_free[k] = [(mz, zv), stage_av[k]]
        zmm = (mz, mz.n)
        zc = p.op("vector", lambda e: e.tensor_copy(z_sb[:], z_ps[:, 0:E]),
                  waits=[zmm, gate], inc=va)
        zps_free = (va, zc)
        if mode == "l1":
            ngv = p.op("vector", lambda e: e.reduce_max(
                psm[:], z_sb[:], axis=AX.X, negate=True), waits=[(va, zc)], inc=va)
            exv = p.op("scalar", lambda e: e.activation(
                ez[:], z_sb[:], AF.Exp, bias=psm[:], accum_out=den[:]),
                waits=[(va, ngv)], inc=sa)
            rdv = p.op("vector", lambda e: e.reciprocal(rden[:], den[:]),
                       waits=[(sa, exv)], inc=va)
            pv = p.op("vector", lambda e: e.tensor_scalar_mul(
                z_sb[:], ez[:], rden[:]), waits=[(va, rdv)], inc=va)
            topin_v = (va, pv)
        else:
            topin_v = (va, zc)
        m1v = p.op("vector", lambda e: e.reduce_max(m12[:, 0:1], z_sb[:], axis=AX.X),
                   waits=[topin_v], inc=va)
        e1v = p.op("vector", lambda e: e.tensor_scalar(
            eq1[:], z_sb[:], m12[:, 0:1], None, ALU.is_equal), waits=[(va, m1v)], inc=va)
        vzv = p.op("vector", lambda e: e.scalar_tensor_tensor(
            vz[:], eq1[:], -1e30, z_sb[:], ALU.mult, ALU.add), waits=[(va, e1v)], inc=va)
        m2v = p.op("vector", lambda e: e.reduce_max(m12[:, 1:2], vz[:], axis=AX.X),
                   waits=[(va, vzv)], inc=va)
        e2v = p.op("vector", lambda e: e.tensor_scalar(
            eq2[:], vz[:], m12[:, 1:2], None, ALU.is_equal), waits=[(va, m2v)], inc=va)
        if mode == "l0":
            sgv = p.op("scalar", lambda e: e.activation(
                w12[:], m12[:], AF.Sigmoid), waits=[(va, m2v)], inc=sa)
            dnv = p.op("vector", lambda e: e.reduce_sum(den[:], w12[:], axis=AX.X),
                       waits=[(sa, sgv)], inc=va)
            rdv = p.op("vector", lambda e: e.reciprocal(rden[:], den[:]),
                       waits=[(va, dnv)], inc=va)
            wv = p.op("vector", lambda e: e.tensor_scalar_mul(
                w12[:], w12[:], rden[:]), waits=[(va, rdv), (sa, sgv)], inc=va)
            w_v = (va, wv)
        else:
            wv = p.op("vector", lambda e: e.tensor_copy(w12[:], m12[:]),
                      waits=[(va, m2v)], inc=va)
            w_v = (va, wv)
        t1v = p.op("vector", lambda e: e.tensor_scalar_mul(
            t1[:], eq1[:], w12[:, 0:1]), waits=[w_v, (va, e1v)], inc=va)
        dwv = p.op("vector", lambda e: e.scalar_tensor_tensor(
            vz[:], eq2[:], w12[:, 1:2], t1[:], ALU.mult, ALU.add),
            waits=[(va, t1v), (va, e2v)], inc=va)
        w = [(va, dwv), consts_v]
        if dwps_free is not None:
            w.append(dwps_free)
        dtv = p.op("tensor", lambda e: e.transpose(
            dw_ps[0:E, 0:128], vz[:, 0:E], ident[:]), waits=w, inc=mz)
        dcv = p.op("vector", lambda e, tc=tc: e.tensor_copy(
            dwT_sb[:, tc * 128:(tc + 1) * 128], dw_ps[0:E, 0:128]),
            waits=[(mz, dtv)], inc=va)
        dwps_free = (va, dcv)
        gate = [(va, dwv), (sa, sa.n)]

    dsv = p.op("gpsimd", lambda e: e.dma_start(out=dwT_hbm[:, 0:T], in_=dwT_sb[:]),
               waits=[(va, va.n)], inc=so, inck=16)
    return so


D = 2048
NDC = 16
F_E = 1408
F_S = 2816
NOC = 16


def plan_moe(p, xn_hbm, dwT_hbm, w1t, w3t, w2t, sw1t, sw3t, sw2t, g_hbm,
             E=16, T=2048, TT=512, name="", apply_silu=True, parts=("bc","A","B","g"), nexp=16, t_off=0):
    n = name
    NTT = T // TT
    NF_E, NF_S = F_E // 128, F_S // 128
    xn_sb = p.sb(f"{n}xn", [128, NDC * TT], BF16)
    dw_row = p.sb(f"{n}dwr", [1, TT])
    ones = p.sb(f"{n}ones", [1, 128])
    dwbc = p.sb(f"{n}dwbc", [128, TT])
    w1b = [p.sb(f"{n}w1b{i}", [128, NDC * 128], BF16) for i in range(2)]
    w3b = [p.sb(f"{n}w3b{i}", [128, NDC * 128], BF16) for i in range(2)]
    w2b = [p.sb(f"{n}w2b{i}", [128, NF_S * 128], BF16) for i in range(2)]
    sa_sb = p.sb(f"{n}sa", [128, TT], BF16)
    tmp_sb = p.sb(f"{n}tmp", [128, TT], BF16)
    h_sb = p.sb(f"{n}h", [128, NF_S * TT], BF16)
    y_sb = p.sb(f"{n}y", [128, NOC * TT])
    g_st = [p.sb(f"{n}g{i}", [128, TT]) for i in range(2)]
    a_ps = p.ps(f"{n}aps", [128, TT])
    b_ps = p.ps(f"{n}bps", [128, TT])
    y_ps = [p.ps(f"{n}yps{i}", [128, TT]) for i in range(2)]
    bc_ps = p.ps(f"{n}bcps", [128, TT])
    ld = p.sem(f"{n}ld")
    w1s = p.sem(f"{n}w1s")
    w3s = p.sem(f"{n}w3s")
    w2s = p.sem(f"{n}w2s")
    mm = p.sem(f"{n}mm")
    va = p.sem(f"{n}va")
    sa = p.sem(f"{n}sa")
    so = p.sem(f"{n}so")

    p.op("vector", lambda e: e.memset(ones[:], 1.0), inc=va)
    consts = [(va, va.n)]
    dwrow_free = None

    xn_free = None
    w1_free = [None, None]; w3_free = [None, None]; w2_free = [None, None]
    aps_free = None; bps_free = None; bc_free = None
    yps_free = [None, None]
    h_free = [None] * NF_S
    y_last = [None] * NOC
    g_free = [None, None]
    sa_free = None; tmp_free = None; dwbc_free = None
    wq13 = 0
    wq2 = 0

    for tt in range(NTT):
        w = [xn_free] if xn_free else []
        xv = p.op("sync", lambda e, tt=tt: e.dma_start(
            out=xn_sb.ap().rearrange("p (c t) -> p c t", t=TT),
            in_=xn_hbm.rearrange("(c p) t -> p c t", p=128)[:, :, t_off + tt * TT:t_off + (tt + 1) * TT]),
            waits=w, inc=ld, inck=16)
        xnv = (ld, xv)
        xn_readers = []

        experts = [("sh", None)] + [("ex", ei) for ei in range(nexp)]
        for kind, ei in experts:
            shared = kind == "sh"
            NF = NF_S if shared else NF_E
            if not shared and "bc" in parts:
                w = [dwrow_free] if dwrow_free else []
                dv_ = p.op("sync", lambda e, ei=ei, tt=tt: e.dma_start(
                    out=dw_row[:], in_=dwT_hbm[ei:ei + 1, t_off + tt * TT:t_off + (tt + 1) * TT]),
                    waits=w, inc=ld, inck=16)
                w = [consts[0], (ld, dv_)]
                if bc_free:
                    w.append(bc_free)
                bv = p.op("tensor", lambda e: e.matmul(
                    bc_ps[:], ones[:], dw_row[:],
                    start=True, stop=True), waits=w, inc=mm)
                dwrow_free = (mm, bv)
                w = [(mm, bv)]
                if dwbc_free:
                    w.append(dwbc_free)
                bcv = p.op("vector", lambda e: e.tensor_copy(dwbc[:], bc_ps[:]),
                           waits=w, inc=va)
                bc_free = (va, bcv)
                dwbc_v = (va, bcv)
            if "A" not in parts:
                continue
            for fc in range(NF):
                bi = wq13 % 2
                wq13 += 1
                w = [w1_free[bi]] if w1_free[bi] else []
                src1 = (sw1t if shared else w1t[ei]) if True else None
                l1 = p.op("sync", lambda e, fc=fc, bi=bi, src=(sw1t if shared else w1t[ei]): e.dma_start(
                    out=w1b[bi].ap().rearrange("p (c f) -> p c f", f=128),
                    in_=src.rearrange("(c p) f -> p c f", p=128)[:, :, fc * 128:(fc + 1) * 128]),
                    waits=w, inc=w1s, inck=16)
                w = [w3_free[bi]] if w3_free[bi] else []
                l3 = p.op("gpsimd", lambda e, fc=fc, bi=bi, src=(sw3t if shared else w3t[ei]): e.dma_start(
                    out=w3b[bi].ap().rearrange("p (c f) -> p c f", f=128),
                    in_=src.rearrange("(c p) f -> p c f", p=128)[:, :, fc * 128:(fc + 1) * 128]),
                    waits=w, inc=w3s, inck=16)
                for dc in range(NDC):
                    w = [(w1s, l1), xnv]
                    if dc == 0 and aps_free:
                        w.append(aps_free)
                    av_ = p.op("tensor", lambda e, dc=dc, bi=bi: e.matmul(
                        a_ps[:], w1b[bi][:, dc * 128:(dc + 1) * 128],
                        xn_sb[:, dc * TT:(dc + 1) * TT],
                        start=(dc == 0), stop=(dc == NDC - 1)), waits=w, inc=mm)
                amm = (mm, av_)
                for dc in range(NDC):
                    w = [(w3s, l3), xnv]
                    if dc == 0 and bps_free:
                        w.append(bps_free)
                    bv_ = p.op("tensor", lambda e, dc=dc, bi=bi: e.matmul(
                        b_ps[:], w3b[bi][:, dc * 128:(dc + 1) * 128],
                        xn_sb[:, dc * TT:(dc + 1) * TT],
                        start=(dc == 0), stop=(dc == NDC - 1)), waits=w, inc=mm)
                bmm = (mm, bv_)
                xn_readers.append((mm, bv_))
                w1_free[bi] = amm
                w3_free[bi] = bmm
                w = [amm]
                if sa_free:
                    w.append(sa_free)
                sv = p.op("scalar", lambda e: e.activation(sa_sb[:], a_ps[:], AF.Silu),
                          waits=w, inc=sa)
                aps_free = (sa, sv)
                if shared:
                    w = [bmm, (sa, sv)]
                    if h_free[fc]:
                        w.append(h_free[fc])
                    hv = p.op("vector", lambda e, fc=fc: e.tensor_mul(
                        h_sb[:, fc * TT:(fc + 1) * TT], sa_sb[:], b_ps[:]),
                        waits=w, inc=va)
                    bps_free = (va, hv)
                else:
                    w = [bmm, dwbc_v] if "bc" in parts else [bmm]
                    if tmp_free:
                        w.append(tmp_free)
                    tv = p.op("vector", lambda e: e.tensor_mul(
                        tmp_sb[:], b_ps[:], dwbc[:]), waits=w, inc=va)
                    bps_free = (va, tv)
                    w = [(va, tv), (sa, sv)]
                    if h_free[fc]:
                        w.append(h_free[fc])
                    hv = p.op("vector", lambda e, fc=fc: e.tensor_mul(
                        h_sb[:, fc * TT:(fc + 1) * TT], sa_sb[:], tmp_sb[:]),
                        waits=w, inc=va)
                    tmp_free = (va, hv)
                sa_free = (va, hv)
                h_free[fc] = (va, hv)
                h_done = (va, hv)
                h_free[fc] = h_done
            h_all = (va, va.n)
            if "B" not in parts:
                continue
            for oc in range(NOC):
                bi = wq2 % 2
                wq2 += 1
                w = [w2_free[bi]] if w2_free[bi] else []
                l2 = p.op("sync", lambda e, oc=oc, bi=bi, src=(sw2t if shared else w2t[ei]), NF=NF: e.dma_start(
                    out=w2b[bi].ap().rearrange("p (c o) -> p c o", o=128)[:, 0:NF, :],
                    in_=src.rearrange("(c p) o -> p c o", p=128)[:, :, oc * 128:(oc + 1) * 128]),
                    waits=w, inc=w2s, inck=16)
                yb = oc % 2
                for fc in range(NF):
                    w = [(w2s, l2), h_all]
                    if fc == 0 and yps_free[yb]:
                        w.append(yps_free[yb])
                    yv = p.op("tensor", lambda e, fc=fc, oc=oc, bi=bi, yb=yb, NF=NF: e.matmul(
                        y_ps[yb][:], w2b[bi][:, fc * 128:(fc + 1) * 128],
                        h_sb[:, fc * TT:(fc + 1) * TT],
                        start=(fc == 0), stop=(fc == NF - 1)), waits=w, inc=mm)
                ymm = (mm, yv)
                w2_free[bi] = ymm
                for fc in range(NF):
                    h_free[fc] = ymm
                w = [ymm]
                if y_last[oc]:
                    w.append(y_last[oc])
                if shared:
                    cv = p.op("vector", lambda e, oc=oc, yb=yb: e.tensor_copy(
                        y_sb[:, oc * TT:(oc + 1) * TT], y_ps[yb][:]), waits=w, inc=va)
                else:
                    cv = p.op("vector", lambda e, oc=oc, yb=yb: e.tensor_add(
                        y_sb[:, oc * TT:(oc + 1) * TT],
                        y_sb[:, oc * TT:(oc + 1) * TT], y_ps[yb][:]), waits=w, inc=va)
                y_last[oc] = (va, cv)
                yps_free[yb] = (va, cv)
        if "g" not in parts or "B" not in parts:
            xn_free = [(mm, mm.n)]
            continue
        for oc in range(NOC):
            gb = oc % 2
            w = [y_last[oc]]
            if g_free[gb]:
                w.append(g_free[gb])
            if apply_silu:
                gv = p.op("scalar", lambda e, oc=oc, gb=gb: e.activation(
                    g_st[gb][:], y_sb[:, oc * TT:(oc + 1) * TT], AF.Silu),
                    waits=w, inc=sa)
            else:
                gv = p.op("scalar", lambda e, oc=oc, gb=gb: e.copy(
                    g_st[gb][:], y_sb[:, oc * TT:(oc + 1) * TT]), waits=w, inc=sa)
            sv = p.op("gpsimd", lambda e, oc=oc, gb=gb, tt=tt: e.dma_start(
                out=g_hbm[oc * 128:(oc + 1) * 128, tt * TT:(tt + 1) * TT],
                in_=g_st[gb][:]), waits=[(sa, gv)], inc=so, inck=16)
            g_free[gb] = (so, sv)
            y_last[oc] = [(so, sv), (sa, gv)]
        xn_free = [(mm, mm.n)]
    return so


D = 2048
NDC = 16


def plan_l0proj(p, xn_hbm, W, B, QT_hbm, KT_hbm, VT_hbm, cos_hbm, sin_hbm, ident_hbm,
                q_tiles, kv_tiles, pos_off=0, name=""):
    n = name
    TT = 512
    cqT = p.sb(f"{n}cqT", [128, NDC * 512], BF16)
    ckvT = p.sb(f"{n}ckvT", [128, NDC * 512], BF16)
    cqrT = p.sb(f"{n}cqrT", [128, NDC * 128], BF16)
    ckrT = p.sb(f"{n}ckrT", [128, NDC * 128], BF16)
    cqcT = p.sb(f"{n}cqcT", [128, 4 * 384], BF16)
    ckcT = p.sb(f"{n}ckcT", [128, 4 * 384], BF16)
    cvT = p.sb(f"{n}cvT", [128, 4 * 512], BF16)
    bias = p.sb(f"{n}bias", [128, 4 * 7])
    ident_b = p.sb(f"{n}idb", [128, 128], BF16)
    xn_sb = [p.sb(f"{n}xn{i}", [128, NDC * TT], BF16) for i in range(2)]
    cq_sb = p.sb(f"{n}cq", [128, 4 * TT], BF16)
    ckv_sb = p.sb(f"{n}ckv", [128, 4 * TT], BF16)
    r_f32 = p.sb(f"{n}rf", [128, TT])
    rt_f32 = p.sb(f"{n}rt", [128, TT])
    t1_f32 = p.sb(f"{n}t1", [128, TT])
    rs_f32 = p.sb(f"{n}rs", [128, TT])
    cos_sb = p.sb(f"{n}cos", [128, 4096])
    sin_sb = p.sb(f"{n}sin", [128, 4096])
    q_st = [p.sb(f"{n}qst{i}", [128, 4 * TT], BF16) for i in range(2)]
    k_st = [p.sb(f"{n}kst{i}", [128, 4 * TT], BF16) for i in range(2)]
    v_fm = p.sb(f"{n}vfm", [128, TT], BF16)
    v_st = [p.sb(f"{n}vst{i}", [128, 4 * TT], BF16) for i in range(2)]
    pp = [p.ps(f"{n}pp{i}", [128, 512]) for i in range(2)]
    pt = [p.ps(f"{n}pt{i}", [128, 1024], BF16) for i in range(2)]
    wld = p.sem(f"{n}wld")
    ld = p.sem(f"{n}ld")
    mm = p.sem(f"{n}mm")
    va = p.sem(f"{n}va")
    sa = p.sem(f"{n}sa")
    so = p.sem(f"{n}so")

    def wload(dst, src, ochunks, osz):
        return p.op("gpsimd", lambda e, dst=dst, src=src: e.dma_start(
            out=dst.ap().rearrange("p (c o) -> p c o", o=osz),
            in_=src.rearrange("(c p) o -> p c o", p=128)), inc=wld, inck=16)
    wload(cqT, W["cqT"], NDC, 512)
    wload(ckvT, W["ckvT"], NDC, 512)
    wload(cqrT, W["cqrT"], NDC, 128)
    wload(ckrT, W["ckrT"], NDC, 128)
    wload(cqcT, W["cqcT"], 4, 384)
    wload(ckcT, W["ckcT"], 4, 384)
    wload(cvT, W["cvT"], 4, 512)
    p.op("gpsimd", lambda e: e.dma_start(out=ident_b[:], in_=ident_hbm), inc=wld, inck=16)
    wv = (wld, wld.n)
    bmap = {"cq_b": 0, "cqc_b": 4, "ckv_b": 8, "ckc_b": 12, "cqr_b": 16, "ckr_b": 17,
            "cv_b": 20}
    for k_, col in bmap.items():
        nch = B[k_].shape[0] // 128
        p.op("sync", lambda e, k_=k_, col=col, nch=nch: e.dma_start(
            out=bias[:, col:col + nch],
            in_=B[k_].rearrange("(c p) -> p c", p=128),
            allow_slow_non_contiguous=True), inc=ld, inck=16)
    p.op("sync", lambda e: e.dma_start(out=cos_sb[:], in_=cos_hbm), inc=ld, inck=16)
    p.op("sync", lambda e: e.dma_start(out=sin_sb[:], in_=sin_hbm), inc=ld, inck=16)
    cv_ld = (ld, ld.n)

    xn_free = [None, None]
    pp_free = [None, None]
    pt_free = [None, None]
    cq_free = None; ckv_free = None
    r_free = None; rt_free = None; t1_free = None
    qst_free = [None, None]; kst_free = [None, None]; vst_free = [None, None]
    vfm_free = None
    ppi = [0]

    def proj(stat_sb, c_cnt, osz, oc, src_sb, src_cnt_stride, nk, waits_extra):
        bi = ppi[0] % 2
        ppi[0] += 1
        first_w = [wv, cv_ld] + waits_extra
        if pp_free[bi] is not None:
            first_w.append(pp_free[bi])
        for k_ in range(nk):
            w = first_w if k_ == 0 else []
            vv = p.op("tensor", lambda e, k_=k_, bi=bi, stat_sb=stat_sb, osz=osz, oc=oc,
                      src_sb=src_sb, s=src_cnt_stride: e.matmul(
                pp[bi][:], stat_sb[:, k_ * osz + oc * 128: k_ * osz + oc * 128 + 128],
                src_sb[:, k_ * s:(k_ + 1) * s] if s else src_sb[:],
                start=(k_ == 0), stop=(k_ == nk - 1)), waits=w, inc=mm)
        return bi, (mm, mm.n)

    all_tiles = sorted(set(q_tiles) | set(kv_tiles))
    for ti, tt in enumerate(all_tiles):
        tb = ti % 2
        do_q = tt in q_tiles
        w = [xn_free[tb]] if xn_free[tb] else []
        xv = p.op("sync", lambda e, tt=tt, tb=tb: e.dma_start(
            out=xn_sb[tb].ap().rearrange("p (c t) -> p c t", t=TT),
            in_=xn_hbm.rearrange("(c p) t -> p c t", p=128)[:, :, tt * TT:(tt + 1) * TT]),
            waits=w, inc=ld, inck=16)
        xnv = (ld, xv)
        xn_mm_last = None

        def rope_gelu(srcT, nk, bcol, stage, stage_free_tv, kind):
            nonlocal r_free, rt_free, t1_free
            bi, pv = proj(srcT, NDC, 128, 0, xn_sb[tb], TT, NDC, [xnv])
            w = [pv]
            if r_free:
                w.append(r_free)
            rv = p.op("vector", lambda e, bi=bi, bcol=bcol: e.tensor_scalar_add(
                r_f32[:], pp[bi][:], bias[:, bcol:bcol + 1]), waits=w, inc=va)
            pp_free[bi] = (va, rv)
            pos = pos_off + tt * TT
            w = [(va, rv), cv_ld]
            if t1_free:
                w.append(t1_free)
            if rt_free:
                w.append(rt_free)
            c1 = p.op("vector", lambda e: e.tensor_copy(
                rs_f32[0:64, :], r_f32[64:128, :]), waits=w, inc=va)
            c2 = p.op("vector", lambda e: e.tensor_copy(
                rs_f32[64:128, :], r_f32[0:64, :]), waits=[(va, c1)], inc=va)
            m1 = p.op("vector", lambda e, pos=pos: e.tensor_mul(
                rt_f32[:], r_f32[:], cos_sb[:, pos:pos + TT]), waits=[(va, c2)], inc=va)
            m2 = p.op("vector", lambda e, pos=pos: e.tensor_mul(
                t1_f32[:], rs_f32[:], sin_sb[:, pos:pos + TT]), waits=[(va, m1)], inc=va)
            b3 = p.op("vector", lambda e: e.tensor_add(
                rt_f32[:], rt_f32[:], t1_f32[:]), waits=[(va, m2)], inc=va)
            r_free = (va, b3)
            t1_free = (va, b3)
            w = [(va, b3)]
            if stage_free_tv:
                w.append(stage_free_tv)
            gv = p.op("scalar", lambda e, stage=stage: e.activation(
                stage[:, 3 * TT:4 * TT], rt_f32[:], AF.Gelu), waits=w, inc=sa)
            rt_free = (sa, gv)
            return (sa, gv)

        if do_q:
            cq_done = []
            for oc in range(4):
                we = [xnv]
                if oc == 0 and cq_free:
                    we.append(cq_free)
                bi, pv = proj(cqT, NDC, 512, oc, xn_sb[tb], TT, NDC, we)
                gv = p.op("scalar", lambda e, bi=bi, oc=oc: e.activation(
                    cq_sb[:, oc * TT:(oc + 1) * TT], pp[bi][:], AF.Gelu,
                    bias=bias[:, oc:oc + 1]), waits=[pv], inc=sa)
                pp_free[bi] = (sa, gv)
                cq_done.append((sa, gv))
            for oc in range(3):
                we = [cq_done[-1]]
                if oc == 0 and qst_free[tb]:
                    we.append(qst_free[tb])
                bi, pv = proj(cqcT, 4, 384, oc, cq_sb, TT, 4, we)
                gv = p.op("scalar", lambda e, bi=bi, oc=oc, tb=tb: e.activation(
                    q_st[tb][:, oc * TT:(oc + 1) * TT], pp[bi][:], AF.Gelu,
                    bias=bias[:, 4 + oc:5 + oc]), waits=[pv], inc=sa)
                pp_free[bi] = (sa, gv)
            cq_free = (mm, mm.n)
            rope_gelu(cqrT, NDC, 16, q_st[tb], None, "q")
            qv = p.op("scalar", lambda e, tb=tb, tt=tt: e.dma_start(
                out=QT_hbm.rearrange("(c p) t -> p c t", p=128)[:, :, (tt - q_tiles[0]) * TT:(tt - q_tiles[0] + 1) * TT],
                in_=q_st[tb].ap().rearrange("p (c t) -> p c t", t=TT)),
                waits=[(sa, sa.n)], inc=so, inck=16)
            qst_free[tb] = (so, qv)
        if tt in kv_tiles:
            ckv_done = []
            for oc in range(4):
                we = [xnv]
                if oc == 0 and ckv_free:
                    we.append(ckv_free)
                bi, pv = proj(ckvT, NDC, 512, oc, xn_sb[tb], TT, NDC, we)
                gv = p.op("scalar", lambda e, bi=bi, oc=oc: e.activation(
                    ckv_sb[:, oc * TT:(oc + 1) * TT], pp[bi][:], AF.Gelu,
                    bias=bias[:, 8 + oc:9 + oc]), waits=[pv], inc=sa)
                pp_free[bi] = (sa, gv)
                ckv_done.append((sa, gv))
            for oc in range(3):
                we = [ckv_done[-1]]
                if oc == 0 and kst_free[tb]:
                    we.append(kst_free[tb])
                bi, pv = proj(ckcT, 4, 384, oc, ckv_sb, TT, 4, we)
                gv = p.op("scalar", lambda e, bi=bi, oc=oc, tb=tb: e.activation(
                    k_st[tb][:, oc * TT:(oc + 1) * TT], pp[bi][:], AF.Gelu,
                    bias=bias[:, 12 + oc:13 + oc]), waits=[pv], inc=sa)
                pp_free[bi] = (sa, gv)
            rope_gelu(ckrT, NDC, 17, k_st[tb], None, "k")
            kv = p.op("scalar", lambda e, tb=tb, tt=tt: e.dma_start(
                out=KT_hbm.rearrange("(c p) t -> p c t", p=128)[:, :, (tt - kv_tiles[0]) * TT:(tt - kv_tiles[0] + 1) * TT],
                in_=k_st[tb].ap().rearrange("p (c t) -> p c t", t=TT)),
                waits=[(sa, sa.n)], inc=so, inck=16)
            kst_free[tb] = (so, kv)
            for oc in range(4):
                we = [ckv_done[-1]]
                bi, pv = proj(cvT, 4, 512, oc, ckv_sb, TT, 4, we)
                w = [pv]
                if vfm_free:
                    w.append(vfm_free)
                ev = p.op("scalar", lambda e, bi=bi, oc=oc: e.activation(
                    v_fm[:], pp[bi][:], AF.Identity, bias=bias[:, 20 + oc:21 + oc]),
                    waits=w, inc=sa)
                pp_free[bi] = (sa, ev)
                for tc in range(4):
                    pb = (oc * 4 + tc) % 2
                    w = [(sa, ev), wv]
                    if pt_free[pb]:
                        w.append(pt_free[pb])
                    tv = p.op("tensor", lambda e, tc=tc, pb=pb: e.transpose(
                        pt[pb][:, 0:128], v_fm[:, tc * 128:(tc + 1) * 128], ident_b[:]),
                        waits=w, inc=mm)
                    w = [(mm, tv)]
                    if tc == 0 and oc == 0 and vst_free[tb]:
                        w.append(vst_free[tb])
                    cvv = p.op("vector", lambda e, tc=tc, oc=oc, pb=pb, tb=tb: e.tensor_copy(
                        v_st[tb][:, tc * 512 + oc * 128: tc * 512 + oc * 128 + 128],
                        pt[pb][:, 0:128]), waits=w, inc=va)
                    pt_free[pb] = (va, cvv)
                vfm_free = (mm, mm.n)
            ckv_free = (mm, mm.n)
            vv = p.op("scalar", lambda e, tb=tb, tt=tt: e.dma_start(
                out=VT_hbm.rearrange("(tc p) o -> p tc o", p=128)[
                    :, (tt - kv_tiles[0]) * 4:(tt - kv_tiles[0] + 1) * 4, :],
                in_=v_st[tb].ap().rearrange("p (tc o) -> p tc o", o=512)),
                waits=[(va, va.n)], inc=so, inck=16)
            vst_free[tb] = (so, vv)
        xn_free[tb] = (mm, mm.n)
    return so



def plan_l0attn(p, QT_hbm, KT_hbm, VT_hbm, g_hbm, x_hbm, W, x1_hbm, Tk, name=""):
    n = name
    TT = 512
    NKC = Tk // 128
    q_off = Tk - 2048
    K_sb = p.sb(f"{n}K", [128, 4 * Tk], BF16)
    V_sb = p.sb(f"{n}V", [128, NKC * 512], BF16)
    Q_sb = [p.sb(f"{n}Q{i}", [128, 4 * TT], BF16) for i in range(2)]
    outT = p.sb(f"{n}oT", [128, 4 * 2048], BF16)
    ob = p.sb(f"{n}ob", [128, 16])
    mask = p.sb(f"{n}mask", [128, 128])
    ident_b = p.sb(f"{n}idb", [128, 128], BF16)
    ident_f = p.sb(f"{n}idf", [128, 128])
    pr = p.sb(f"{n}pr", [128, 4 * Tk], BF16)
    prT = [p.sb(f"{n}prT{i}", [128, TT], BF16) for i in range(2)]
    rm = p.sb(f"{n}rm", [128, 4])
    m2 = p.sb(f"{n}m2", [128, 1])
    nM8 = p.sb(f"{n}nM8", [128, 4])
    lsum = p.sb(f"{n}lsum", [128, 4])
    lpart = p.sb(f"{n}lpart", [128, 1])
    rlam = p.sb(f"{n}rlam", [128, 4])
    O_fm = p.sb(f"{n}Ofm", [128, 4 * TT], BF16)
    sg = p.sb(f"{n}sg", [128, 16 * TT])
    g_sb = [p.sb(f"{n}g{i}", [128, TT]) for i in range(2)]
    x_sb = [p.sb(f"{n}x{i}", [128, 2048]) for i in range(2)]
    x1_st = [p.sb(f"{n}x1{i}", [128, 2048]) for i in range(2)]
    S_ps = p.ps(f"{n}Sps", [128, 2048])
    pt_ps = p.ps(f"{n}ptps", [128, 1024], BF16)
    pq = [p.ps(f"{n}pq{i}", [128, 512]) for i in range(2)]
    ld = p.sem(f"{n}ld")
    wld = p.sem(f"{n}wld")
    mm = p.sem(f"{n}mm")
    va = p.sem(f"{n}va")
    sa = p.sem(f"{n}sa")
    so = p.sem(f"{n}so")

    p.op("gpsimd", lambda e: e.dma_start(
        out=K_sb.ap().rearrange("p (c t) -> p c t", t=Tk),
        in_=KT_hbm.rearrange("(c p) t -> p c t", p=128)), inc=wld, inck=16)
    p.op("gpsimd", lambda e: e.dma_start(
        out=V_sb.ap().rearrange("p (c o) -> p c o", o=512),
        in_=VT_hbm.rearrange("(c p) o -> p c o", p=128)), inc=wld, inck=16)
    p.op("gpsimd", lambda e: e.dma_start(
        out=outT.ap().rearrange("p (c o) -> p c o", o=2048),
        in_=W["outT"].rearrange("(c p) o -> p c o", p=128)), inc=wld, inck=16)
    p.op("gpsimd", lambda e: e.dma_start(out=mask[:], in_=W["mask"]), inc=wld, inck=16)
    p.op("gpsimd", lambda e: e.dma_start(out=ident_b[:], in_=W["ident"]), inc=wld, inck=16)
    p.op("gpsimd", lambda e: e.dma_start(out=ident_f[:], in_=W["ident"]), inc=wld, inck=16)
    p.op("sync", lambda e: e.dma_start(
        out=ob[:], in_=W["out_b"].rearrange("(c p) -> p c", p=128),
        allow_slow_non_contiguous=True), inc=ld, inck=16)
    consts = [(wld, wld.n), (ld, ld.n)]

    Q_free = [None, None]
    pr_free = None
    Sps_free = None
    Ops_free = None
    pt_free = None
    prT_free = [None, None]
    Ofm_free = None
    pq_free = [None, None]
    sg_free = None
    g_free = [None, None]
    x_free = [None, None]
    x1_free = [None, None]
    stat_free = None
    lpart_free = None

    for qt in range(4):
        qb = qt % 2
        w = [consts[0], consts[1]]
        if Q_free[qb]:
            w.append(Q_free[qb])
        qv = p.op("sync", lambda e, qt=qt, qb=qb: e.dma_start(
            out=Q_sb[qb].ap().rearrange("p (c t) -> p c t", t=TT),
            in_=QT_hbm.rearrange("(c p) t -> p c t", p=128)[:, :, qt * TT:(qt + 1) * TT]),
            waits=w, inc=ld, inck=16)
        qld = (ld, qv)
        pair_done = []
        for hd in range(8):
            half, tile = hd % 2, hd // 2
            exp_done = []
            for qc in range(4):
                gq = q_off // 128 + qt * 4 + qc
                valid = (gq + 1) * 128
                ngrp = (valid + 2047) // 2048
                rm_col = rm[:, qc:qc + 1]
                for kg in range(ngrp):
                    vl = min(valid - kg * 2048, 2048)
                    nkt = (vl + 511) // 512
                    for kt in range(nkt):
                        w = [qld]
                        if kt == 0:
                            if Sps_free:
                                w.append(Sps_free)
                            if kg == 0 and qc == 0 and stat_free:
                                w.append(stat_free)
                        mv = p.op("tensor", lambda e, kg=kg, kt=kt, qc=qc, half=half,
                                  tile=tile, qb=qb: e.matmul(
                            S_ps[:, kt * 512:(kt + 1) * 512],
                            Q_sb[qb][half * 64:(half + 1) * 64,
                                     tile * TT + qc * 128: tile * TT + qc * 128 + 128],
                            K_sb[half * 64:(half + 1) * 64,
                                 tile * Tk + kg * 2048 + kt * 512:
                                 tile * Tk + kg * 2048 + kt * 512 + 512],
                            start=True, stop=True), waits=w, inc=mm)
                    smm = (mm, mm.n)
                    diag_off = valid - 128 - kg * 2048
                    if 0 <= diag_off < 2048:
                        dmv = p.op("vector", lambda e, diag_off=diag_off: e.tensor_add(
                            S_ps[:, diag_off:diag_off + 128],
                            S_ps[:, diag_off:diag_off + 128], mask[:]),
                            waits=[smm, consts[0]], inc=va)
                        smm = (va, dmv)
                    if kg == 0:
                        xv_ = p.op("vector", lambda e, vl=vl, rm_col=rm_col: e.reduce_max(
                            rm_col, S_ps[:, 0:vl], axis=AX.X), waits=[smm], inc=va)
                    else:
                        xv_ = p.op("vector", lambda e, vl=vl: e.reduce_max(
                            m2[:], S_ps[:, 0:vl], axis=AX.X), waits=[smm], inc=va)
                        xv_ = p.op("vector", lambda e, rm_col=rm_col: e.tensor_max(
                            rm_col, rm_col, m2[:]), waits=[(va, xv_)], inc=va)
                    Sps_free = (va, xv_)
                nv = p.op("vector", lambda e, qc=qc: e.tensor_scalar_mul(
                    nM8[:, qc:qc + 1], rm[:, qc:qc + 1], -0.125),
                    waits=[(va, va.n)], inc=va)
                for kg in range(ngrp):
                    vl = min(valid - kg * 2048, 2048)
                    nkt = (vl + 511) // 512
                    for kt in range(nkt):
                        w = [qld]
                        if kt == 0 and Sps_free:
                            w.append(Sps_free)
                        mv = p.op("tensor", lambda e, kg=kg, kt=kt, qc=qc, half=half,
                                  tile=tile, qb=qb: e.matmul(
                            S_ps[:, kt * 512:(kt + 1) * 512],
                            Q_sb[qb][half * 64:(half + 1) * 64,
                                     tile * TT + qc * 128: tile * TT + qc * 128 + 128],
                            K_sb[half * 64:(half + 1) * 64,
                                 tile * Tk + kg * 2048 + kt * 512:
                                 tile * Tk + kg * 2048 + kt * 512 + 512],
                            start=True, stop=True), waits=w, inc=mm)
                    smm = (mm, mm.n)
                    diag_off = valid - 128 - kg * 2048
                    if 0 <= diag_off < 2048:
                        dmv = p.op("vector", lambda e, diag_off=diag_off: e.tensor_add(
                            S_ps[:, diag_off:diag_off + 128],
                            S_ps[:, diag_off:diag_off + 128], mask[:]),
                            waits=[smm, consts[0]], inc=va)
                        smm = (va, dmv)
                    w = [smm, (va, nv)]
                    if kg == 0 and pr_free:
                        w.append(pr_free)
                    if lpart_free:
                        w.append(lpart_free)
                    ev = p.op("scalar", lambda e, vl=vl, kg=kg, qc=qc: e.activation(
                        pr[:, qc * Tk + kg * 2048: qc * Tk + kg * 2048 + vl],
                        S_ps[:, 0:vl], AF.Exp, scale=0.125, bias=nM8[:, qc:qc + 1],
                        accum_out=lpart[:]), waits=w, inc=sa)
                    Sps_free = (sa, ev)
                    if kg == 0:
                        lv = p.op("vector", lambda e, qc=qc: e.tensor_copy(
                            lsum[:, qc:qc + 1], lpart[:]), waits=[(sa, ev)], inc=va)
                    else:
                        lv = p.op("vector", lambda e, qc=qc: e.tensor_add(
                            lsum[:, qc:qc + 1], lsum[:, qc:qc + 1], lpart[:]),
                            waits=[(sa, ev)], inc=va)
                    lpart_free = (va, lv)
                rv = p.op("vector", lambda e, qc=qc: e.reciprocal(
                    rlam[:, qc:qc + 1], lsum[:, qc:qc + 1]), waits=[(va, va.n)], inc=va)
                pv = p.op("vector", lambda e, qc=qc, valid=valid: e.tensor_scalar_mul(
                    pr[:, qc * Tk: qc * Tk + valid], pr[:, qc * Tk: qc * Tk + valid],
                    rlam[:, qc:qc + 1]), waits=[(va, rv)], inc=va)
                exp_done.append((va, pv))
            stat_free = (va, va.n)
            probs_v = exp_done[-1]
            gq_last = q_off // 128 + qt * 4 + 3
            nkc = gq_last + 1
            first_pv = None
            for kc in range(nkc):
                ti = kc % 2
                w_tr = [probs_v, consts[0]]
                if pt_free:
                    w_tr.append(pt_free)
                if prT_free[ti]:
                    w_tr.append(prT_free[ti])
                cpv = None
                for qc in range(4):
                    gq = q_off // 128 + qt * 4 + qc
                    if kc > gq:
                        continue
                    tv = p.op("tensor", lambda e, kc=kc, qc=qc: e.transpose(
                        pt_ps[:, 0:128], pr[:, qc * Tk + kc * 128: qc * Tk + kc * 128 + 128],
                        ident_b[:]), waits=w_tr if qc == 0 or cpv is None else [(va, cpv)],
                        inc=mm)
                    cpv = p.op("vector", lambda e, qc=qc, ti=ti: e.tensor_copy(
                        prT[ti][:, qc * 128:(qc + 1) * 128], pt_ps[:, 0:128]),
                        waits=[(mm, tv)], inc=va)
                    w_tr = [(va, cpv)]
                for qc in range(4):
                    gq = q_off // 128 + qt * 4 + qc
                    if kc <= gq:
                        continue
                    cpv = p.op("vector", lambda e, qc=qc, ti=ti: e.memset(
                        prT[ti][:, qc * 128:(qc + 1) * 128], 0.0),
                        waits=[(va, cpv)] if cpv else w_tr, inc=va)
                pt_free = (va, cpv)
                trv = (va, cpv)
                w = [trv, consts[0]]
                if kc == 0 and pq_free[half]:
                    w.append(pq_free[half])
                pvv = p.op("tensor", lambda e, kc=kc, hd=hd, half=half, nkc=nkc: e.matmul(
                    pq[half][half * 64:(half + 1) * 64, :],
                    V_sb[:, kc * 512 + hd * 64: kc * 512 + hd * 64 + 64],
                    prT[kc % 2][:], start=(kc == 0), stop=(kc == nkc - 1)),
                    waits=w, inc=mm)
                prT_free[ti] = (mm, pvv)
            pr_free = (mm, mm.n)
            ov = p.op("vector", lambda e, tile=tile, half=half: e.tensor_copy(
                O_fm[half * 64:(half + 1) * 64, tile * TT:(tile + 1) * TT],
                pq[half][half * 64:(half + 1) * 64, :]), waits=[(mm, mm.n)], inc=va)
            pq_free[half] = (va, ov)
            if half == 1:
                pair_done.append((va, ov))
        ow = pair_done[-1]
        for oc in range(16):
            bi = oc % 2
            w = [g_free[bi]] if g_free[bi] else []
            gv = p.op("sync", lambda e, oc=oc, qt=qt, bi=bi: e.dma_start(
                out=g_sb[bi][:], in_=g_hbm[oc * 128:(oc + 1) * 128, qt * TT:(qt + 1) * TT]),
                waits=w, inc=ld, inck=16)
            w = [ow, consts[0]]
            if pq_free[bi]:
                w.append(pq_free[bi])
            for c in range(4):
                vv = p.op("tensor", lambda e, c=c, oc=oc, bi=bi: e.matmul(
                    pq[bi][:], outT[:, c * 2048 + oc * 128: c * 2048 + oc * 128 + 128],
                    O_fm[:, c * TT:(c + 1) * TT],
                    start=(c == 0), stop=(c == 3)), waits=w if c == 0 else [], inc=mm)
            omm = (mm, mm.n)
            w = [omm]
            if oc < 2 and sg_free:
                w.append(sg_free)
            sv = p.op("scalar", lambda e, oc=oc, bi=bi: e.activation(
                sg[:, oc * TT:(oc + 1) * TT], pq[bi][:], AF.Identity,
                bias=ob[:, oc:oc + 1]), waits=w, inc=sa)
            pq_free[bi] = (sa, sv)
            mv = p.op("vector", lambda e, oc=oc, bi=bi: e.tensor_mul(
                sg[:, oc * TT:(oc + 1) * TT], sg[:, oc * TT:(oc + 1) * TT], g_sb[bi][:]),
                waits=[(sa, sv), (ld, gv)], inc=va)
            g_free[bi] = (va, mv)
        sg_done = (va, va.n)
        for qc in range(4):
            xb = qc % 2
            w = [x_free[xb]] if x_free[xb] else []
            xv_ = p.op("sync", lambda e, qt=qt, qc=qc, xb=xb: e.dma_start(
                out=x_sb[xb][:], in_=x_hbm[qt * TT + qc * 128: qt * TT + qc * 128 + 128, :]),
                waits=w, inc=ld, inck=16)
            last_cp = None
            for oc in range(16):
                bi = oc % 2
                w = [sg_done, consts[0]]
                if pq_free[bi]:
                    w.append(pq_free[bi])
                tv = p.op("tensor", lambda e, qc=qc, oc=oc, bi=bi: e.transpose(
                    pq[bi][:, 0:128], sg[:, oc * TT + qc * 128: oc * TT + qc * 128 + 128],
                    ident_f[:]), waits=w, inc=mm)
                w = [(mm, tv)]
                if oc < 2 and x1_free[xb]:
                    w.append(x1_free[xb])
                cv = p.op("vector", lambda e, oc=oc, bi=bi, xb=xb: e.tensor_copy(
                    x1_st[xb][:, oc * 128:(oc + 1) * 128], pq[bi][:, 0:128]),
                    waits=w, inc=va)
                pq_free[bi] = (va, cv)
                last_cp = cv
            av = p.op("vector", lambda e, xb=xb: e.tensor_add(
                x1_st[xb][:], x1_st[xb][:], x_sb[xb][:]),
                waits=[(va, last_cp), (ld, xv_)], inc=va)
            x_free[xb] = (va, av)
            stv = p.op("scalar", lambda e, qt=qt, qc=qc, xb=xb: e.dma_start(
                out=x1_hbm[qt * TT + qc * 128: qt * TT + qc * 128 + 128, :],
                in_=x1_st[xb][:]), waits=[(va, av)], inc=so, inck=16)
            x1_free[xb] = (so, stv)
        sg_free = (va, va.n)
        Q_free[qb] = (mm, mm.n)
    return so


D = 2048
NDC = 16
NOC = 16


def plan_l1ffn(p, xn_hbm, w1T_hbm, gateT, upT, downT, loraAT, loraBT, g1_hbm,
               T=2048, TT=512, name="", apply_silu=True, nexp=8):
    n = name
    NTT = T // TT
    E = 8
    xn_sb = p.sb(f"{n}xn", [128, NDC * TT], BF16)
    dw_row = p.sb(f"{n}dwr", [1, TT])
    ones = p.sb(f"{n}ones", [1, 128])
    wbc = p.sb(f"{n}wbc", [128, TT])
    gt_b = [p.sb(f"{n}gt{i}", [128, NDC * 256], BF16) for i in range(2)]
    up_b = [p.sb(f"{n}up{i}", [128, NDC * 256], BF16) for i in range(2)]
    dn_b = [p.sb(f"{n}dn{i}", [128, 2 * 2048], BF16) for i in range(2)]
    la_b = [p.sb(f"{n}la{i}", [128, NDC * 16], BF16) for i in range(2)]
    lb_b = [p.sb(f"{n}lb{i}", [16, 2048], BF16) for i in range(2)]
    ga_sb = p.sb(f"{n}ga", [128, 2 * TT], BF16)
    tmp_sb = p.sb(f"{n}tmp", [128, TT], BF16)
    ge_sb = p.sb(f"{n}ge", [128, 2 * TT], BF16)
    lrw = p.sb(f"{n}lrw", [16, TT], BF16)
    y_sb = p.sb(f"{n}y", [128, NOC * TT])
    g_st = [p.sb(f"{n}g{i}", [128, TT]) for i in range(2)]
    a_ps = p.ps(f"{n}aps", [128, TT])
    b_ps = p.ps(f"{n}bps", [128, TT])
    l_ps = p.ps(f"{n}lps", [128, TT])
    y_ps = [p.ps(f"{n}yps{i}", [128, TT]) for i in range(2)]
    bc_ps = p.ps(f"{n}bcps", [128, TT])
    ld = p.sem(f"{n}ld")
    wg = p.sem(f"{n}wg")
    mm = p.sem(f"{n}mm")
    va = p.sem(f"{n}va")
    sa = p.sem(f"{n}sa")
    so = p.sem(f"{n}so")

    p.op("vector", lambda e: e.memset(ones[:], 1.0), inc=va)
    ones_v = (va, va.n)

    xn_free = None
    wfree = {k: [None, None] for k in ("gt", "up", "dn", "la", "lb")}
    aps_free = None; bps_free = None; lps_free = None; bc_free = None
    yps_free = [None, None]
    y_last = [None] * NOC
    g_free = [None, None]
    ga_free = [None, None]; tmp_free = None; ge_free = None; lrw_free = None
    dwrow_free = None
    wqi = 0

    for tt in range(NTT):
        w = [xn_free] if xn_free else []
        xv = p.op("sync", lambda e, tt=tt: e.dma_start(
            out=xn_sb.ap().rearrange("p (c t) -> p c t", t=TT),
            in_=xn_hbm.rearrange("(c p) t -> p c t", p=128)[:, :, tt * TT:(tt + 1) * TT]),
            waits=w, inc=ld, inck=16)
        xnv = (ld, xv)
        for ei in range(nexp):
            bi = wqi % 2
            wqi += 1
            lds = {}
            for key, dst, src, osz, nch in (
                    ("gt", gt_b[bi], gateT[ei], 256, NDC),
                    ("up", up_b[bi], upT[ei], 256, NDC),
                    ("dn", dn_b[bi], downT[ei], 2048, 2),
                    ("la", la_b[bi], loraAT[ei], 16, NDC)):
                w = [wfree[key][bi]] if wfree[key][bi] else []
                lv = p.op("gpsimd", lambda e, dst=dst, src=src, osz=osz: e.dma_start(
                    out=dst.ap().rearrange("p (c o) -> p c o", o=osz),
                    in_=src.rearrange("(c p) o -> p c o", p=128)), waits=w, inc=wg, inck=16)
                lds[key] = lv
            w = [wfree["lb"][bi]] if wfree["lb"][bi] else []
            lv = p.op("gpsimd", lambda e, dst=lb_b[bi], src=loraBT[ei]: e.dma_start(
                out=dst[:], in_=src), waits=w, inc=wg, inck=16)
            lds["lb"] = lv
            w = [dwrow_free] if dwrow_free else []
            dv_ = p.op("sync", lambda e, ei=ei, tt=tt: e.dma_start(
                out=dw_row[:], in_=w1T_hbm[ei:ei + 1, tt * TT:(tt + 1) * TT]),
                waits=w, inc=ld, inck=16)
            w = [ones_v, (ld, dv_)]
            if bc_free:
                w.append(bc_free)
            bv = p.op("tensor", lambda e: e.matmul(
                bc_ps[:], ones[:], dw_row[:], start=True, stop=True), waits=w, inc=mm)
            dwrow_free = (mm, bv)
            w = [(mm, bv)]
            bcv = p.op("vector", lambda e: e.tensor_copy(wbc[:], bc_ps[:]), waits=w, inc=va)
            bc_free = (va, bcv)
            for fc in range(2):
                w = [(wg, lds["gt"]), xnv]
                if aps_free:
                    w.append(aps_free)
                for dc in range(NDC):
                    av_ = p.op("tensor", lambda e, dc=dc, fc=fc, bi=bi: e.matmul(
                        a_ps[:], gt_b[bi][:, dc * 256 + fc * 128: dc * 256 + fc * 128 + 128],
                        xn_sb[:, dc * TT:(dc + 1) * TT],
                        start=(dc == 0), stop=(dc == NDC - 1)),
                        waits=w if dc == 0 else [], inc=mm)
                amm = (mm, mm.n)
                w = [(wg, lds["up"]), xnv]
                if bps_free:
                    w.append(bps_free)
                for dc in range(NDC):
                    bv_ = p.op("tensor", lambda e, dc=dc, fc=fc, bi=bi: e.matmul(
                        b_ps[:], up_b[bi][:, dc * 256 + fc * 128: dc * 256 + fc * 128 + 128],
                        xn_sb[:, dc * TT:(dc + 1) * TT],
                        start=(dc == 0), stop=(dc == NDC - 1)),
                        waits=w if dc == 0 else [], inc=mm)
                bmm = (mm, mm.n)
                w = [amm]
                if ga_free[fc]:
                    w.append(ga_free[fc])
                sv = p.op("scalar", lambda e, fc=fc: e.activation(
                    ga_sb[:, fc * TT:(fc + 1) * TT], a_ps[:], AF.Silu), waits=w, inc=sa)
                aps_free = (sa, sv)
                w = [bmm, (va, bcv)]
                if tmp_free:
                    w.append(tmp_free)
                tv = p.op("vector", lambda e: e.tensor_mul(tmp_sb[:], b_ps[:], wbc[:]),
                          waits=w, inc=va)
                bps_free = (va, tv)
                w = [(va, tv), (sa, sv)]
                if ge_free and fc == 0:
                    w.append(ge_free)
                gv = p.op("vector", lambda e, fc=fc: e.tensor_mul(
                    ge_sb[:, fc * TT:(fc + 1) * TT], ga_sb[:, fc * TT:(fc + 1) * TT],
                    tmp_sb[:]), waits=w, inc=va)
                tmp_free = (va, gv)
                ga_free[fc] = (va, gv)
            ge_done = (va, va.n)
            w = [(wg, lds["la"]), xnv]
            if lps_free:
                w.append(lps_free)
            for dc in range(NDC):
                lv_ = p.op("tensor", lambda e, dc=dc, bi=bi: e.matmul(
                    l_ps[0:16, :], la_b[bi][:, dc * 16:(dc + 1) * 16],
                    xn_sb[:, dc * TT:(dc + 1) * TT],
                    start=(dc == 0), stop=(dc == NDC - 1)),
                    waits=w if dc == 0 else [], inc=mm)
            lmm = (mm, mm.n)
            w = [lmm, (va, bcv)]
            if lrw_free:
                w.append(lrw_free)
            lwv = p.op("vector", lambda e: e.tensor_mul(
                lrw[:], l_ps[0:16, :], wbc[0:16, :]), waits=w, inc=va)
            lps_free = (va, lwv)
            for oc in range(NOC):
                yb = oc % 2
                w = [ge_done, (va, lwv), (wg, lds["dn"]), (wg, lds["lb"])]
                if yps_free[yb]:
                    w.append(yps_free[yb])
                for fc in range(2):
                    yv = p.op("tensor", lambda e, fc=fc, oc=oc, bi=bi, yb=yb: e.matmul(
                        y_ps[yb][:], dn_b[bi][:, fc * 2048 + oc * 128: fc * 2048 + oc * 128 + 128],
                        ge_sb[:, fc * TT:(fc + 1) * TT],
                        start=(fc == 0), stop=False), waits=w if fc == 0 else [], inc=mm)
                yv = p.op("tensor", lambda e, oc=oc, bi=bi, yb=yb: e.matmul(
                    y_ps[yb][:], lb_b[bi][0:16, oc * 128:(oc + 1) * 128],
                    lrw[:], start=False, stop=True), waits=[], inc=mm)
                ymm = (mm, yv)
                ge_free = ymm
                lrw_free = ymm
                w = [ymm]
                if y_last[oc]:
                    w.append(y_last[oc])
                if ei == 0:
                    cv = p.op("vector", lambda e, oc=oc, yb=yb: e.tensor_copy(
                        y_sb[:, oc * TT:(oc + 1) * TT], y_ps[yb][:]), waits=w, inc=va)
                else:
                    cv = p.op("vector", lambda e, oc=oc, yb=yb: e.tensor_add(
                        y_sb[:, oc * TT:(oc + 1) * TT],
                        y_sb[:, oc * TT:(oc + 1) * TT], y_ps[yb][:]), waits=w, inc=va)
                y_last[oc] = (va, cv)
                yps_free[yb] = (va, cv)
            for k in ("gt", "up", "dn", "la"):
                wfree[k][bi] = (mm, mm.n)
            wfree["lb"][bi] = (mm, mm.n)
        for oc in range(NOC):
            gb = oc % 2
            w = [y_last[oc]]
            if g_free[gb]:
                w.append(g_free[gb])
            if apply_silu:
                gv = p.op("scalar", lambda e, oc=oc, gb=gb: e.activation(
                    g_st[gb][:], y_sb[:, oc * TT:(oc + 1) * TT], AF.Silu), waits=w, inc=sa)
            else:
                gv = p.op("scalar", lambda e, oc=oc, gb=gb: e.copy(
                    g_st[gb][:], y_sb[:, oc * TT:(oc + 1) * TT]), waits=w, inc=sa)
            sv = p.op("scalar", lambda e, oc=oc, gb=gb, tt=tt: e.dma_start(
                out=g1_hbm[oc * 128:(oc + 1) * 128, tt * TT:(tt + 1) * TT],
                in_=g_st[gb][:]), waits=[(sa, gv)], inc=so, inck=16)
            g_free[gb] = (so, sv)
            y_last[oc] = [(so, sv), (sa, gv)]
        xn_free = (mm, mm.n)
    return so


NDC = 16
S05 = 0.22360679774997896


def plan_l1attn(p, xn_hbm, Wv, Wg, Wq, Wo, Bg, Bq, dpow_hbm, g1_hbm, x1_hbm,
                mem_hbm, cc_in, cc_out, x2_hbm, ident_hbm, name=""):
    n = name
    TT = 512
    T = 2048
    xn_sb = [p.sb(f"{n}xn{i}", [128, NDC * TT], BF16) for i in range(2)]
    wv_b = [p.sb(f"{n}wv{i}", [128, NDC * 128], BF16) for i in range(2)]
    wg_b = [p.sb(f"{n}wg{i}", [128, NDC * 128], BF16) for i in range(2)]
    wq_b = [p.sb(f"{n}wq{i}", [128, NDC * 128], BF16) for i in range(2)]
    wo_b = [p.sb(f"{n}wo{i}", [128, NDC * 128], BF16) for i in range(2)]
    bgq = p.sb(f"{n}bgq", [128, 32])
    dec = p.sb(f"{n}dec", [128, 1])
    dpow = p.sb(f"{n}dpow", [128, 2048])
    ident_f = p.sb(f"{n}idf", [128, 128])
    sig = p.sb(f"{n}sig", [128, TT])
    store_c = p.sb(f"{n}stc", [128, TT])
    mem_c = p.sb(f"{n}memc", [128, TT])
    sstate = p.sb(f"{n}sst", [128, 16])
    carry = p.sb(f"{n}carry", [128, 16])
    carry_in = p.sb(f"{n}cin", [128, 16])
    memf = [p.sb(f"{n}memf{i}", [128, TT]) for i in range(2)]
    loadW = p.sb(f"{n}ldw", [128, NDC * T], BF16)
    sg = p.sb(f"{n}sg", [128, 16 * TT])
    g_sb = [p.sb(f"{n}g{i}", [128, TT]) for i in range(2)]
    x2_st = [p.sb(f"{n}x2{i}", [128, 2048]) for i in range(2)]
    v_ps = p.ps(f"{n}vps", [128, TT])
    gp_ps = p.ps(f"{n}gps", [128, TT])
    q_ps = p.ps(f"{n}qps", [128, TT])
    pq = [p.ps(f"{n}pq{i}", [128, 512]) for i in range(2)]
    ld = p.sem(f"{n}ld")
    wg_s = p.sem(f"{n}wgs")
    mm = p.sem(f"{n}mm")
    va = p.sem(f"{n}va")
    sa = p.sem(f"{n}sa")
    so = p.sem(f"{n}so")
    go = p.sem(f"{n}go")
    cc = p.sem(f"{n}cc")

    p.op("sync", lambda e: e.dma_start(
        out=bgq[:, 0:16], in_=Bg.rearrange("(c p) -> p c", p=128),
        allow_slow_non_contiguous=True), inc=ld, inck=16)
    p.op("sync", lambda e: e.dma_start(
        out=bgq[:, 16:32], in_=Bq.rearrange("(c p) -> p c", p=128),
        allow_slow_non_contiguous=True), inc=ld, inck=16)
    p.op("sync", lambda e: e.dma_start(out=dpow[:], in_=dpow_hbm), inc=ld, inck=16)
    p.op("sync", lambda e: e.dma_start(out=ident_f[:], in_=ident_hbm), inc=ld, inck=16)
    p.op("vector", lambda e: e.memset(dec[:], 0.95), inc=va)
    p.op("vector", lambda e: e.memset(sstate[:], 0.0), inc=va)
    consts = [(ld, ld.n), (va, va.n)]

    def wblock(dst, src, mc, waits):
        return p.op("gpsimd", lambda e, dst=dst, src=src, mc=mc: e.dma_start(
            out=dst.ap().rearrange("p (c o) -> p c o", o=128),
            in_=src.rearrange("(c p) o -> p c o", p=128)[:, :, mc * 128:(mc + 1) * 128]),
            waits=waits, inc=wg_s, inck=16)

    wfree = {"v": [None, None], "g": [None, None], "q": [None, None], "o": [None, None]}
    xn_free = [None, None]
    vps_free = None; gps_free = None; qps_free = None
    sig_free = None; stc_free = None; memc_free = None
    sst_free = [None] * 16
    carry_last = None
    mem_st_last = None

    for tt in range(4):
        tb = tt % 2
        w = [xn_free[tb]] if xn_free[tb] else []
        xv = p.op("sync", lambda e, tt=tt, tb=tb: e.dma_start(
            out=xn_sb[tb].ap().rearrange("p (c t) -> p c t", t=TT),
            in_=xn_hbm.rearrange("(c p) t -> p c t", p=128)[:, :, tt * TT:(tt + 1) * TT]),
            waits=w, inc=ld, inck=16)
        xnv = (ld, xv)
        for mc in range(16):
            bi = mc % 2
            lv1 = wblock(wv_b[bi], Wv, mc, [wfree["v"][bi]] if wfree["v"][bi] else [])
            lv2 = wblock(wg_b[bi], Wg, mc, [wfree["g"][bi]] if wfree["g"][bi] else [])
            w = [(wg_s, lv1), xnv]
            if vps_free:
                w.append(vps_free)
            for dc in range(NDC):
                vv = p.op("tensor", lambda e, dc=dc, tb=tb, bi=bi: e.matmul(
                    v_ps[:], wv_b[bi][:, dc * 128:(dc + 1) * 128],
                    xn_sb[tb][:, dc * TT:(dc + 1) * TT],
                    start=(dc == 0), stop=(dc == NDC - 1)),
                    waits=w if dc == 0 else [], inc=mm)
            vmm = (mm, mm.n)
            w = [(wg_s, lv2), xnv]
            if gps_free:
                w.append(gps_free)
            for dc in range(NDC):
                gv = p.op("tensor", lambda e, dc=dc, tb=tb, bi=bi: e.matmul(
                    gp_ps[:], wg_b[bi][:, dc * 128:(dc + 1) * 128],
                    xn_sb[tb][:, dc * TT:(dc + 1) * TT],
                    start=(dc == 0), stop=(dc == NDC - 1)),
                    waits=w if dc == 0 else [], inc=mm)
            gmm = (mm, mm.n)
            wfree["v"][bi] = vmm
            wfree["g"][bi] = gmm
            w = [gmm, consts[0]]
            if sig_free:
                w.append(sig_free)
            sv = p.op("scalar", lambda e, mc=mc: e.activation(
                sig[:], gp_ps[:], AF.Sigmoid, bias=bgq[:, mc:mc + 1]), waits=w, inc=sa)
            gps_free = (sa, sv)
            w = [vmm, (sa, sv)]
            if stc_free:
                w.append(stc_free)
            stv = p.op("vector", lambda e: e.scalar_tensor_tensor(
                store_c[:], v_ps[:], S05, sig[:], ALU.mult, ALU.mult), waits=w, inc=va)
            vps_free = (va, stv)
            sig_free = (va, stv)
            w = [(va, stv), consts[1]]
            if memc_free:
                w.append(memc_free)
            if sst_free[mc]:
                w.append(sst_free[mc])
            scv = p.op("vector", lambda e, mc=mc: e.tensor_tensor_scan(
                mem_c[:], bass.AP(dec.ap().tensor, dec.ap().offset, [[1, 128], [0, TT]]),
                store_c[:], sstate[:, mc:mc + 1], ALU.mult, ALU.add),
                waits=w, inc=va)
            stc_free = (va, scv)
            uv = p.op("vector", lambda e, mc=mc: e.tensor_copy(
                sstate[:, mc:mc + 1], mem_c[:, TT - 1:TT]), waits=[(va, scv)], inc=va)
            sst_free[mc] = (va, uv)
            if tt == 3:
                cv = p.op("vector", lambda e, mc=mc: e.tensor_copy(
                    carry[:, mc:mc + 1], mem_c[:, TT - 1:TT]), waits=[(va, scv)], inc=va)
                carry_last = (va, cv)
            mv = p.op("gpsimd", lambda e, mc=mc, tt=tt: e.dma_start(
                out=mem_hbm[mc * 128:(mc + 1) * 128, tt * TT:(tt + 1) * TT],
                in_=mem_c[:]), waits=[(va, uv)], inc=go, inck=16)
            memc_free = (go, mv)
            mem_st_last = (go, mv)
        xn_free[tb] = (mm, mm.n)

    civ = p.op("gpsimd", lambda e: e.dma_start(
        out=bass.AP(cc_in.tensor, 0, [[1, 128], [128, 16]]), in_=carry[:],
        allow_slow_non_contiguous=True), waits=[carry_last], inc=go, inck=16)
    ccv = p.op("gpsimd", lambda e: e.collective_compute(
        "AllGather", ALU.bypass, replica_groups=[[0, 1], [2, 3], [4, 5], [6, 7]],
        ins=[cc_in.opt()], outs=[cc_out.opt()]), waits=[(go, go.n)], inc=cc, inck=1)
    cinv = p.op("gpsimd", lambda e: e.dma_start(
        out=carry_in[:], in_=bass.AP(cc_out.tensor, 0, [[1, 128], [128, 16]]),
        allow_slow_non_contiguous=True), waits=[(cc, ccv)], inc=go, inck=16)
    cin_v = (go, cinv)

    mem_free = [None, None]
    for tt in range(4):
        tb = tt % 2
        w = [xn_free[tb]] if xn_free[tb] else []
        xv = p.op("sync", lambda e, tt=tt, tb=tb: e.dma_start(
            out=xn_sb[tb].ap().rearrange("p (c t) -> p c t", t=TT),
            in_=xn_hbm.rearrange("(c p) t -> p c t", p=128)[:, :, tt * TT:(tt + 1) * TT]),
            waits=w, inc=ld, inck=16)
        xnv = (ld, xv)
        for mc in range(16):
            bi = mc % 2
            lq = wblock(wq_b[bi], Wq, mc, [wfree["q"][bi]] if wfree["q"][bi] else [])
            w = [mem_free[bi] if mem_free[bi] else None, mem_st_last]
            mlv = p.op("sync", lambda e, mc=mc, tt=tt, bi=bi: e.dma_start(
                out=memf[bi][:], in_=mem_hbm[mc * 128:(mc + 1) * 128, tt * TT:(tt + 1) * TT]),
                waits=w, inc=ld, inck=16)
            crv = p.op("vector", lambda e, mc=mc, tt=tt, bi=bi: e.scalar_tensor_tensor(
                memf[bi][:], dpow[:, tt * TT:(tt + 1) * TT], carry_in[:, mc:mc + 1],
                memf[bi][:], ALU.mult, ALU.add),
                waits=[(ld, mlv), cin_v, consts[0]], inc=va)
            w = [(wg_s, lq), xnv]
            if qps_free:
                w.append(qps_free)
            for dc in range(NDC):
                qv = p.op("tensor", lambda e, dc=dc, tb=tb, bi=bi: e.matmul(
                    q_ps[:], wq_b[bi][:, dc * 128:(dc + 1) * 128],
                    xn_sb[tb][:, dc * TT:(dc + 1) * TT],
                    start=(dc == 0), stop=(dc == NDC - 1)),
                    waits=w if dc == 0 else [], inc=mm)
            qmm = (mm, mm.n)
            wfree["q"][bi] = qmm
            w = [qmm, consts[0]]
            if sig_free:
                w.append(sig_free)
            sv = p.op("scalar", lambda e, mc=mc: e.activation(
                sig[:], q_ps[:], AF.Sigmoid, bias=bgq[:, 16 + mc:17 + mc]),
                waits=w, inc=sa)
            qps_free = (sa, sv)
            lwv = p.op("vector", lambda e, mc=mc, tt=tt, bi=bi: e.scalar_tensor_tensor(
                loadW[:, mc * T + tt * TT: mc * T + (tt + 1) * TT],
                memf[bi][:], S05, sig[:], ALU.mult, ALU.mult),
                waits=[(sa, sv), (va, crv)], inc=va)
            sig_free = (va, lwv)
            mem_free[bi] = (va, lwv)
        xn_free[tb] = (mm, mm.n)
    ldw_done = (va, va.n)

    g_free = [None, None]
    pq_free = [None, None]
    sg_free = None
    x2_free = [None, None]
    for tt in range(4):
        for ec in range(16):
            bi = ec % 2
            wo_l = wblock(wo_b[bi], Wo, ec, [wfree["o"][bi]] if wfree["o"][bi] else [])
            w = [g_free[bi]] if g_free[bi] else []
            gv = p.op("sync", lambda e, ec=ec, tt=tt, bi=bi: e.dma_start(
                out=g_sb[bi][:], in_=g1_hbm[ec * 128:(ec + 1) * 128, tt * TT:(tt + 1) * TT]),
                waits=w, inc=ld, inck=16)
            w = [ldw_done, (wg_s, wo_l)]
            if pq_free[bi]:
                w.append(pq_free[bi])
            for mc in range(NDC):
                vv = p.op("tensor", lambda e, mc=mc, tt=tt, bi=bi: e.matmul(
                    pq[bi][:], wo_b[bi][:, mc * 128:(mc + 1) * 128],
                    loadW[:, mc * T + tt * TT: mc * T + (tt + 1) * TT],
                    start=(mc == 0), stop=(mc == NDC - 1)),
                    waits=w if mc == 0 else [], inc=mm)
            omm = (mm, mm.n)
            wfree["o"][bi] = omm
            w = [omm, (ld, gv)]
            if ec < 2 and sg_free:
                w.append(sg_free)
            mv = p.op("vector", lambda e, ec=ec, bi=bi: e.tensor_mul(
                sg[:, ec * TT:(ec + 1) * TT], pq[bi][:], g_sb[bi][:]), waits=w, inc=va)
            pq_free[bi] = (va, mv)
            g_free[bi] = (va, mv)
        sg_done = (va, va.n)
        for qc in range(4):
            xb = qc % 2
            w = [x2_free[xb]] if x2_free[xb] else []
            xv_ = p.op("sync", lambda e, tt=tt, qc=qc, xb=xb: e.dma_start(
                out=x2_st[xb][:], in_=x1_hbm[tt * TT + qc * 128: tt * TT + qc * 128 + 128, :]),
                waits=w, inc=ld, inck=16)
            last_cp = None
            for ec in range(16):
                bi = ec % 2
                w = [sg_done, consts[0]]
                if pq_free[bi]:
                    w.append(pq_free[bi])
                tv = p.op("tensor", lambda e, qc=qc, ec=ec, bi=bi: e.transpose(
                    pq[bi][:, 0:128], sg[:, ec * TT + qc * 128: ec * TT + qc * 128 + 128],
                    ident_f[:]), waits=w, inc=mm)
                w = [(mm, tv), (ld, xv_)]
                cv = p.op("vector", lambda e, ec=ec, bi=bi, xb=xb: e.tensor_add(
                    x2_st[xb][:, ec * 128:(ec + 1) * 128],
                    x2_st[xb][:, ec * 128:(ec + 1) * 128], pq[bi][:, 0:128]),
                    waits=w, inc=va)
                pq_free[bi] = (va, cv)
                last_cp = cv
            stv = p.op("scalar", lambda e, tt=tt, qc=qc, xb=xb: e.dma_start(
                out=x2_hbm[tt * TT + qc * 128: tt * TT + qc * 128 + 128, :],
                in_=x2_st[xb][:]), waits=[(va, last_cp)], inc=so, inck=16)
            x2_free[xb] = [(so, stv)]
        sg_free = (va, va.n)
    return so


import ml_dtypes
from contextlib import ExitStack as _ES

_BF = ml_dtypes.bfloat16


def _cs(arr, c, fm=False):
    b, h = c // 2, c % 2
    sl = np.asarray(arr[b, h * 2048:(h + 1) * 2048])
    return np.ascontiguousarray(sl.T) if fm else np.ascontiguousarray(sl)


def _tr(a):
    return np.ascontiguousarray(np.asarray(a, np.float32).T)


def _tr3(a):
    return np.ascontiguousarray(np.asarray(a).transpose(0, 2, 1))


_BUILD_CACHE = {}


def _build(key, fn):
    if key not in _BUILD_CACHE:
        _BUILD_CACHE[key] = fn()
    return _BUILD_CACHE[key]


def _b_u1(mode, E):
    nc = bass.Bass("TRN2", target_bir_lowering=False, debug=False)
    x = nc.dram_tensor("x", [2048, 2048], F32, kind="ExternalInput").ap()
    gwT = nc.dram_tensor("gwT", [2048, E], F32, kind="ExternalInput").ap()
    ident = nc.dram_tensor("ident", [128, 128], F32, kind="ExternalInput").ap()
    xn = nc.dram_tensor("xn", [2048, 2048], BF16, kind="ExternalOutput").ap()
    dwT = nc.dram_tensor("dwT", [E, 2048], F32, kind="ExternalOutput").ap()
    with _ES() as ctx:
        p = P(nc, ctx)
        so = plan_ln_router(p, x, gwT, xn, dwT, ident, mode, E, name="u1_")
        p.wait("gpsimd", (so, so.n))
        p.emit()
    return nc


def _b_u2():
    nc = bass.Bass("TRN2", target_bir_lowering=False, debug=False)
    xn = nc.dram_tensor("xn", [2048, 2048], BF16, kind="ExternalInput").ap()
    Wn = dict(cqT=[2048, 512], cqcT=[512, 384], cqrT=[2048, 128], ckvT=[2048, 512],
              ckrT=[2048, 128], ckcT=[512, 384], cvT=[512, 512])
    W = {k: nc.dram_tensor(k, s, F32, kind="ExternalInput").ap() for k, s in Wn.items()}
    Bn = dict(cq_b=512, cqc_b=384, cqr_b=128, ckv_b=512, ckr_b=128, ckc_b=384, cv_b=512)
    B = {k: nc.dram_tensor(k, [s], F32, kind="ExternalInput").ap() for k, s in Bn.items()}
    cos = nc.dram_tensor("cos", [128, 4096], F32, kind="ExternalInput").ap()
    sin = nc.dram_tensor("sin", [128, 4096], F32, kind="ExternalInput").ap()
    ident = nc.dram_tensor("ident", [128, 128], F32, kind="ExternalInput").ap()
    QT = nc.dram_tensor("QT", [512, 2048], BF16, kind="ExternalOutput").ap()
    KT = nc.dram_tensor("KT", [512, 2048], BF16, kind="ExternalOutput").ap()
    VT = nc.dram_tensor("VT", [2048, 512], BF16, kind="ExternalOutput").ap()
    with _ES() as ctx:
        p = P(nc, ctx)
        so = plan_l0proj(p, xn, W, B, QT, KT, VT, cos, sin, ident,
                         q_tiles=[0, 1, 2, 3], kv_tiles=[0, 1, 2, 3], pos_off=0, name="p_")
        p.wait("scalar", (so, so.n))
        p.emit()
    return nc


def _b_u3():
    nc = bass.Bass("TRN2", target_bir_lowering=False, debug=False)
    xn = nc.dram_tensor("xn", [2048, 2048], BF16, kind="ExternalInput").ap()
    dwT = nc.dram_tensor("dwT", [16, 2048], F32, kind="ExternalInput").ap()
    w1t = nc.dram_tensor("w1t", [16, 2048, 1408], BF16, kind="ExternalInput").ap()
    w3t = nc.dram_tensor("w3t", [16, 2048, 1408], BF16, kind="ExternalInput").ap()
    w2t = nc.dram_tensor("w2t", [16, 1408, 2048], BF16, kind="ExternalInput").ap()
    sw1t = nc.dram_tensor("sw1t", [2048, 2816], BF16, kind="ExternalInput").ap()
    sw3t = nc.dram_tensor("sw3t", [2048, 2816], BF16, kind="ExternalInput").ap()
    sw2t = nc.dram_tensor("sw2t", [2816, 2048], BF16, kind="ExternalInput").ap()
    g = nc.dram_tensor("g", [2048, 2048], F32, kind="ExternalOutput").ap()
    with _ES() as ctx:
        p = P(nc, ctx)
        so = plan_moe(p, xn, dwT, w1t, w3t, w2t, sw1t, sw3t, sw2t, g, name="m_")
        p.wait("gpsimd", (so, so.n))
        p.emit()
    return nc


def _b_u4(Tk):
    nc = bass.Bass("TRN2", target_bir_lowering=False, debug=False)
    QT = nc.dram_tensor("QT", [512, 2048], BF16, kind="ExternalInput").ap()
    KT = nc.dram_tensor("KT", [512, Tk], BF16, kind="ExternalInput").ap()
    VT = nc.dram_tensor("VT", [Tk, 512], BF16, kind="ExternalInput").ap()
    g = nc.dram_tensor("g", [2048, 2048], F32, kind="ExternalInput").ap()
    x = nc.dram_tensor("x", [2048, 2048], F32, kind="ExternalInput").ap()
    W = {"outT": nc.dram_tensor("outT", [512, 2048], F32, kind="ExternalInput").ap(),
         "out_b": nc.dram_tensor("out_b", [2048], F32, kind="ExternalInput").ap(),
         "mask": nc.dram_tensor("mask", [128, 128], F32, kind="ExternalInput").ap(),
         "ident": nc.dram_tensor("ident", [128, 128], F32, kind="ExternalInput").ap()}
    x1 = nc.dram_tensor("x1", [2048, 2048], F32, kind="ExternalOutput").ap()
    with _ES() as ctx:
        p = P(nc, ctx)
        so = plan_l0attn(p, QT, KT, VT, g, x, W, x1, Tk, name="a_")
        p.wait("scalar", (so, so.n))
        p.emit()
    return nc


def _b_u6():
    nc = bass.Bass("TRN2", target_bir_lowering=False, debug=False)
    xn = nc.dram_tensor("xn", [2048, 2048], BF16, kind="ExternalInput").ap()
    w1T = nc.dram_tensor("w1T", [8, 2048], F32, kind="ExternalInput").ap()
    gateT = nc.dram_tensor("gateT", [8, 2048, 256], F32, kind="ExternalInput").ap()
    upT = nc.dram_tensor("upT", [8, 2048, 256], F32, kind="ExternalInput").ap()
    downT = nc.dram_tensor("downT", [8, 256, 2048], F32, kind="ExternalInput").ap()
    loraAT = nc.dram_tensor("loraAT", [8, 2048, 16], F32, kind="ExternalInput").ap()
    loraBT = nc.dram_tensor("loraBT", [8, 16, 2048], F32, kind="ExternalInput").ap()
    g1 = nc.dram_tensor("g1", [2048, 2048], F32, kind="ExternalOutput").ap()
    with _ES() as ctx:
        p = P(nc, ctx)
        so = plan_l1ffn(p, xn, w1T, gateT, upT, downT, loraAT, loraBT, g1, name="f_")
        p.wait("scalar", (so, so.n))
        p.emit()
    return nc


def _b_u7():
    nc = bass.Bass("TRN2", target_bir_lowering=False, debug=False)
    xn = nc.dram_tensor("xn", [2048, 2048], BF16, kind="ExternalInput").ap()
    Wv = nc.dram_tensor("Wv", [2048, 2048], F32, kind="ExternalInput").ap()
    Wg = nc.dram_tensor("Wg", [2048, 2048], F32, kind="ExternalInput").ap()
    Wq = nc.dram_tensor("Wq", [2048, 2048], F32, kind="ExternalInput").ap()
    Wo = nc.dram_tensor("Wo", [2048, 2048], F32, kind="ExternalInput").ap()
    Bg = nc.dram_tensor("Bg", [2048], F32, kind="ExternalInput").ap()
    Bq = nc.dram_tensor("Bq", [2048], F32, kind="ExternalInput").ap()
    dpow = nc.dram_tensor("dpow", [128, 2048], F32, kind="ExternalInput").ap()
    g1 = nc.dram_tensor("g1", [2048, 2048], F32, kind="ExternalInput").ap()
    x1 = nc.dram_tensor("x1", [2048, 2048], F32, kind="ExternalInput").ap()
    ident = nc.dram_tensor("ident", [128, 128], F32, kind="ExternalInput").ap()
    mem = nc.dram_tensor("memscr", [2048, 2048], F32, kind="Internal").ap()
    cc_in = nc.dram_tensor("cc_in", [1, 2048], F32, kind="Internal").ap()
    cc_out = nc.dram_tensor("cc_out", [2, 2048], F32, kind="Internal").ap()
    x2 = nc.dram_tensor("x2", [2048, 2048], F32, kind="ExternalOutput").ap()
    with _ES() as ctx:
        p = P(nc, ctx)
        so = plan_l1attn(p, xn, Wv, Wg, Wq, Wo, Bg, Bq, dpow, g1, x1,
                         mem, cc_in, cc_out, x2, ident, name="r_")
        p.wait("scalar", (so, so.n))
        p.emit()
    return nc


def kernel(**inp):
    ident = np.eye(128, dtype=np.float32)
    inv_freq = 1.0 / 10000.0 ** (np.arange(0, 128, 2, dtype=np.float32) / 128)
    t_ = np.arange(4096, dtype=np.float32)
    emb = np.concatenate([np.outer(t_, inv_freq)] * 2, axis=1)
    cosT = np.ascontiguousarray(np.cos(emb).T.astype(np.float32))
    sinT = np.sin(emb).T.astype(np.float32)
    sinT[0:64] = -sinT[0:64]
    sinT = np.ascontiguousarray(sinT)
    cos_sh = [cosT, np.ascontiguousarray(np.roll(cosT, -2048, axis=1))]
    sin_sh = [sinT, np.ascontiguousarray(np.roll(sinT, -2048, axis=1))]
    mask = np.where(np.arange(128)[None, :] <= np.arange(128)[:, None],
                    np.float32(0), np.float32(-1e30))
    dpow1 = np.tile((0.95 ** (np.arange(2048, dtype=np.float64) + 1)
                     ).astype(np.float32), (128, 1))
    dpow0 = np.zeros((128, 2048), np.float32)
    cores = list(range(8))

    nc = _build("fA", _b_fusedA)
    gwT = _tr(inp["L0_ffn_gate_w"])
    w1t = _tr3(inp["L0_ffn_w1"]); w3t = _tr3(inp["L0_ffn_w3"]); w2t = _tr3(inp["L0_ffn_w2"])
    sw1t = np.ascontiguousarray(np.asarray(inp["L0_ffn_sw1"]).T)
    sw3t = np.ascontiguousarray(np.asarray(inp["L0_ffn_sw3"]).T)
    sw2t = np.ascontiguousarray(np.asarray(inp["L0_ffn_sw2"]).T)
    Wm = dict(cqT=_tr(inp["L0_attn_CQ_w"]), cqcT=_tr(inp["L0_attn_CQC_w"]),
              cqrT=_tr(inp["L0_attn_CQR_w"]), ckvT=_tr(inp["L0_attn_CKV_w"]),
              ckrT=_tr(inp["L0_attn_CKR_w"]), ckcT=_tr(inp["L0_attn_CKC_w"]),
              cvT=_tr(inp["L0_attn_CV_w"]))
    Bm = dict(cq_b=inp["L0_attn_CQ_b"], cqc_b=inp["L0_attn_CQC_b"],
              cqr_b=inp["L0_attn_CQR_b"], ckv_b=inp["L0_attn_CKV_b"],
              ckr_b=inp["L0_attn_CKR_b"], ckc_b=inp["L0_attn_CKC_b"],
              cv_b=inp["L0_attn_CV_b"])
    cosA = [np.ascontiguousarray(np.roll(cos_sh[0], 2048, axis=1)), cos_sh[0]]
    sinA = [np.ascontiguousarray(np.roll(sin_sh[0], 2048, axis=1)), sin_sh[0]]
    x_sh = [_cs(inp["L0_attn_x"], c) for c in cores]
    xfull = []
    for c in cores:
        b, h = c // 2, c % 2
        if h == 1:
            xf = np.ascontiguousarray(np.asarray(inp["L0_attn_x"][b]))
        else:
            xf = np.ascontiguousarray(np.concatenate([x_sh[c], x_sh[c]], axis=0))
        xfull.append(xf)
    ims = []
    for c in cores:
        im = {"x": xfull[c], "gwT": gwT, "ident": ident,
              "cos": cosA[c % 2], "sin": sinA[c % 2],
              "w1t": w1t, "w3t": w3t, "w2t": w2t,
              "sw1t": sw1t, "sw3t": sw3t, "sw2t": sw2t}
        im.update(Wm); im.update(Bm)
        ims.append(im)
    r = run_bass_kernel_spmd(nc, ims, core_ids=cores)
    QT = [r.results[c]["QT"] for c in cores]
    KTf = [r.results[c]["KT"] for c in cores]
    VTf = [r.results[c]["VT"] for c in cores]
    g0 = [r.results[c]["g"] for c in cores]

    outT = _tr(inp["L0_attn_OUT_w"])
    out_b = np.asarray(inp["L0_attn_OUT_b"])
    x1 = [None] * 8
    for h, cset in ((0, [0, 2, 4, 6]), (1, [1, 3, 5, 7])):
        Tk = 2048 * (h + 1)
        nc = _build(("u4", Tk), lambda Tk=Tk: _b_u4(Tk))
        ims = []
        for c in cset:
            if h == 0:
                Kf = np.ascontiguousarray(KTf[c][:, 2048:])
                Vf = np.ascontiguousarray(VTf[c][2048:, :])
            else:
                Kf, Vf = KTf[c], VTf[c]
            ims.append({"QT": QT[c], "KT": Kf, "VT": Vf, "g": g0[c], "x": x_sh[c],
                        "outT": outT, "out_b": out_b, "mask": mask, "ident": ident})
        r = run_bass_kernel_spmd(nc, ims, core_ids=cset)
        for i, c in enumerate(cset):
            x1[c] = r.results[i]["x1"]

    nc = _build("u1l1", lambda: _b_u1("l1", 8))
    GT = np.ascontiguousarray(np.asarray(inp["L1_ffn_G"], np.float32))
    r = run_bass_kernel_spmd(nc, [{"x": x1[c], "gwT": GT, "ident": ident}
                                  for c in cores], core_ids=cores)
    xn1 = [r.results[c]["xn"] for c in cores]
    w1T = [r.results[c]["dwT"] for c in cores]

    nc = _build("u6", _b_u6)
    gateT = _tr3(inp["L1_ffn_gate_w"]); upT = _tr3(inp["L1_ffn_up_w"])
    downT = _tr3(inp["L1_ffn_down_w"])
    loraAT = _tr3(inp["L1_ffn_lora_A"]); loraBT = _tr3(inp["L1_ffn_lora_B"])
    r = run_bass_kernel_spmd(nc, [
        {"xn": xn1[c], "w1T": w1T[c], "gateT": gateT, "upT": upT, "downT": downT,
         "loraAT": loraAT, "loraBT": loraBT} for c in cores], core_ids=cores)
    g1 = [r.results[c]["g1"] for c in cores]

    nc = _build("u7", _b_u7)
    com = {"Wv": np.asarray(inp["L1_attn_Wv"]), "Wg": np.asarray(inp["L1_attn_Wg"]),
           "Wq": np.asarray(inp["L1_attn_Wq"]), "Wo": np.asarray(inp["L1_attn_Wo"]),
           "Bg": np.asarray(inp["L1_attn_bg"]), "Bq": np.asarray(inp["L1_attn_bq"]),
           "ident": ident}
    ims = []
    for c in cores:
        im = {"xn": xn1[c], "g1": g1[c], "x1": x1[c],
              "dpow": dpow1 if c % 2 else dpow0}
        im.update(com)
        ims.append(im)
    r = run_bass_kernel_spmd(nc, ims, core_ids=cores)
    out = np.empty((4, 4096, 2048), np.float32)
    for c in cores:
        b, h = c // 2, c % 2
        out[b, h * 2048:(h + 1) * 2048] = r.results[c]["x2"]
    return out


def _b_fusedA():
    nc = bass.Bass("TRN2", target_bir_lowering=False, debug=False)
    x = nc.dram_tensor("x", [4096, 2048], F32, kind="ExternalInput").ap()
    gwT = nc.dram_tensor("gwT", [2048, 16], F32, kind="ExternalInput").ap()
    ident = nc.dram_tensor("ident", [128, 128], F32, kind="ExternalInput").ap()
    Wn = dict(cqT=[2048, 512], cqcT=[512, 384], cqrT=[2048, 128], ckvT=[2048, 512],
              ckrT=[2048, 128], ckcT=[512, 384], cvT=[512, 512])
    W = {k: nc.dram_tensor(k, s, F32, kind="ExternalInput").ap() for k, s in Wn.items()}
    Bn = dict(cq_b=512, cqc_b=384, cqr_b=128, ckv_b=512, ckr_b=128, ckc_b=384, cv_b=512)
    B = {k: nc.dram_tensor(k, [s], F32, kind="ExternalInput").ap() for k, s in Bn.items()}
    cos = nc.dram_tensor("cos", [128, 4096], F32, kind="ExternalInput").ap()
    sin = nc.dram_tensor("sin", [128, 4096], F32, kind="ExternalInput").ap()
    w1t = nc.dram_tensor("w1t", [16, 2048, 1408], BF16, kind="ExternalInput").ap()
    w3t = nc.dram_tensor("w3t", [16, 2048, 1408], BF16, kind="ExternalInput").ap()
    w2t = nc.dram_tensor("w2t", [16, 1408, 2048], BF16, kind="ExternalInput").ap()
    sw1t = nc.dram_tensor("sw1t", [2048, 2816], BF16, kind="ExternalInput").ap()
    sw3t = nc.dram_tensor("sw3t", [2048, 2816], BF16, kind="ExternalInput").ap()
    sw2t = nc.dram_tensor("sw2t", [2816, 2048], BF16, kind="ExternalInput").ap()
    xn = nc.dram_tensor("xn", [2048, 4096], BF16, kind="Internal").ap()
    dwT = nc.dram_tensor("dwT", [16, 4096], F32, kind="Internal").ap()
    QT = nc.dram_tensor("QT", [512, 2048], BF16, kind="ExternalOutput").ap()
    KT = nc.dram_tensor("KT", [512, 4096], BF16, kind="ExternalOutput").ap()
    VT = nc.dram_tensor("VT", [4096, 512], BF16, kind="ExternalOutput").ap()
    g = nc.dram_tensor("g", [2048, 2048], F32, kind="ExternalOutput").ap()
    with _ES() as ctx:
        p = P(nc, ctx)
        p.push_scope()
        plan_ln_router(p, x, gwT, xn, dwT, ident, "l0", 16, T=4096, name="u1_")
        p.pop_scope()
        p.barrier()
        p.push_scope()
        plan_l0proj(p, xn, W, B, QT, KT, VT, cos, sin, ident,
                    q_tiles=[4, 5, 6, 7], kv_tiles=[0, 1, 2, 3, 4, 5, 6, 7],
                    pos_off=0, name="p_")
        p.pop_scope()
        p.barrier()
        p.push_scope()
        so = plan_moe(p, xn, dwT, w1t, w3t, w2t, sw1t, sw3t, sw2t, g,
                      name="m_", t_off=2048)
        p.pop_scope()
        p.barrier()
        p.emit()
    return nc


# version 05 (final)
# speedup vs baseline: 1.1699x; 1.1699x over previous
import numpy as np
from contextlib import ExitStack
import concourse.bass as bass
import concourse.mybir as mybir
from concourse.bass_utils import run_bass_kernel_spmd

F32, BF16 = mybir.dt.float32, mybir.dt.bfloat16
AF, ALU, AX = mybir.ActivationFunctionType, mybir.AluOpType, mybir.AxisListType
ENGS = ("sync", "tensor", "vector", "scalar", "gpsimd")


def bc(tile_ap, n):
    a = tile_ap
    return bass.AP(a.tensor, a.offset, [list(a.ap[0]), [0, n]])


class Sem:
    __slots__ = ("h", "n")
    def __init__(self, h):
        self.h = h
        self.n = 0


class P:
    def __init__(self, nc, ctx):
        self.nc = nc
        self.ctx = ctx
        self.q = {e: [] for e in ENGS}
        self.all_sems = []
        self._scopes = []

    def push_scope(self):
        from contextlib import ExitStack
        self._scopes.append(ExitStack())

    def pop_scope(self):
        self._scopes.pop().close()

    def _actx(self):
        return self._scopes[-1] if self._scopes else self.ctx

    def sem(self, name):
        s = Sem(self.ctx.enter_context(self.nc.semaphore(name)))
        self.all_sems.append(s)
        return s

    def sb(self, name, shape, dt=F32):
        return self._actx().enter_context(self.nc.sbuf_tensor(name, shape, dt))

    def ps(self, name, shape, dt=F32):
        return self._actx().enter_context(self.nc.psum_tensor(name, shape, dt))

    def barrier(self):
        snap = [(s, s.n) for s in self.all_sems if s.n > 0]
        for eng in ENGS:
            self.wait(eng, *snap)

    @staticmethod
    def _flat(waits):
        out = []
        for x in waits:
            if x is None:
                continue
            if isinstance(x, list):
                out.extend((s.h, v) for (s, v) in x if x is not None)
            else:
                s, v = x
                out.append((s.h, v))
        return out

    def op(self, eng, fn, waits=(), inc=None, inck=1):
        w = self._flat(waits)
        if inc is not None and inck >= 16 and inc.n > 0:
            w.append((inc.h, inc.n))
        tgt = None if inc is None else inc.h
        if inc is not None:
            inc.n += inck
        post = None if inc is None else inc.n
        def g(e, fn=fn, w=w, tgt=tgt, inck=inck):
            for h, v in w:
                e.wait_ge(h, v)
            ins = fn(e)
            if tgt is not None:
                ins.then_inc(tgt, inck)
        self.q[eng].append(g)
        return post

    def wait(self, eng, *waits):
        w = self._flat(waits)
        def g(e, w=w):
            for h, v in w:
                e.wait_ge(h, v)
        self.q[eng].append(g)

    def emit(self):
        with self.nc.Block() as block:
            for eng in ENGS:
                if self.q[eng]:
                    getattr(block, eng)(lambda e, eng=eng: [f(e) for f in self.q[eng]])


D = 2048
NCH = D // 128


def plan_ln_router(p, x_hbm, gwT_hbm, xn_hbm, dwT_hbm, ident_hbm, mode, E, T=2048,
                   name="", do_tr=True, do_router=True):
    n = name
    NT = T // 128
    x_tm = [p.sb(f"{n}x{i}", [128, D]) for i in range(2)]
    xn_tm = [p.sb(f"{n}xn{i}", [128, D]) for i in range(2)]
    xnf = p.sb(f"{n}xnf", [128, D])
    stage = [p.sb(f"{n}stg{i}", [128, D], BF16) for i in range(2)]
    junk = p.sb(f"{n}junk", [128, D], BF16)
    s1 = [p.sb(f"{n}s1{i}", [128, 1]) for i in range(2)]
    mu = [p.sb(f"{n}mu{i}", [128, 1]) for i in range(2)]
    mu2 = [p.sb(f"{n}mu2{i}", [128, 1]) for i in range(2)]
    var = [p.sb(f"{n}var{i}", [128, 1]) for i in range(2)]
    rstd = [p.sb(f"{n}rstd{i}", [128, 1]) for i in range(2)]
    ssq = [p.sb(f"{n}ssq{i}", [128, 1]) for i in range(2)]
    eps = p.sb(f"{n}eps", [128, 1])
    gwT = p.sb(f"{n}gwT", [128, NCH * E])
    ident = p.sb(f"{n}id", [128, 128])
    dwT_sb = p.sb(f"{n}dwT", [E, T])
    z_sb = p.sb(f"{n}z", [128, E])
    m12 = p.sb(f"{n}m12", [128, 2])
    eq1 = p.sb(f"{n}eq1", [128, E])
    eq2 = p.sb(f"{n}eq2", [128, E])
    vz = p.sb(f"{n}vz", [128, E])
    den = p.sb(f"{n}den", [128, 1])
    rden = p.sb(f"{n}rden", [128, 1])
    w12 = p.sb(f"{n}w12", [128, 2])
    t1 = p.sb(f"{n}t1", [128, E])
    ez = p.sb(f"{n}ez", [128, E])
    psm = p.sb(f"{n}psm", [128, 1])
    pst = [p.ps(f"{n}pst{i}", [128, 512]) for i in range(4)]
    z_ps = p.ps(f"{n}zps", [128, 512])
    dw_ps = p.ps(f"{n}dwps", [128, 512])
    ld = p.sem(f"{n}ld")
    va = p.sem(f"{n}va")
    sa = p.sem(f"{n}sa")
    mm = p.sem(f"{n}mm")
    mz = p.sem(f"{n}mz")
    so = p.sem(f"{n}so")

    p.op("sync", lambda e: e.dma_start(
        out=gwT.ap().rearrange("p (c e) -> p c e", e=E),
        in_=gwT_hbm.rearrange("(c p) e -> p c e", p=128)), inc=ld, inck=16)
    p.op("sync", lambda e: e.dma_start(out=ident[:], in_=ident_hbm), inc=ld, inck=16)
    p.op("vector", lambda e: e.memset(eps[:], 1e-5), inc=va)
    ld_base = ld.n
    consts_v = (ld, ld_base)
    eps_v = (va, va.n)

    x_free = [None, None]
    xn_free = [None, None]
    gate = None
    stage_free = [None, None]
    bank_free = [None] * 4
    xnf_free = [None] * NCH
    stage_av = [None] * NCH
    zps_free = None
    dwps_free = None

    for tc in range(NT):
        bi = tc % 2
        w = [consts_v] if tc < 2 else []
        if x_free[bi] is not None:
            w.append(x_free[bi])
        ldv = p.op("sync", lambda e, tc=tc, bi=bi: e.dma_start(
            out=x_tm[bi][:], in_=x_hbm[tc * 128:(tc + 1) * 128, :]), waits=w,
            inc=ld, inck=16)
        xv = (ld, ldv)
        sqv = p.op("scalar", lambda e, bi=bi: e.activation(
            junk[:], x_tm[bi][:], AF.Square, accum_out=ssq[bi][:]),
            waits=[xv, gate], inc=sa)
        s1v = p.op("vector", lambda e, bi=bi: e.reduce_sum(
            s1[bi][:], x_tm[bi][:], axis=AX.X), waits=[xv], inc=va)
        muv = p.op("vector", lambda e, bi=bi: e.tensor_scalar_mul(
            mu[bi][:], s1[bi][:], 1.0 / D), waits=[(va, s1v)], inc=va)
        mu2v = p.op("vector", lambda e, bi=bi: e.tensor_mul(
            mu2[bi][:], mu[bi][:], mu[bi][:]), waits=[(va, muv)], inc=va)
        varv = p.op("vector", lambda e, bi=bi: e.scalar_tensor_tensor(
            var[bi][:], ssq[bi][:], 1.0 / D, mu2[bi][:], ALU.mult, ALU.subtract),
            waits=[(sa, sqv), (va, mu2v)], inc=va)
        lnv = p.op("scalar", lambda e, bi=bi: e.activation(
            rstd[bi][:], var[bi][:], AF.Ln, bias=eps[:]),
            waits=[(va, varv), eps_v, gate], inc=sa)
        rsv = p.op("scalar", lambda e, bi=bi: e.activation(
            rstd[bi][:], rstd[bi][:], AF.Exp, scale=-0.5), waits=[(sa, lnv)], inc=sa)
        w = [(sa, rsv), (va, muv)]
        if xn_free[bi] is not None:
            w.append(xn_free[bi])
        xnv = p.op("vector", lambda e, bi=bi: e.tensor_scalar(
            xn_tm[bi][:], x_tm[bi][:], mu[bi][:], rstd[bi][:], ALU.subtract, ALU.mult),
            waits=w, inc=va)
        x_free[bi] = (va, xnv)
        if not do_tr:
            gate = None
            continue
        ev_d = [None] * NCH
        for k in range(NCH):
            b = k % 4
            w = [(va, xnv)] if k < 4 else [(va, xnv)]
            if bank_free[b] is not None:
                w.append(bank_free[b])
            tv = p.op("tensor", lambda e, k=k, bi=bi, b=b: e.transpose(
                pst[b][:, 0:128], xn_tm[bi][:, k * 128:(k + 1) * 128], ident[:]),
                waits=w, inc=mm)
            w = [(mm, tv)]
            if xnf_free[k] is not None:
                w.append(xnf_free[k])
            dv = p.op("vector", lambda e, k=k, b=b: e.tensor_copy(
                xnf[:, k * 128:(k + 1) * 128], pst[b][:, 0:128]), waits=w, inc=va)
            w2 = [(va, dv)]
            if k == 0 and stage_free[bi] is not None:
                w2.append(stage_free[bi])
            av = p.op("scalar", lambda e, k=k, b=b, bi=bi: e.copy(
                stage[bi][:, k * 128:(k + 1) * 128], xnf[:, k * 128:(k + 1) * 128]),
                waits=w2, inc=sa)
            stage_av[k] = (sa, av)
            bank_free[b] = [(va, dv)]
            ev_d[k] = (va, dv)
        xn_free[bi] = (mm, mm.n)
        sv = p.op("gpsimd", lambda e, bi=bi, tc=tc: e.dma_start(
            out=xn_hbm.rearrange("(c p) t -> p c t", p=128)[:, :, tc * 128:(tc + 1) * 128],
            in_=stage[bi].ap().rearrange("p (c t) -> p c t", t=128)),
            waits=[(sa, sa.n)], inc=so, inck=16)
        stage_free[bi] = (so, sv)
        if not do_router:
            gate = None
            continue
        for k in range(NCH):
            w = [ev_d[k]]
            if k == 0 and zps_free is not None:
                w.append(zps_free)
            if k == 0:
                w.append(consts_v)
            zv = p.op("tensor", lambda e, k=k: e.matmul(
                z_ps[:, 0:E], xnf[:, k * 128:(k + 1) * 128], gwT[:, k * E:(k + 1) * E],
                start=(k == 0), stop=(k == NCH - 1)), waits=w, inc=mz)
            xnf_free[k] = [(mz, zv), stage_av[k]]
        zmm = (mz, mz.n)
        zc = p.op("vector", lambda e: e.tensor_copy(z_sb[:], z_ps[:, 0:E]),
                  waits=[zmm, gate], inc=va)
        zps_free = (va, zc)
        if mode == "l1":
            ngv = p.op("vector", lambda e: e.reduce_max(
                psm[:], z_sb[:], axis=AX.X, negate=True), waits=[(va, zc)], inc=va)
            exv = p.op("scalar", lambda e: e.activation(
                ez[:], z_sb[:], AF.Exp, bias=psm[:], accum_out=den[:]),
                waits=[(va, ngv)], inc=sa)
            rdv = p.op("vector", lambda e: e.reciprocal(rden[:], den[:]),
                       waits=[(sa, exv)], inc=va)
            pv = p.op("vector", lambda e: e.tensor_scalar_mul(
                z_sb[:], ez[:], rden[:]), waits=[(va, rdv)], inc=va)
            topin_v = (va, pv)
        else:
            topin_v = (va, zc)
        m1v = p.op("vector", lambda e: e.reduce_max(m12[:, 0:1], z_sb[:], axis=AX.X),
                   waits=[topin_v], inc=va)
        e1v = p.op("vector", lambda e: e.tensor_scalar(
            eq1[:], z_sb[:], m12[:, 0:1], None, ALU.is_equal), waits=[(va, m1v)], inc=va)
        vzv = p.op("vector", lambda e: e.scalar_tensor_tensor(
            vz[:], eq1[:], -1e30, z_sb[:], ALU.mult, ALU.add), waits=[(va, e1v)], inc=va)
        m2v = p.op("vector", lambda e: e.reduce_max(m12[:, 1:2], vz[:], axis=AX.X),
                   waits=[(va, vzv)], inc=va)
        e2v = p.op("vector", lambda e: e.tensor_scalar(
            eq2[:], vz[:], m12[:, 1:2], None, ALU.is_equal), waits=[(va, m2v)], inc=va)
        if mode == "l0":
            sgv = p.op("scalar", lambda e: e.activation(
                w12[:], m12[:], AF.Sigmoid), waits=[(va, m2v)], inc=sa)
            dnv = p.op("vector", lambda e: e.reduce_sum(den[:], w12[:], axis=AX.X),
                       waits=[(sa, sgv)], inc=va)
            rdv = p.op("vector", lambda e: e.reciprocal(rden[:], den[:]),
                       waits=[(va, dnv)], inc=va)
            wv = p.op("vector", lambda e: e.tensor_scalar_mul(
                w12[:], w12[:], rden[:]), waits=[(va, rdv), (sa, sgv)], inc=va)
            w_v = (va, wv)
        else:
            wv = p.op("vector", lambda e: e.tensor_copy(w12[:], m12[:]),
                      waits=[(va, m2v)], inc=va)
            w_v = (va, wv)
        t1v = p.op("vector", lambda e: e.tensor_scalar_mul(
            t1[:], eq1[:], w12[:, 0:1]), waits=[w_v, (va, e1v)], inc=va)
        dwv = p.op("vector", lambda e: e.scalar_tensor_tensor(
            vz[:], eq2[:], w12[:, 1:2], t1[:], ALU.mult, ALU.add),
            waits=[(va, t1v), (va, e2v)], inc=va)
        w = [(va, dwv), consts_v]
        if dwps_free is not None:
            w.append(dwps_free)
        dtv = p.op("tensor", lambda e: e.transpose(
            dw_ps[0:E, 0:128], vz[:, 0:E], ident[:]), waits=w, inc=mz)
        dcv = p.op("vector", lambda e, tc=tc: e.tensor_copy(
            dwT_sb[:, tc * 128:(tc + 1) * 128], dw_ps[0:E, 0:128]),
            waits=[(mz, dtv)], inc=va)
        dwps_free = (va, dcv)
        gate = [(va, dwv), (sa, sa.n)]

    dsv = p.op("gpsimd", lambda e: e.dma_start(out=dwT_hbm[:, 0:T], in_=dwT_sb[:]),
               waits=[(va, va.n)], inc=so, inck=16)
    return so


D = 2048
NDC = 16
F_E = 1408
F_S = 2816
NOC = 16


def plan_moe(p, xn_hbm, dwT_hbm, w1t, w3t, w2t, sw1t, sw3t, sw2t, g_hbm,
             E=16, T=2048, TT=512, name="", apply_silu=True, parts=("bc","A","B","g"), nexp=16, t_off=0):
    n = name
    NTT = T // TT
    NF_E, NF_S = F_E // 128, F_S // 128
    xn_sb = p.sb(f"{n}xn", [128, NDC * TT], BF16)
    dw_row = p.sb(f"{n}dwr", [1, TT])
    ones = p.sb(f"{n}ones", [1, 128])
    dwbc = p.sb(f"{n}dwbc", [128, TT])
    w1b = [p.sb(f"{n}w1b{i}", [128, NDC * 128], BF16) for i in range(2)]
    w3b = [p.sb(f"{n}w3b{i}", [128, NDC * 128], BF16) for i in range(2)]
    w2b = [p.sb(f"{n}w2b{i}", [128, NF_S * 128], BF16) for i in range(2)]
    sa_sb = p.sb(f"{n}sa", [128, TT], BF16)
    tmp_sb = p.sb(f"{n}tmp", [128, TT], BF16)
    h_sb = p.sb(f"{n}h", [128, NF_S * TT], BF16)
    y_sb = p.sb(f"{n}y", [128, NOC * TT])
    g_st = [p.sb(f"{n}g{i}", [128, TT]) for i in range(2)]
    a_ps = p.ps(f"{n}aps", [128, TT])
    b_ps = p.ps(f"{n}bps", [128, TT])
    y_ps = [p.ps(f"{n}yps{i}", [128, TT]) for i in range(2)]
    bc_ps = p.ps(f"{n}bcps", [128, TT])
    ld = p.sem(f"{n}ld")
    w1s = p.sem(f"{n}w1s")
    w3s = p.sem(f"{n}w3s")
    w2s = p.sem(f"{n}w2s")
    mm = p.sem(f"{n}mm")
    va = p.sem(f"{n}va")
    sa = p.sem(f"{n}sa")
    so = p.sem(f"{n}so")

    p.op("vector", lambda e: e.memset(ones[:], 1.0), inc=va)
    consts = [(va, va.n)]
    dwrow_free = None

    xn_free = None
    w1_free = [None, None]; w3_free = [None, None]; w2_free = [None, None]
    aps_free = None; bps_free = None; bc_free = None
    yps_free = [None, None]
    h_free = [None] * NF_S
    y_last = [None] * NOC
    g_free = [None, None]
    sa_free = None; tmp_free = None; dwbc_free = None
    wq13 = 0
    wq2 = 0

    for tt in range(NTT):
        w = [xn_free] if xn_free else []
        xv = p.op("sync", lambda e, tt=tt: e.dma_start(
            out=xn_sb.ap().rearrange("p (c t) -> p c t", t=TT),
            in_=xn_hbm.rearrange("(c p) t -> p c t", p=128)[:, :, t_off + tt * TT:t_off + (tt + 1) * TT]),
            waits=w, inc=ld, inck=16)
        xnv = (ld, xv)
        xn_readers = []

        experts = [("sh", None)] + [("ex", ei) for ei in range(nexp)]
        for kind, ei in experts:
            shared = kind == "sh"
            NF = NF_S if shared else NF_E
            if not shared and "bc" in parts:
                w = [dwrow_free] if dwrow_free else []
                dv_ = p.op("sync", lambda e, ei=ei, tt=tt: e.dma_start(
                    out=dw_row[:], in_=dwT_hbm[ei:ei + 1, t_off + tt * TT:t_off + (tt + 1) * TT]),
                    waits=w, inc=ld, inck=16)
                w = [consts[0], (ld, dv_)]
                if bc_free:
                    w.append(bc_free)
                bv = p.op("tensor", lambda e: e.matmul(
                    bc_ps[:], ones[:], dw_row[:],
                    start=True, stop=True), waits=w, inc=mm)
                dwrow_free = (mm, bv)
                w = [(mm, bv)]
                if dwbc_free:
                    w.append(dwbc_free)
                bcv = p.op("vector", lambda e: e.tensor_copy(dwbc[:], bc_ps[:]),
                           waits=w, inc=va)
                bc_free = (va, bcv)
                dwbc_v = (va, bcv)
            if "A" not in parts:
                continue
            for fc in range(NF):
                bi = wq13 % 2
                wq13 += 1
                w = [w1_free[bi]] if w1_free[bi] else []
                src1 = (sw1t if shared else w1t[ei]) if True else None
                l1 = p.op("sync", lambda e, fc=fc, bi=bi, src=(sw1t if shared else w1t[ei]): e.dma_start(
                    out=w1b[bi].ap().rearrange("p (c f) -> p c f", f=128),
                    in_=src.rearrange("(c p) f -> p c f", p=128)[:, :, fc * 128:(fc + 1) * 128]),
                    waits=w, inc=w1s, inck=16)
                w = [w3_free[bi]] if w3_free[bi] else []
                l3 = p.op("gpsimd", lambda e, fc=fc, bi=bi, src=(sw3t if shared else w3t[ei]): e.dma_start(
                    out=w3b[bi].ap().rearrange("p (c f) -> p c f", f=128),
                    in_=src.rearrange("(c p) f -> p c f", p=128)[:, :, fc * 128:(fc + 1) * 128]),
                    waits=w, inc=w3s, inck=16)
                for dc in range(NDC):
                    w = [(w1s, l1), xnv]
                    if dc == 0 and aps_free:
                        w.append(aps_free)
                    av_ = p.op("tensor", lambda e, dc=dc, bi=bi: e.matmul(
                        a_ps[:], w1b[bi][:, dc * 128:(dc + 1) * 128],
                        xn_sb[:, dc * TT:(dc + 1) * TT],
                        start=(dc == 0), stop=(dc == NDC - 1)), waits=w, inc=mm)
                amm = (mm, av_)
                for dc in range(NDC):
                    w = [(w3s, l3), xnv]
                    if dc == 0 and bps_free:
                        w.append(bps_free)
                    bv_ = p.op("tensor", lambda e, dc=dc, bi=bi: e.matmul(
                        b_ps[:], w3b[bi][:, dc * 128:(dc + 1) * 128],
                        xn_sb[:, dc * TT:(dc + 1) * TT],
                        start=(dc == 0), stop=(dc == NDC - 1)), waits=w, inc=mm)
                bmm = (mm, bv_)
                xn_readers.append((mm, bv_))
                w1_free[bi] = amm
                w3_free[bi] = bmm
                w = [amm]
                if sa_free:
                    w.append(sa_free)
                sv = p.op("scalar", lambda e: e.activation(sa_sb[:], a_ps[:], AF.Silu),
                          waits=w, inc=sa)
                aps_free = (sa, sv)
                if shared:
                    w = [bmm, (sa, sv)]
                    if h_free[fc]:
                        w.append(h_free[fc])
                    hv = p.op("vector", lambda e, fc=fc: e.tensor_mul(
                        h_sb[:, fc * TT:(fc + 1) * TT], sa_sb[:], b_ps[:]),
                        waits=w, inc=va)
                    bps_free = (va, hv)
                else:
                    w = [bmm, dwbc_v] if "bc" in parts else [bmm]
                    if tmp_free:
                        w.append(tmp_free)
                    tv = p.op("vector", lambda e: e.tensor_mul(
                        tmp_sb[:], b_ps[:], dwbc[:]), waits=w, inc=va)
                    bps_free = (va, tv)
                    w = [(va, tv), (sa, sv)]
                    if h_free[fc]:
                        w.append(h_free[fc])
                    hv = p.op("vector", lambda e, fc=fc: e.tensor_mul(
                        h_sb[:, fc * TT:(fc + 1) * TT], sa_sb[:], tmp_sb[:]),
                        waits=w, inc=va)
                    tmp_free = (va, hv)
                sa_free = (va, hv)
                h_free[fc] = (va, hv)
                h_done = (va, hv)
                h_free[fc] = h_done
            h_all = (va, va.n)
            if "B" not in parts:
                continue
            for oc in range(NOC):
                bi = wq2 % 2
                wq2 += 1
                w = [w2_free[bi]] if w2_free[bi] else []
                l2 = p.op("sync", lambda e, oc=oc, bi=bi, src=(sw2t if shared else w2t[ei]), NF=NF: e.dma_start(
                    out=w2b[bi].ap().rearrange("p (c o) -> p c o", o=128)[:, 0:NF, :],
                    in_=src.rearrange("(c p) o -> p c o", p=128)[:, :, oc * 128:(oc + 1) * 128]),
                    waits=w, inc=w2s, inck=16)
                yb = oc % 2
                for fc in range(NF):
                    w = [(w2s, l2), h_all]
                    if fc == 0 and yps_free[yb]:
                        w.append(yps_free[yb])
                    yv = p.op("tensor", lambda e, fc=fc, oc=oc, bi=bi, yb=yb, NF=NF: e.matmul(
                        y_ps[yb][:], w2b[bi][:, fc * 128:(fc + 1) * 128],
                        h_sb[:, fc * TT:(fc + 1) * TT],
                        start=(fc == 0), stop=(fc == NF - 1)), waits=w, inc=mm)
                ymm = (mm, yv)
                w2_free[bi] = ymm
                for fc in range(NF):
                    h_free[fc] = ymm
                w = [ymm]
                if y_last[oc]:
                    w.append(y_last[oc])
                if shared:
                    cv = p.op("vector", lambda e, oc=oc, yb=yb: e.tensor_copy(
                        y_sb[:, oc * TT:(oc + 1) * TT], y_ps[yb][:]), waits=w, inc=va)
                else:
                    cv = p.op("vector", lambda e, oc=oc, yb=yb: e.tensor_add(
                        y_sb[:, oc * TT:(oc + 1) * TT],
                        y_sb[:, oc * TT:(oc + 1) * TT], y_ps[yb][:]), waits=w, inc=va)
                y_last[oc] = (va, cv)
                yps_free[yb] = (va, cv)
        if "g" not in parts or "B" not in parts:
            xn_free = [(mm, mm.n)]
            continue
        for oc in range(NOC):
            gb = oc % 2
            w = [y_last[oc]]
            if g_free[gb]:
                w.append(g_free[gb])
            if apply_silu:
                gv = p.op("scalar", lambda e, oc=oc, gb=gb: e.activation(
                    g_st[gb][:], y_sb[:, oc * TT:(oc + 1) * TT], AF.Silu),
                    waits=w, inc=sa)
            else:
                gv = p.op("scalar", lambda e, oc=oc, gb=gb: e.copy(
                    g_st[gb][:], y_sb[:, oc * TT:(oc + 1) * TT]), waits=w, inc=sa)
            sv = p.op("gpsimd", lambda e, oc=oc, gb=gb, tt=tt: e.dma_start(
                out=g_hbm[oc * 128:(oc + 1) * 128, tt * TT:(tt + 1) * TT],
                in_=g_st[gb][:]), waits=[(sa, gv)], inc=so, inck=16)
            g_free[gb] = (so, sv)
            y_last[oc] = [(so, sv), (sa, gv)]
        xn_free = [(mm, mm.n)]
    return so


D = 2048
NDC = 16


def plan_l0proj(p, xn_hbm, W, B, QT_hbm, KT_hbm, VT_hbm, cos_hbm, sin_hbm, ident_hbm,
                q_tiles, kv_tiles, pos_off=0, name=""):
    n = name
    TT = 512
    cqT = p.sb(f"{n}cqT", [128, NDC * 512], BF16)
    ckvT = p.sb(f"{n}ckvT", [128, NDC * 512], BF16)
    cqrT = p.sb(f"{n}cqrT", [128, NDC * 128], BF16)
    ckrT = p.sb(f"{n}ckrT", [128, NDC * 128], BF16)
    cqcT = p.sb(f"{n}cqcT", [128, 4 * 384], BF16)
    ckcT = p.sb(f"{n}ckcT", [128, 4 * 384], BF16)
    cvT = p.sb(f"{n}cvT", [128, 4 * 512], BF16)
    bias = p.sb(f"{n}bias", [128, 4 * 7])
    ident_b = p.sb(f"{n}idb", [128, 128], BF16)
    xn_sb = [p.sb(f"{n}xn{i}", [128, NDC * TT], BF16) for i in range(2)]
    cq_sb = p.sb(f"{n}cq", [128, 4 * TT], BF16)
    ckv_sb = p.sb(f"{n}ckv", [128, 4 * TT], BF16)
    r_f32 = p.sb(f"{n}rf", [128, TT])
    rt_f32 = p.sb(f"{n}rt", [128, TT])
    t1_f32 = p.sb(f"{n}t1", [128, TT])
    rs_f32 = p.sb(f"{n}rs", [128, TT])
    cos_sb = p.sb(f"{n}cos", [128, 4096])
    sin_sb = p.sb(f"{n}sin", [128, 4096])
    q_st = [p.sb(f"{n}qst{i}", [128, 4 * TT], BF16) for i in range(2)]
    k_st = [p.sb(f"{n}kst{i}", [128, 4 * TT], BF16) for i in range(2)]
    v_fm = p.sb(f"{n}vfm", [128, TT], BF16)
    v_st = [p.sb(f"{n}vst{i}", [128, 4 * TT], BF16) for i in range(2)]
    pp = [p.ps(f"{n}pp{i}", [128, 512]) for i in range(2)]
    pt = [p.ps(f"{n}pt{i}", [128, 1024], BF16) for i in range(2)]
    wld = p.sem(f"{n}wld")
    ld = p.sem(f"{n}ld")
    mm = p.sem(f"{n}mm")
    va = p.sem(f"{n}va")
    sa = p.sem(f"{n}sa")
    so = p.sem(f"{n}so")

    def wload(dst, src, ochunks, osz):
        return p.op("gpsimd", lambda e, dst=dst, src=src: e.dma_start(
            out=dst.ap().rearrange("p (c o) -> p c o", o=osz),
            in_=src.rearrange("(c p) o -> p c o", p=128)), inc=wld, inck=16)
    wload(cqT, W["cqT"], NDC, 512)
    wload(ckvT, W["ckvT"], NDC, 512)
    wload(cqrT, W["cqrT"], NDC, 128)
    wload(ckrT, W["ckrT"], NDC, 128)
    wload(cqcT, W["cqcT"], 4, 384)
    wload(ckcT, W["ckcT"], 4, 384)
    wload(cvT, W["cvT"], 4, 512)
    p.op("gpsimd", lambda e: e.dma_start(out=ident_b[:], in_=ident_hbm), inc=wld, inck=16)
    wv = (wld, wld.n)
    bmap = {"cq_b": 0, "cqc_b": 4, "ckv_b": 8, "ckc_b": 12, "cqr_b": 16, "ckr_b": 17,
            "cv_b": 20}
    for k_, col in bmap.items():
        nch = B[k_].shape[0] // 128
        p.op("sync", lambda e, k_=k_, col=col, nch=nch: e.dma_start(
            out=bias[:, col:col + nch],
            in_=B[k_].rearrange("(c p) -> p c", p=128),
            allow_slow_non_contiguous=True), inc=ld, inck=16)
    p.op("sync", lambda e: e.dma_start(out=cos_sb[:], in_=cos_hbm), inc=ld, inck=16)
    p.op("sync", lambda e: e.dma_start(out=sin_sb[:], in_=sin_hbm), inc=ld, inck=16)
    cv_ld = (ld, ld.n)

    xn_free = [None, None]
    pp_free = [None, None]
    pt_free = [None, None]
    cq_free = None; ckv_free = None
    r_free = None; rt_free = None; t1_free = None
    qst_free = [None, None]; kst_free = [None, None]; vst_free = [None, None]
    vfm_free = None
    ppi = [0]

    def proj(stat_sb, c_cnt, osz, oc, src_sb, src_cnt_stride, nk, waits_extra):
        bi = ppi[0] % 2
        ppi[0] += 1
        first_w = [wv, cv_ld] + waits_extra
        if pp_free[bi] is not None:
            first_w.append(pp_free[bi])
        for k_ in range(nk):
            w = first_w if k_ == 0 else []
            vv = p.op("tensor", lambda e, k_=k_, bi=bi, stat_sb=stat_sb, osz=osz, oc=oc,
                      src_sb=src_sb, s=src_cnt_stride: e.matmul(
                pp[bi][:], stat_sb[:, k_ * osz + oc * 128: k_ * osz + oc * 128 + 128],
                src_sb[:, k_ * s:(k_ + 1) * s] if s else src_sb[:],
                start=(k_ == 0), stop=(k_ == nk - 1)), waits=w, inc=mm)
        return bi, (mm, mm.n)

    all_tiles = sorted(set(q_tiles) | set(kv_tiles))
    for ti, tt in enumerate(all_tiles):
        tb = ti % 2
        do_q = tt in q_tiles
        w = [xn_free[tb]] if xn_free[tb] else []
        xv = p.op("sync", lambda e, tt=tt, tb=tb: e.dma_start(
            out=xn_sb[tb].ap().rearrange("p (c t) -> p c t", t=TT),
            in_=xn_hbm.rearrange("(c p) t -> p c t", p=128)[:, :, tt * TT:(tt + 1) * TT]),
            waits=w, inc=ld, inck=16)
        xnv = (ld, xv)
        xn_mm_last = None

        def rope_gelu(srcT, nk, bcol, stage, stage_free_tv, kind):
            nonlocal r_free, rt_free, t1_free
            bi, pv = proj(srcT, NDC, 128, 0, xn_sb[tb], TT, NDC, [xnv])
            w = [pv]
            if r_free:
                w.append(r_free)
            rv = p.op("vector", lambda e, bi=bi, bcol=bcol: e.tensor_scalar_add(
                r_f32[:], pp[bi][:], bias[:, bcol:bcol + 1]), waits=w, inc=va)
            pp_free[bi] = (va, rv)
            pos = pos_off + tt * TT
            w = [(va, rv), cv_ld]
            if t1_free:
                w.append(t1_free)
            if rt_free:
                w.append(rt_free)
            c1 = p.op("vector", lambda e: e.tensor_copy(
                rs_f32[0:64, :], r_f32[64:128, :]), waits=w, inc=va)
            c2 = p.op("vector", lambda e: e.tensor_copy(
                rs_f32[64:128, :], r_f32[0:64, :]), waits=[(va, c1)], inc=va)
            m1 = p.op("vector", lambda e, pos=pos: e.tensor_mul(
                rt_f32[:], r_f32[:], cos_sb[:, pos:pos + TT]), waits=[(va, c2)], inc=va)
            m2 = p.op("vector", lambda e, pos=pos: e.tensor_mul(
                t1_f32[:], rs_f32[:], sin_sb[:, pos:pos + TT]), waits=[(va, m1)], inc=va)
            b3 = p.op("vector", lambda e: e.tensor_add(
                rt_f32[:], rt_f32[:], t1_f32[:]), waits=[(va, m2)], inc=va)
            r_free = (va, b3)
            t1_free = (va, b3)
            w = [(va, b3)]
            if stage_free_tv:
                w.append(stage_free_tv)
            gv = p.op("scalar", lambda e, stage=stage: e.activation(
                stage[:, 3 * TT:4 * TT], rt_f32[:], AF.Gelu), waits=w, inc=sa)
            rt_free = (sa, gv)
            return (sa, gv)

        if do_q:
            cq_done = []
            for oc in range(4):
                we = [xnv]
                if oc == 0 and cq_free:
                    we.append(cq_free)
                bi, pv = proj(cqT, NDC, 512, oc, xn_sb[tb], TT, NDC, we)
                gv = p.op("scalar", lambda e, bi=bi, oc=oc: e.activation(
                    cq_sb[:, oc * TT:(oc + 1) * TT], pp[bi][:], AF.Gelu,
                    bias=bias[:, oc:oc + 1]), waits=[pv], inc=sa)
                pp_free[bi] = (sa, gv)
                cq_done.append((sa, gv))
            for oc in range(3):
                we = [cq_done[-1]]
                if oc == 0 and qst_free[tb]:
                    we.append(qst_free[tb])
                bi, pv = proj(cqcT, 4, 384, oc, cq_sb, TT, 4, we)
                gv = p.op("scalar", lambda e, bi=bi, oc=oc, tb=tb: e.activation(
                    q_st[tb][:, oc * TT:(oc + 1) * TT], pp[bi][:], AF.Gelu,
                    bias=bias[:, 4 + oc:5 + oc]), waits=[pv], inc=sa)
                pp_free[bi] = (sa, gv)
            cq_free = (mm, mm.n)
            rope_gelu(cqrT, NDC, 16, q_st[tb], None, "q")
            qv = p.op("scalar", lambda e, tb=tb, tt=tt: e.dma_start(
                out=QT_hbm.rearrange("(c p) t -> p c t", p=128)[:, :, (tt - q_tiles[0]) * TT:(tt - q_tiles[0] + 1) * TT],
                in_=q_st[tb].ap().rearrange("p (c t) -> p c t", t=TT)),
                waits=[(sa, sa.n)], inc=so, inck=16)
            qst_free[tb] = (so, qv)
        if tt in kv_tiles:
            ckv_done = []
            for oc in range(4):
                we = [xnv]
                if oc == 0 and ckv_free:
                    we.append(ckv_free)
                bi, pv = proj(ckvT, NDC, 512, oc, xn_sb[tb], TT, NDC, we)
                gv = p.op("scalar", lambda e, bi=bi, oc=oc: e.activation(
                    ckv_sb[:, oc * TT:(oc + 1) * TT], pp[bi][:], AF.Gelu,
                    bias=bias[:, 8 + oc:9 + oc]), waits=[pv], inc=sa)
                pp_free[bi] = (sa, gv)
                ckv_done.append((sa, gv))
            for oc in range(3):
                we = [ckv_done[-1]]
                if oc == 0 and kst_free[tb]:
                    we.append(kst_free[tb])
                bi, pv = proj(ckcT, 4, 384, oc, ckv_sb, TT, 4, we)
                gv = p.op("scalar", lambda e, bi=bi, oc=oc, tb=tb: e.activation(
                    k_st[tb][:, oc * TT:(oc + 1) * TT], pp[bi][:], AF.Gelu,
                    bias=bias[:, 12 + oc:13 + oc]), waits=[pv], inc=sa)
                pp_free[bi] = (sa, gv)
            rope_gelu(ckrT, NDC, 17, k_st[tb], None, "k")
            kv = p.op("scalar", lambda e, tb=tb, tt=tt: e.dma_start(
                out=KT_hbm.rearrange("(c p) t -> p c t", p=128)[:, :, (tt - kv_tiles[0]) * TT:(tt - kv_tiles[0] + 1) * TT],
                in_=k_st[tb].ap().rearrange("p (c t) -> p c t", t=TT)),
                waits=[(sa, sa.n)], inc=so, inck=16)
            kst_free[tb] = (so, kv)
            for oc in range(4):
                we = [ckv_done[-1]]
                bi, pv = proj(cvT, 4, 512, oc, ckv_sb, TT, 4, we)
                w = [pv]
                if vfm_free:
                    w.append(vfm_free)
                ev = p.op("scalar", lambda e, bi=bi, oc=oc: e.activation(
                    v_fm[:], pp[bi][:], AF.Identity, bias=bias[:, 20 + oc:21 + oc]),
                    waits=w, inc=sa)
                pp_free[bi] = (sa, ev)
                for tc in range(4):
                    pb = (oc * 4 + tc) % 2
                    w = [(sa, ev), wv]
                    if pt_free[pb]:
                        w.append(pt_free[pb])
                    tv = p.op("tensor", lambda e, tc=tc, pb=pb: e.transpose(
                        pt[pb][:, 0:128], v_fm[:, tc * 128:(tc + 1) * 128], ident_b[:]),
                        waits=w, inc=mm)
                    w = [(mm, tv)]
                    if tc == 0 and oc == 0 and vst_free[tb]:
                        w.append(vst_free[tb])
                    cvv = p.op("vector", lambda e, tc=tc, oc=oc, pb=pb, tb=tb: e.tensor_copy(
                        v_st[tb][:, tc * 512 + oc * 128: tc * 512 + oc * 128 + 128],
                        pt[pb][:, 0:128]), waits=w, inc=va)
                    pt_free[pb] = (va, cvv)
                vfm_free = (mm, mm.n)
            ckv_free = (mm, mm.n)
            vv = p.op("scalar", lambda e, tb=tb, tt=tt: e.dma_start(
                out=VT_hbm.rearrange("(tc p) o -> p tc o", p=128)[
                    :, (tt - kv_tiles[0]) * 4:(tt - kv_tiles[0] + 1) * 4, :],
                in_=v_st[tb].ap().rearrange("p (tc o) -> p tc o", o=512)),
                waits=[(va, va.n)], inc=so, inck=16)
            vst_free[tb] = (so, vv)
        xn_free[tb] = (mm, mm.n)
    return so



def plan_l0attn(p, QT_hbm, KT_hbm, VT_hbm, g_hbm, x_hbm, W, x1_hbm, Tk, name="",
                pmask_hbm=None):
    n = name
    TT = 512
    NKC = Tk // 128
    q_off = Tk - 2048
    K_sb = p.sb(f"{n}K", [128, 4 * Tk], BF16)
    V_sb = p.sb(f"{n}V", [128, NKC * 512], BF16)
    Q_sb = [p.sb(f"{n}Q{i}", [128, 4 * TT], BF16) for i in range(2)]
    outT = p.sb(f"{n}oT", [128, 4 * 2048], BF16)
    ob = p.sb(f"{n}ob", [128, 16])
    mask = p.sb(f"{n}mask", [128, 128])
    pm_sb = p.sb(f"{n}pm", [1, Tk]) if pmask_hbm is not None else None
    ones_q = p.sb(f"{n}onq", [1, 128]) if pmask_hbm is not None else None
    ident_b = p.sb(f"{n}idb", [128, 128], BF16)
    ident_f = p.sb(f"{n}idf", [128, 128])
    pr = p.sb(f"{n}pr", [128, 4 * Tk], BF16)
    prT = [p.sb(f"{n}prT{i}", [128, TT], BF16) for i in range(2)]
    rm = p.sb(f"{n}rm", [128, 4])
    m2 = p.sb(f"{n}m2", [128, 1])
    nM8 = p.sb(f"{n}nM8", [128, 4])
    lsum = p.sb(f"{n}lsum", [128, 4])
    lpart = p.sb(f"{n}lpart", [128, 1])
    rlam = p.sb(f"{n}rlam", [128, 4])
    O_fm = p.sb(f"{n}Ofm", [128, 4 * TT], BF16)
    sg = p.sb(f"{n}sg", [128, 16 * TT])
    g_sb = [p.sb(f"{n}g{i}", [128, TT]) for i in range(2)]
    x1_st = [p.sb(f"{n}x1{i}", [128, 2048]) for i in range(2)]
    S_ps = p.ps(f"{n}Sps", [128, 2048])
    pt_ps = p.ps(f"{n}ptps", [128, 1024], BF16)
    pq = [p.ps(f"{n}pq{i}", [128, 512]) for i in range(2)]
    ld = p.sem(f"{n}ld")
    wld = p.sem(f"{n}wld")
    mm = p.sem(f"{n}mm")
    va = p.sem(f"{n}va")
    sa = p.sem(f"{n}sa")
    so = p.sem(f"{n}so")

    p.op("gpsimd", lambda e: e.dma_start(
        out=K_sb.ap().rearrange("p (c t) -> p c t", t=Tk),
        in_=KT_hbm.rearrange("(c p) t -> p c t", p=128)), inc=wld, inck=16)
    p.op("gpsimd", lambda e: e.dma_start(
        out=V_sb.ap().rearrange("p (c o) -> p c o", o=512),
        in_=VT_hbm.rearrange("(c p) o -> p c o", p=128)), inc=wld, inck=16)
    p.op("gpsimd", lambda e: e.dma_start(
        out=outT.ap().rearrange("p (c o) -> p c o", o=2048),
        in_=W["outT"].rearrange("(c p) o -> p c o", p=128)), inc=wld, inck=16)
    p.op("gpsimd", lambda e: e.dma_start(out=mask[:], in_=W["mask"]), inc=wld, inck=16)
    p.op("gpsimd", lambda e: e.dma_start(out=ident_b[:], in_=W["ident"]), inc=wld, inck=16)
    p.op("gpsimd", lambda e: e.dma_start(out=ident_f[:], in_=W["ident"]), inc=wld, inck=16)
    p.op("sync", lambda e: e.dma_start(
        out=ob[:], in_=W["out_b"].rearrange("(c p) -> p c", p=128),
        allow_slow_non_contiguous=True), inc=ld, inck=16)
    if pmask_hbm is not None:
        p.op("sync", lambda e: e.dma_start(out=pm_sb[:], in_=pmask_hbm), inc=ld, inck=16)
        p.op("vector", lambda e: e.memset(ones_q[:], 1.0), inc=va)
    consts = [(wld, wld.n), (ld, ld.n), (va, va.n)]

    Q_free = [None, None]
    pr_free = None
    Sps_free = None
    Ops_free = None
    pt_free = None
    prT_free = [None, None]
    Ofm_free = None
    pq_free = [None, None]
    sg_free = None
    g_free = [None, None]
    x1_free = [None, None]
    stat_free = None
    lpart_free = None

    for qt in range(4):
        qb = qt % 2
        w = [consts[0], consts[1]]
        if Q_free[qb]:
            w.append(Q_free[qb])
        qv = p.op("sync", lambda e, qt=qt, qb=qb: e.dma_start(
            out=Q_sb[qb].ap().rearrange("p (c t) -> p c t", t=TT),
            in_=QT_hbm.rearrange("(c p) t -> p c t", p=128)[:, :, qt * TT:(qt + 1) * TT]),
            waits=w, inc=ld, inck=16)
        qld = (ld, qv)
        pair_done = []
        for hd in range(8):
            half, tile = hd % 2, hd // 2
            exp_done = []
            for qc in range(4):
                gq = q_off // 128 + qt * 4 + qc
                valid = (gq + 1) * 128
                ngrp = (valid + 2047) // 2048
                rm_col = rm[:, qc:qc + 1]
                for kg in range(ngrp):
                    vl = min(valid - kg * 2048, 2048)
                    nkt = (vl + 511) // 512
                    for kt in range(nkt):
                        w = [qld]
                        if kt == 0:
                            if Sps_free:
                                w.append(Sps_free)
                            if kg == 0 and qc == 0 and stat_free:
                                w.append(stat_free)
                        kbase = kg * 2048 + kt * 512
                        aug = pmask_hbm is not None and kbase < 2048
                        mv = p.op("tensor", lambda e, kg=kg, kt=kt, qc=qc, half=half,
                                  tile=tile, qb=qb, aug=aug: e.matmul(
                            S_ps[:, kt * 512:(kt + 1) * 512],
                            Q_sb[qb][half * 64:(half + 1) * 64,
                                     tile * TT + qc * 128: tile * TT + qc * 128 + 128],
                            K_sb[half * 64:(half + 1) * 64,
                                 tile * Tk + kg * 2048 + kt * 512:
                                 tile * Tk + kg * 2048 + kt * 512 + 512],
                            start=True, stop=not aug), waits=w, inc=mm)
                        if aug:
                            mv = p.op("tensor", lambda e, kt=kt, kbase=kbase: e.matmul(
                                S_ps[:, kt * 512:(kt + 1) * 512], ones_q[:],
                                pm_sb[0:1, kbase:kbase + 512],
                                start=False, stop=True), waits=[], inc=mm)
                    smm = (mm, mm.n)
                    diag_off = valid - 128 - kg * 2048
                    if 0 <= diag_off < 2048:
                        dmv = p.op("vector", lambda e, diag_off=diag_off: e.tensor_add(
                            S_ps[:, diag_off:diag_off + 128],
                            S_ps[:, diag_off:diag_off + 128], mask[:]),
                            waits=[smm, consts[0]], inc=va)
                        smm = (va, dmv)
                    if kg == 0:
                        xv_ = p.op("vector", lambda e, vl=vl, rm_col=rm_col: e.reduce_max(
                            rm_col, S_ps[:, 0:vl], axis=AX.X), waits=[smm], inc=va)
                    else:
                        xv_ = p.op("vector", lambda e, vl=vl: e.reduce_max(
                            m2[:], S_ps[:, 0:vl], axis=AX.X), waits=[smm], inc=va)
                        xv_ = p.op("vector", lambda e, rm_col=rm_col: e.tensor_max(
                            rm_col, rm_col, m2[:]), waits=[(va, xv_)], inc=va)
                    Sps_free = (va, xv_)
                nv = p.op("vector", lambda e, qc=qc: e.tensor_scalar_mul(
                    nM8[:, qc:qc + 1], rm[:, qc:qc + 1], -0.125),
                    waits=[(va, va.n)], inc=va)
                for kg in range(ngrp):
                    vl = min(valid - kg * 2048, 2048)
                    nkt = (vl + 511) // 512
                    for kt in range(nkt):
                        w = [qld]
                        if kt == 0 and Sps_free:
                            w.append(Sps_free)
                        kbase = kg * 2048 + kt * 512
                        aug = pmask_hbm is not None and kbase < 2048
                        mv = p.op("tensor", lambda e, kg=kg, kt=kt, qc=qc, half=half,
                                  tile=tile, qb=qb, aug=aug: e.matmul(
                            S_ps[:, kt * 512:(kt + 1) * 512],
                            Q_sb[qb][half * 64:(half + 1) * 64,
                                     tile * TT + qc * 128: tile * TT + qc * 128 + 128],
                            K_sb[half * 64:(half + 1) * 64,
                                 tile * Tk + kg * 2048 + kt * 512:
                                 tile * Tk + kg * 2048 + kt * 512 + 512],
                            start=True, stop=not aug), waits=w, inc=mm)
                        if aug:
                            mv = p.op("tensor", lambda e, kt=kt, kbase=kbase: e.matmul(
                                S_ps[:, kt * 512:(kt + 1) * 512], ones_q[:],
                                pm_sb[0:1, kbase:kbase + 512],
                                start=False, stop=True), waits=[], inc=mm)
                    smm = (mm, mm.n)
                    diag_off = valid - 128 - kg * 2048
                    if 0 <= diag_off < 2048:
                        dmv = p.op("vector", lambda e, diag_off=diag_off: e.tensor_add(
                            S_ps[:, diag_off:diag_off + 128],
                            S_ps[:, diag_off:diag_off + 128], mask[:]),
                            waits=[smm, consts[0]], inc=va)
                        smm = (va, dmv)
                    w = [smm, (va, nv)]
                    if kg == 0 and pr_free:
                        w.append(pr_free)
                    if lpart_free:
                        w.append(lpart_free)
                    ev = p.op("scalar", lambda e, vl=vl, kg=kg, qc=qc: e.activation(
                        pr[:, qc * Tk + kg * 2048: qc * Tk + kg * 2048 + vl],
                        S_ps[:, 0:vl], AF.Exp, scale=0.125, bias=nM8[:, qc:qc + 1],
                        accum_out=lpart[:]), waits=w, inc=sa)
                    Sps_free = (sa, ev)
                    if kg == 0:
                        lv = p.op("vector", lambda e, qc=qc: e.tensor_copy(
                            lsum[:, qc:qc + 1], lpart[:]), waits=[(sa, ev)], inc=va)
                    else:
                        lv = p.op("vector", lambda e, qc=qc: e.tensor_add(
                            lsum[:, qc:qc + 1], lsum[:, qc:qc + 1], lpart[:]),
                            waits=[(sa, ev)], inc=va)
                    lpart_free = (va, lv)
                rv = p.op("vector", lambda e, qc=qc: e.reciprocal(
                    rlam[:, qc:qc + 1], lsum[:, qc:qc + 1]), waits=[(va, va.n)], inc=va)
                pv = p.op("vector", lambda e, qc=qc, valid=valid: e.tensor_scalar_mul(
                    pr[:, qc * Tk: qc * Tk + valid], pr[:, qc * Tk: qc * Tk + valid],
                    rlam[:, qc:qc + 1]), waits=[(va, rv)], inc=va)
                exp_done.append((va, pv))
            stat_free = (va, va.n)
            probs_v = exp_done[-1]
            gq_last = q_off // 128 + qt * 4 + 3
            nkc = gq_last + 1
            first_pv = None
            for kc in range(nkc):
                ti = kc % 2
                w_tr = [probs_v, consts[0]]
                if pt_free:
                    w_tr.append(pt_free)
                if prT_free[ti]:
                    w_tr.append(prT_free[ti])
                cpv = None
                for qc in range(4):
                    gq = q_off // 128 + qt * 4 + qc
                    if kc > gq:
                        continue
                    tv = p.op("tensor", lambda e, kc=kc, qc=qc: e.transpose(
                        pt_ps[:, 0:128], pr[:, qc * Tk + kc * 128: qc * Tk + kc * 128 + 128],
                        ident_b[:]), waits=w_tr if qc == 0 or cpv is None else [(va, cpv)],
                        inc=mm)
                    cpv = p.op("vector", lambda e, qc=qc, ti=ti: e.tensor_copy(
                        prT[ti][:, qc * 128:(qc + 1) * 128], pt_ps[:, 0:128]),
                        waits=[(mm, tv)], inc=va)
                    w_tr = [(va, cpv)]
                for qc in range(4):
                    gq = q_off // 128 + qt * 4 + qc
                    if kc <= gq:
                        continue
                    cpv = p.op("vector", lambda e, qc=qc, ti=ti: e.memset(
                        prT[ti][:, qc * 128:(qc + 1) * 128], 0.0),
                        waits=[(va, cpv)] if cpv else w_tr, inc=va)
                pt_free = (va, cpv)
                trv = (va, cpv)
                w = [trv, consts[0]]
                if kc == 0 and pq_free[half]:
                    w.append(pq_free[half])
                pvv = p.op("tensor", lambda e, kc=kc, hd=hd, half=half, nkc=nkc: e.matmul(
                    pq[half][half * 64:(half + 1) * 64, :],
                    V_sb[:, kc * 512 + hd * 64: kc * 512 + hd * 64 + 64],
                    prT[kc % 2][:], start=(kc == 0), stop=(kc == nkc - 1)),
                    waits=w, inc=mm)
                prT_free[ti] = (mm, pvv)
            pr_free = (mm, mm.n)
            ov = p.op("vector", lambda e, tile=tile, half=half: e.tensor_copy(
                O_fm[half * 64:(half + 1) * 64, tile * TT:(tile + 1) * TT],
                pq[half][half * 64:(half + 1) * 64, :]), waits=[(mm, mm.n)], inc=va)
            pq_free[half] = (va, ov)
            if half == 1:
                pair_done.append((va, ov))
        ow = pair_done[-1]
        for oc in range(16):
            bi = oc % 2
            w = [g_free[bi]] if g_free[bi] else []
            gv = p.op("sync", lambda e, oc=oc, qt=qt, bi=bi: e.dma_start(
                out=g_sb[bi][:], in_=g_hbm[oc * 128:(oc + 1) * 128, qt * TT:(qt + 1) * TT]),
                waits=w, inc=ld, inck=16)
            w = [ow, consts[0]]
            if pq_free[bi]:
                w.append(pq_free[bi])
            for c in range(4):
                vv = p.op("tensor", lambda e, c=c, oc=oc, bi=bi: e.matmul(
                    pq[bi][:], outT[:, c * 2048 + oc * 128: c * 2048 + oc * 128 + 128],
                    O_fm[:, c * TT:(c + 1) * TT],
                    start=(c == 0), stop=(c == 3)), waits=w if c == 0 else [], inc=mm)
            omm = (mm, mm.n)
            w = [omm]
            if oc < 2 and sg_free:
                w.append(sg_free)
            sv = p.op("scalar", lambda e, oc=oc, bi=bi: e.activation(
                sg[:, oc * TT:(oc + 1) * TT], pq[bi][:], AF.Identity,
                bias=ob[:, oc:oc + 1]), waits=w, inc=sa)
            pq_free[bi] = (sa, sv)
            mv = p.op("vector", lambda e, oc=oc, bi=bi: e.tensor_mul(
                sg[:, oc * TT:(oc + 1) * TT], sg[:, oc * TT:(oc + 1) * TT], g_sb[bi][:]),
                waits=[(sa, sv), (ld, gv)], inc=va)
            g_free[bi] = (va, mv)
        sg_done = (va, va.n)
        for qc in range(4):
            xb = qc % 2
            w = [x1_free[xb]] if x1_free[xb] else []
            xv_ = p.op("sync", lambda e, qt=qt, qc=qc, xb=xb: e.dma_start(
                out=x1_st[xb][:], in_=x_hbm[qt * TT + qc * 128: qt * TT + qc * 128 + 128, :]),
                waits=w, inc=ld, inck=16)
            last_cp = None
            for oc in range(16):
                bi = oc % 2
                w = [sg_done, consts[0]]
                if pq_free[bi]:
                    w.append(pq_free[bi])
                tv = p.op("tensor", lambda e, qc=qc, oc=oc, bi=bi: e.transpose(
                    pq[bi][:, 0:128], sg[:, oc * TT + qc * 128: oc * TT + qc * 128 + 128],
                    ident_f[:]), waits=w, inc=mm)
                w = [(mm, tv), (ld, xv_)]
                cv = p.op("vector", lambda e, oc=oc, bi=bi, xb=xb: e.tensor_add(
                    x1_st[xb][:, oc * 128:(oc + 1) * 128],
                    x1_st[xb][:, oc * 128:(oc + 1) * 128], pq[bi][:, 0:128]),
                    waits=w, inc=va)
                pq_free[bi] = (va, cv)
                last_cp = cv
            stv = p.op("scalar", lambda e, qt=qt, qc=qc, xb=xb: e.dma_start(
                out=x1_hbm[qt * TT + qc * 128: qt * TT + qc * 128 + 128, :],
                in_=x1_st[xb][:]), waits=[(va, last_cp)], inc=so, inck=16)
            x1_free[xb] = [(so, stv)]
        sg_free = (va, va.n)
        Q_free[qb] = (mm, mm.n)
    return so


D = 2048
NDC = 16
NOC = 16


def plan_l1ffn(p, xn_hbm, w1T_hbm, gateT, upT, downT, loraAT, loraBT, g1_hbm,
               T=2048, TT=512, name="", apply_silu=True, nexp=8):
    n = name
    NTT = T // TT
    E = 8
    xn_sb = p.sb(f"{n}xn", [128, NDC * TT], BF16)
    dw_row = p.sb(f"{n}dwr", [1, TT])
    ones = p.sb(f"{n}ones", [1, 128])
    wbc = p.sb(f"{n}wbc", [128, TT])
    gt_b = [p.sb(f"{n}gt{i}", [128, NDC * 256], BF16) for i in range(2)]
    up_b = [p.sb(f"{n}up{i}", [128, NDC * 256], BF16) for i in range(2)]
    dn_b = [p.sb(f"{n}dn{i}", [128, 2 * 2048], BF16) for i in range(2)]
    la_b = [p.sb(f"{n}la{i}", [128, NDC * 16], BF16) for i in range(2)]
    lb_b = [p.sb(f"{n}lb{i}", [16, 2048], BF16) for i in range(2)]
    ga_sb = p.sb(f"{n}ga", [128, 2 * TT], BF16)
    tmp_sb = p.sb(f"{n}tmp", [128, TT], BF16)
    ge_sb = p.sb(f"{n}ge", [128, 2 * TT], BF16)
    lrw = p.sb(f"{n}lrw", [16, TT], BF16)
    y_sb = p.sb(f"{n}y", [128, NOC * TT])
    g_st = [p.sb(f"{n}g{i}", [128, TT]) for i in range(2)]
    a_ps = p.ps(f"{n}aps", [128, TT])
    b_ps = p.ps(f"{n}bps", [128, TT])
    l_ps = p.ps(f"{n}lps", [128, TT])
    y_ps = [p.ps(f"{n}yps{i}", [128, TT]) for i in range(2)]
    bc_ps = p.ps(f"{n}bcps", [128, TT])
    ld = p.sem(f"{n}ld")
    wg = p.sem(f"{n}wg")
    mm = p.sem(f"{n}mm")
    va = p.sem(f"{n}va")
    sa = p.sem(f"{n}sa")
    so = p.sem(f"{n}so")

    p.op("vector", lambda e: e.memset(ones[:], 1.0), inc=va)
    ones_v = (va, va.n)

    xn_free = None
    wfree = {k: [None, None] for k in ("gt", "up", "dn", "la", "lb")}
    aps_free = None; bps_free = None; lps_free = None; bc_free = None
    yps_free = [None, None]
    y_last = [None] * NOC
    g_free = [None, None]
    ga_free = [None, None]; tmp_free = None; ge_free = None; lrw_free = None
    dwrow_free = None
    wqi = 0

    for tt in range(NTT):
        w = [xn_free] if xn_free else []
        xv = p.op("sync", lambda e, tt=tt: e.dma_start(
            out=xn_sb.ap().rearrange("p (c t) -> p c t", t=TT),
            in_=xn_hbm.rearrange("(c p) t -> p c t", p=128)[:, :, tt * TT:(tt + 1) * TT]),
            waits=w, inc=ld, inck=16)
        xnv = (ld, xv)
        for ei in range(nexp):
            bi = wqi % 2
            wqi += 1
            lds = {}
            for key, dst, src, osz, nch in (
                    ("gt", gt_b[bi], gateT[ei], 256, NDC),
                    ("up", up_b[bi], upT[ei], 256, NDC),
                    ("dn", dn_b[bi], downT[ei], 2048, 2),
                    ("la", la_b[bi], loraAT[ei], 16, NDC)):
                w = [wfree[key][bi]] if wfree[key][bi] else []
                lv = p.op("gpsimd", lambda e, dst=dst, src=src, osz=osz: e.dma_start(
                    out=dst.ap().rearrange("p (c o) -> p c o", o=osz),
                    in_=src.rearrange("(c p) o -> p c o", p=128)), waits=w, inc=wg, inck=16)
                lds[key] = lv
            w = [wfree["lb"][bi]] if wfree["lb"][bi] else []
            lv = p.op("gpsimd", lambda e, dst=lb_b[bi], src=loraBT[ei]: e.dma_start(
                out=dst[:], in_=src), waits=w, inc=wg, inck=16)
            lds["lb"] = lv
            w = [dwrow_free] if dwrow_free else []
            dv_ = p.op("sync", lambda e, ei=ei, tt=tt: e.dma_start(
                out=dw_row[:], in_=w1T_hbm[ei:ei + 1, tt * TT:(tt + 1) * TT]),
                waits=w, inc=ld, inck=16)
            w = [ones_v, (ld, dv_)]
            if bc_free:
                w.append(bc_free)
            bv = p.op("tensor", lambda e: e.matmul(
                bc_ps[:], ones[:], dw_row[:], start=True, stop=True), waits=w, inc=mm)
            dwrow_free = (mm, bv)
            w = [(mm, bv)]
            bcv = p.op("vector", lambda e: e.tensor_copy(wbc[:], bc_ps[:]), waits=w, inc=va)
            bc_free = (va, bcv)
            for fc in range(2):
                w = [(wg, lds["gt"]), xnv]
                if aps_free:
                    w.append(aps_free)
                for dc in range(NDC):
                    av_ = p.op("tensor", lambda e, dc=dc, fc=fc, bi=bi: e.matmul(
                        a_ps[:], gt_b[bi][:, dc * 256 + fc * 128: dc * 256 + fc * 128 + 128],
                        xn_sb[:, dc * TT:(dc + 1) * TT],
                        start=(dc == 0), stop=(dc == NDC - 1)),
                        waits=w if dc == 0 else [], inc=mm)
                amm = (mm, mm.n)
                w = [(wg, lds["up"]), xnv]
                if bps_free:
                    w.append(bps_free)
                for dc in range(NDC):
                    bv_ = p.op("tensor", lambda e, dc=dc, fc=fc, bi=bi: e.matmul(
                        b_ps[:], up_b[bi][:, dc * 256 + fc * 128: dc * 256 + fc * 128 + 128],
                        xn_sb[:, dc * TT:(dc + 1) * TT],
                        start=(dc == 0), stop=(dc == NDC - 1)),
                        waits=w if dc == 0 else [], inc=mm)
                bmm = (mm, mm.n)
                w = [amm]
                if ga_free[fc]:
                    w.append(ga_free[fc])
                sv = p.op("scalar", lambda e, fc=fc: e.activation(
                    ga_sb[:, fc * TT:(fc + 1) * TT], a_ps[:], AF.Silu), waits=w, inc=sa)
                aps_free = (sa, sv)
                w = [bmm, (va, bcv)]
                if tmp_free:
                    w.append(tmp_free)
                tv = p.op("vector", lambda e: e.tensor_mul(tmp_sb[:], b_ps[:], wbc[:]),
                          waits=w, inc=va)
                bps_free = (va, tv)
                w = [(va, tv), (sa, sv)]
                if ge_free and fc == 0:
                    w.append(ge_free)
                gv = p.op("vector", lambda e, fc=fc: e.tensor_mul(
                    ge_sb[:, fc * TT:(fc + 1) * TT], ga_sb[:, fc * TT:(fc + 1) * TT],
                    tmp_sb[:]), waits=w, inc=va)
                tmp_free = (va, gv)
                ga_free[fc] = (va, gv)
            ge_done = (va, va.n)
            w = [(wg, lds["la"]), xnv]
            if lps_free:
                w.append(lps_free)
            for dc in range(NDC):
                lv_ = p.op("tensor", lambda e, dc=dc, bi=bi: e.matmul(
                    l_ps[0:16, :], la_b[bi][:, dc * 16:(dc + 1) * 16],
                    xn_sb[:, dc * TT:(dc + 1) * TT],
                    start=(dc == 0), stop=(dc == NDC - 1)),
                    waits=w if dc == 0 else [], inc=mm)
            lmm = (mm, mm.n)
            w = [lmm, (va, bcv)]
            if lrw_free:
                w.append(lrw_free)
            lwv = p.op("vector", lambda e: e.tensor_mul(
                lrw[:], l_ps[0:16, :], wbc[0:16, :]), waits=w, inc=va)
            lps_free = (va, lwv)
            for oc in range(NOC):
                yb = oc % 2
                w = [ge_done, (va, lwv), (wg, lds["dn"]), (wg, lds["lb"])]
                if yps_free[yb]:
                    w.append(yps_free[yb])
                for fc in range(2):
                    yv = p.op("tensor", lambda e, fc=fc, oc=oc, bi=bi, yb=yb: e.matmul(
                        y_ps[yb][:], dn_b[bi][:, fc * 2048 + oc * 128: fc * 2048 + oc * 128 + 128],
                        ge_sb[:, fc * TT:(fc + 1) * TT],
                        start=(fc == 0), stop=False), waits=w if fc == 0 else [], inc=mm)
                yv = p.op("tensor", lambda e, oc=oc, bi=bi, yb=yb: e.matmul(
                    y_ps[yb][:], lb_b[bi][0:16, oc * 128:(oc + 1) * 128],
                    lrw[:], start=False, stop=True), waits=[], inc=mm)
                ymm = (mm, yv)
                ge_free = ymm
                lrw_free = ymm
                w = [ymm]
                if y_last[oc]:
                    w.append(y_last[oc])
                if ei == 0:
                    cv = p.op("vector", lambda e, oc=oc, yb=yb: e.tensor_copy(
                        y_sb[:, oc * TT:(oc + 1) * TT], y_ps[yb][:]), waits=w, inc=va)
                else:
                    cv = p.op("vector", lambda e, oc=oc, yb=yb: e.tensor_add(
                        y_sb[:, oc * TT:(oc + 1) * TT],
                        y_sb[:, oc * TT:(oc + 1) * TT], y_ps[yb][:]), waits=w, inc=va)
                y_last[oc] = (va, cv)
                yps_free[yb] = (va, cv)
            for k in ("gt", "up", "dn", "la"):
                wfree[k][bi] = (mm, mm.n)
            wfree["lb"][bi] = (mm, mm.n)
        for oc in range(NOC):
            gb = oc % 2
            w = [y_last[oc]]
            if g_free[gb]:
                w.append(g_free[gb])
            if apply_silu:
                gv = p.op("scalar", lambda e, oc=oc, gb=gb: e.activation(
                    g_st[gb][:], y_sb[:, oc * TT:(oc + 1) * TT], AF.Silu), waits=w, inc=sa)
            else:
                gv = p.op("scalar", lambda e, oc=oc, gb=gb: e.copy(
                    g_st[gb][:], y_sb[:, oc * TT:(oc + 1) * TT]), waits=w, inc=sa)
            sv = p.op("scalar", lambda e, oc=oc, gb=gb, tt=tt: e.dma_start(
                out=g1_hbm[oc * 128:(oc + 1) * 128, tt * TT:(tt + 1) * TT],
                in_=g_st[gb][:]), waits=[(sa, gv)], inc=so, inck=16)
            g_free[gb] = (so, sv)
            y_last[oc] = [(so, sv), (sa, gv)]
        xn_free = (mm, mm.n)
    return so


NDC = 16
S05 = 0.22360679774997896


def plan_l1attn(p, xn_hbm, Wv, Wg, Wq, Wo, Bg, Bq, dpow_hbm, g1_hbm, x1_hbm,
                mem_hbm, cc_in, cc_out, x2_hbm, ident_hbm, name=""):
    n = name
    TT = 512
    T = 2048
    xn_sb = [p.sb(f"{n}xn{i}", [128, NDC * TT], BF16) for i in range(2)]
    wv_b = [p.sb(f"{n}wv{i}", [128, NDC * 128], BF16) for i in range(2)]
    wg_b = [p.sb(f"{n}wg{i}", [128, NDC * 128], BF16) for i in range(2)]
    wq_b = [p.sb(f"{n}wq{i}", [128, NDC * 128], BF16) for i in range(2)]
    wo_b = [p.sb(f"{n}wo{i}", [128, NDC * 128], BF16) for i in range(2)]
    bgq = p.sb(f"{n}bgq", [128, 32])
    dec = p.sb(f"{n}dec", [128, 1])
    dpow = p.sb(f"{n}dpow", [128, 2048])
    ident_f = p.sb(f"{n}idf", [128, 128])
    sig = p.sb(f"{n}sig", [128, TT])
    store_c = p.sb(f"{n}stc", [128, TT])
    mem_c = p.sb(f"{n}memc", [128, TT])
    sstate = p.sb(f"{n}sst", [128, 16])
    carry = p.sb(f"{n}carry", [128, 16])
    carry_in = p.sb(f"{n}cin", [128, 16])
    memf = [p.sb(f"{n}memf{i}", [128, TT]) for i in range(2)]
    loadW = p.sb(f"{n}ldw", [128, NDC * T], BF16)
    sg = p.sb(f"{n}sg", [128, 16 * TT])
    g_sb = [p.sb(f"{n}g{i}", [128, TT]) for i in range(2)]
    x2_st = [p.sb(f"{n}x2{i}", [128, 2048]) for i in range(2)]
    v_ps = p.ps(f"{n}vps", [128, TT])
    gp_ps = p.ps(f"{n}gps", [128, TT])
    q_ps = p.ps(f"{n}qps", [128, TT])
    pq = [p.ps(f"{n}pq{i}", [128, 512]) for i in range(2)]
    ld = p.sem(f"{n}ld")
    wg_s = p.sem(f"{n}wgs")
    mm = p.sem(f"{n}mm")
    va = p.sem(f"{n}va")
    sa = p.sem(f"{n}sa")
    so = p.sem(f"{n}so")
    go = p.sem(f"{n}go")
    cc = p.sem(f"{n}cc")

    p.op("sync", lambda e: e.dma_start(
        out=bgq[:, 0:16], in_=Bg.rearrange("(c p) -> p c", p=128),
        allow_slow_non_contiguous=True), inc=ld, inck=16)
    p.op("sync", lambda e: e.dma_start(
        out=bgq[:, 16:32], in_=Bq.rearrange("(c p) -> p c", p=128),
        allow_slow_non_contiguous=True), inc=ld, inck=16)
    p.op("sync", lambda e: e.dma_start(out=dpow[:], in_=dpow_hbm), inc=ld, inck=16)
    p.op("sync", lambda e: e.dma_start(out=ident_f[:], in_=ident_hbm), inc=ld, inck=16)
    p.op("vector", lambda e: e.memset(dec[:], 0.95), inc=va)
    p.op("vector", lambda e: e.memset(sstate[:], 0.0), inc=va)
    consts = [(ld, ld.n), (va, va.n)]

    def wblock(dst, src, mc, waits):
        return p.op("gpsimd", lambda e, dst=dst, src=src, mc=mc: e.dma_start(
            out=dst.ap().rearrange("p (c o) -> p c o", o=128),
            in_=src.rearrange("(c p) o -> p c o", p=128)[:, :, mc * 128:(mc + 1) * 128]),
            waits=waits, inc=wg_s, inck=16)

    wfree = {"v": [None, None], "g": [None, None], "q": [None, None], "o": [None, None]}
    xn_free = [None, None]
    vps_free = None; gps_free = None; qps_free = None
    sig_free = None; stc_free = None; memc_free = None
    sst_free = [None] * 16
    carry_last = None
    mem_st_last = None

    for tt in range(4):
        tb = tt % 2
        w = [xn_free[tb]] if xn_free[tb] else []
        xv = p.op("sync", lambda e, tt=tt, tb=tb: e.dma_start(
            out=xn_sb[tb].ap().rearrange("p (c t) -> p c t", t=TT),
            in_=xn_hbm.rearrange("(c p) t -> p c t", p=128)[:, :, tt * TT:(tt + 1) * TT]),
            waits=w, inc=ld, inck=16)
        xnv = (ld, xv)
        for mc in range(16):
            bi = mc % 2
            lv1 = wblock(wv_b[bi], Wv, mc, [wfree["v"][bi]] if wfree["v"][bi] else [])
            lv2 = wblock(wg_b[bi], Wg, mc, [wfree["g"][bi]] if wfree["g"][bi] else [])
            w = [(wg_s, lv1), xnv]
            if vps_free:
                w.append(vps_free)
            for dc in range(NDC):
                vv = p.op("tensor", lambda e, dc=dc, tb=tb, bi=bi: e.matmul(
                    v_ps[:], wv_b[bi][:, dc * 128:(dc + 1) * 128],
                    xn_sb[tb][:, dc * TT:(dc + 1) * TT],
                    start=(dc == 0), stop=(dc == NDC - 1)),
                    waits=w if dc == 0 else [], inc=mm)
            vmm = (mm, mm.n)
            w = [(wg_s, lv2), xnv]
            if gps_free:
                w.append(gps_free)
            for dc in range(NDC):
                gv = p.op("tensor", lambda e, dc=dc, tb=tb, bi=bi: e.matmul(
                    gp_ps[:], wg_b[bi][:, dc * 128:(dc + 1) * 128],
                    xn_sb[tb][:, dc * TT:(dc + 1) * TT],
                    start=(dc == 0), stop=(dc == NDC - 1)),
                    waits=w if dc == 0 else [], inc=mm)
            gmm = (mm, mm.n)
            wfree["v"][bi] = vmm
            wfree["g"][bi] = gmm
            w = [gmm, consts[0]]
            if sig_free:
                w.append(sig_free)
            sv = p.op("scalar", lambda e, mc=mc: e.activation(
                sig[:], gp_ps[:], AF.Sigmoid, bias=bgq[:, mc:mc + 1]), waits=w, inc=sa)
            gps_free = (sa, sv)
            w = [vmm, (sa, sv)]
            if stc_free:
                w.append(stc_free)
            stv = p.op("vector", lambda e: e.scalar_tensor_tensor(
                store_c[:], v_ps[:], S05, sig[:], ALU.mult, ALU.mult), waits=w, inc=va)
            vps_free = (va, stv)
            sig_free = (va, stv)
            w = [(va, stv), consts[1]]
            if memc_free:
                w.append(memc_free)
            if sst_free[mc]:
                w.append(sst_free[mc])
            scv = p.op("vector", lambda e, mc=mc: e.tensor_tensor_scan(
                mem_c[:], bass.AP(dec.ap().tensor, dec.ap().offset, [[1, 128], [0, TT]]),
                store_c[:], sstate[:, mc:mc + 1], ALU.mult, ALU.add),
                waits=w, inc=va)
            stc_free = (va, scv)
            uv = p.op("vector", lambda e, mc=mc: e.tensor_copy(
                sstate[:, mc:mc + 1], mem_c[:, TT - 1:TT]), waits=[(va, scv)], inc=va)
            sst_free[mc] = (va, uv)
            if tt == 3:
                cv = p.op("vector", lambda e, mc=mc: e.tensor_copy(
                    carry[:, mc:mc + 1], mem_c[:, TT - 1:TT]), waits=[(va, scv)], inc=va)
                carry_last = (va, cv)
            mv = p.op("gpsimd", lambda e, mc=mc, tt=tt: e.dma_start(
                out=mem_hbm[mc * 128:(mc + 1) * 128, tt * TT:(tt + 1) * TT],
                in_=mem_c[:]), waits=[(va, uv)], inc=go, inck=16)
            memc_free = (go, mv)
            mem_st_last = (go, mv)
        xn_free[tb] = (mm, mm.n)

    civ = p.op("gpsimd", lambda e: e.dma_start(
        out=bass.AP(cc_in.tensor, 0, [[1, 128], [128, 16]]), in_=carry[:],
        allow_slow_non_contiguous=True), waits=[carry_last], inc=go, inck=16)
    ccv = p.op("gpsimd", lambda e: e.collective_compute(
        "AllGather", ALU.bypass, replica_groups=[[0, 1], [2, 3], [4, 5], [6, 7]],
        ins=[cc_in.opt()], outs=[cc_out.opt()]), waits=[(go, go.n)], inc=cc, inck=1)
    cinv = p.op("gpsimd", lambda e: e.dma_start(
        out=carry_in[:], in_=bass.AP(cc_out.tensor, 0, [[1, 128], [128, 16]]),
        allow_slow_non_contiguous=True), waits=[(cc, ccv)], inc=go, inck=16)
    cin_v = (go, cinv)

    mem_free = [None, None]
    for tt in range(4):
        tb = tt % 2
        w = [xn_free[tb]] if xn_free[tb] else []
        xv = p.op("sync", lambda e, tt=tt, tb=tb: e.dma_start(
            out=xn_sb[tb].ap().rearrange("p (c t) -> p c t", t=TT),
            in_=xn_hbm.rearrange("(c p) t -> p c t", p=128)[:, :, tt * TT:(tt + 1) * TT]),
            waits=w, inc=ld, inck=16)
        xnv = (ld, xv)
        for mc in range(16):
            bi = mc % 2
            lq = wblock(wq_b[bi], Wq, mc, [wfree["q"][bi]] if wfree["q"][bi] else [])
            w = [mem_free[bi] if mem_free[bi] else None, mem_st_last]
            mlv = p.op("sync", lambda e, mc=mc, tt=tt, bi=bi: e.dma_start(
                out=memf[bi][:], in_=mem_hbm[mc * 128:(mc + 1) * 128, tt * TT:(tt + 1) * TT]),
                waits=w, inc=ld, inck=16)
            crv = p.op("vector", lambda e, mc=mc, tt=tt, bi=bi: e.scalar_tensor_tensor(
                memf[bi][:], dpow[:, tt * TT:(tt + 1) * TT], carry_in[:, mc:mc + 1],
                memf[bi][:], ALU.mult, ALU.add),
                waits=[(ld, mlv), cin_v, consts[0]], inc=va)
            w = [(wg_s, lq), xnv]
            if qps_free:
                w.append(qps_free)
            for dc in range(NDC):
                qv = p.op("tensor", lambda e, dc=dc, tb=tb, bi=bi: e.matmul(
                    q_ps[:], wq_b[bi][:, dc * 128:(dc + 1) * 128],
                    xn_sb[tb][:, dc * TT:(dc + 1) * TT],
                    start=(dc == 0), stop=(dc == NDC - 1)),
                    waits=w if dc == 0 else [], inc=mm)
            qmm = (mm, mm.n)
            wfree["q"][bi] = qmm
            w = [qmm, consts[0]]
            if sig_free:
                w.append(sig_free)
            sv = p.op("scalar", lambda e, mc=mc: e.activation(
                sig[:], q_ps[:], AF.Sigmoid, bias=bgq[:, 16 + mc:17 + mc]),
                waits=w, inc=sa)
            qps_free = (sa, sv)
            lwv = p.op("vector", lambda e, mc=mc, tt=tt, bi=bi: e.scalar_tensor_tensor(
                loadW[:, mc * T + tt * TT: mc * T + (tt + 1) * TT],
                memf[bi][:], S05, sig[:], ALU.mult, ALU.mult),
                waits=[(sa, sv), (va, crv)], inc=va)
            sig_free = (va, lwv)
            mem_free[bi] = (va, lwv)
        xn_free[tb] = (mm, mm.n)
    ldw_done = (va, va.n)

    g_free = [None, None]
    pq_free = [None, None]
    sg_free = None
    x2_free = [None, None]
    for tt in range(4):
        for ec in range(16):
            bi = ec % 2
            wo_l = wblock(wo_b[bi], Wo, ec, [wfree["o"][bi]] if wfree["o"][bi] else [])
            w = [g_free[bi]] if g_free[bi] else []
            gv = p.op("sync", lambda e, ec=ec, tt=tt, bi=bi: e.dma_start(
                out=g_sb[bi][:], in_=g1_hbm[ec * 128:(ec + 1) * 128, tt * TT:(tt + 1) * TT]),
                waits=w, inc=ld, inck=16)
            w = [ldw_done, (wg_s, wo_l)]
            if pq_free[bi]:
                w.append(pq_free[bi])
            for mc in range(NDC):
                vv = p.op("tensor", lambda e, mc=mc, tt=tt, bi=bi: e.matmul(
                    pq[bi][:], wo_b[bi][:, mc * 128:(mc + 1) * 128],
                    loadW[:, mc * T + tt * TT: mc * T + (tt + 1) * TT],
                    start=(mc == 0), stop=(mc == NDC - 1)),
                    waits=w if mc == 0 else [], inc=mm)
            omm = (mm, mm.n)
            wfree["o"][bi] = omm
            w = [omm, (ld, gv)]
            if ec < 2 and sg_free:
                w.append(sg_free)
            mv = p.op("vector", lambda e, ec=ec, bi=bi: e.tensor_mul(
                sg[:, ec * TT:(ec + 1) * TT], pq[bi][:], g_sb[bi][:]), waits=w, inc=va)
            pq_free[bi] = (va, mv)
            g_free[bi] = (va, mv)
        sg_done = (va, va.n)
        for qc in range(4):
            xb = qc % 2
            w = [x2_free[xb]] if x2_free[xb] else []
            xv_ = p.op("sync", lambda e, tt=tt, qc=qc, xb=xb: e.dma_start(
                out=x2_st[xb][:], in_=x1_hbm[tt * TT + qc * 128: tt * TT + qc * 128 + 128, :]),
                waits=w, inc=ld, inck=16)
            last_cp = None
            for ec in range(16):
                bi = ec % 2
                w = [sg_done, consts[0]]
                if pq_free[bi]:
                    w.append(pq_free[bi])
                tv = p.op("tensor", lambda e, qc=qc, ec=ec, bi=bi: e.transpose(
                    pq[bi][:, 0:128], sg[:, ec * TT + qc * 128: ec * TT + qc * 128 + 128],
                    ident_f[:]), waits=w, inc=mm)
                w = [(mm, tv), (ld, xv_)]
                cv = p.op("vector", lambda e, ec=ec, bi=bi, xb=xb: e.tensor_add(
                    x2_st[xb][:, ec * 128:(ec + 1) * 128],
                    x2_st[xb][:, ec * 128:(ec + 1) * 128], pq[bi][:, 0:128]),
                    waits=w, inc=va)
                pq_free[bi] = (va, cv)
                last_cp = cv
            stv = p.op("scalar", lambda e, tt=tt, qc=qc, xb=xb: e.dma_start(
                out=x2_hbm[tt * TT + qc * 128: tt * TT + qc * 128 + 128, :],
                in_=x2_st[xb][:]), waits=[(va, last_cp)], inc=so, inck=16)
            x2_free[xb] = [(so, stv)]
        sg_free = (va, va.n)
    return so


import ml_dtypes
from contextlib import ExitStack as _ES

_BF = ml_dtypes.bfloat16


def _cs(arr, c, fm=False):
    b, h = c // 2, c % 2
    sl = np.asarray(arr[b, h * 2048:(h + 1) * 2048])
    return np.ascontiguousarray(sl.T) if fm else np.ascontiguousarray(sl)


def _tr(a):
    return np.ascontiguousarray(np.asarray(a, np.float32).T)


def _tr3(a):
    return np.ascontiguousarray(np.asarray(a).transpose(0, 2, 1))


_BUILD_CACHE = {}


def _build(key, fn):
    if key not in _BUILD_CACHE:
        _BUILD_CACHE[key] = fn()
    return _BUILD_CACHE[key]


def _b_u1(mode, E):
    nc = bass.Bass("TRN2", target_bir_lowering=False, debug=False)
    x = nc.dram_tensor("x", [2048, 2048], F32, kind="ExternalInput").ap()
    gwT = nc.dram_tensor("gwT", [2048, E], F32, kind="ExternalInput").ap()
    ident = nc.dram_tensor("ident", [128, 128], F32, kind="ExternalInput").ap()
    xn = nc.dram_tensor("xn", [2048, 2048], BF16, kind="ExternalOutput").ap()
    dwT = nc.dram_tensor("dwT", [E, 2048], F32, kind="ExternalOutput").ap()
    with _ES() as ctx:
        p = P(nc, ctx)
        so = plan_ln_router(p, x, gwT, xn, dwT, ident, mode, E, name="u1_")
        p.wait("gpsimd", (so, so.n))
        p.emit()
    return nc


def _b_u2():
    nc = bass.Bass("TRN2", target_bir_lowering=False, debug=False)
    xn = nc.dram_tensor("xn", [2048, 2048], BF16, kind="ExternalInput").ap()
    Wn = dict(cqT=[2048, 512], cqcT=[512, 384], cqrT=[2048, 128], ckvT=[2048, 512],
              ckrT=[2048, 128], ckcT=[512, 384], cvT=[512, 512])
    W = {k: nc.dram_tensor(k, s, F32, kind="ExternalInput").ap() for k, s in Wn.items()}
    Bn = dict(cq_b=512, cqc_b=384, cqr_b=128, ckv_b=512, ckr_b=128, ckc_b=384, cv_b=512)
    B = {k: nc.dram_tensor(k, [s], F32, kind="ExternalInput").ap() for k, s in Bn.items()}
    cos = nc.dram_tensor("cos", [128, 4096], F32, kind="ExternalInput").ap()
    sin = nc.dram_tensor("sin", [128, 4096], F32, kind="ExternalInput").ap()
    ident = nc.dram_tensor("ident", [128, 128], F32, kind="ExternalInput").ap()
    QT = nc.dram_tensor("QT", [512, 2048], BF16, kind="ExternalOutput").ap()
    KT = nc.dram_tensor("KT", [512, 2048], BF16, kind="ExternalOutput").ap()
    VT = nc.dram_tensor("VT", [2048, 512], BF16, kind="ExternalOutput").ap()
    with _ES() as ctx:
        p = P(nc, ctx)
        so = plan_l0proj(p, xn, W, B, QT, KT, VT, cos, sin, ident,
                         q_tiles=[0, 1, 2, 3], kv_tiles=[0, 1, 2, 3], pos_off=0, name="p_")
        p.wait("scalar", (so, so.n))
        p.emit()
    return nc


def _b_u3():
    nc = bass.Bass("TRN2", target_bir_lowering=False, debug=False)
    xn = nc.dram_tensor("xn", [2048, 2048], BF16, kind="ExternalInput").ap()
    dwT = nc.dram_tensor("dwT", [16, 2048], F32, kind="ExternalInput").ap()
    w1t = nc.dram_tensor("w1t", [16, 2048, 1408], BF16, kind="ExternalInput").ap()
    w3t = nc.dram_tensor("w3t", [16, 2048, 1408], BF16, kind="ExternalInput").ap()
    w2t = nc.dram_tensor("w2t", [16, 1408, 2048], BF16, kind="ExternalInput").ap()
    sw1t = nc.dram_tensor("sw1t", [2048, 2816], BF16, kind="ExternalInput").ap()
    sw3t = nc.dram_tensor("sw3t", [2048, 2816], BF16, kind="ExternalInput").ap()
    sw2t = nc.dram_tensor("sw2t", [2816, 2048], BF16, kind="ExternalInput").ap()
    g = nc.dram_tensor("g", [2048, 2048], F32, kind="ExternalOutput").ap()
    with _ES() as ctx:
        p = P(nc, ctx)
        so = plan_moe(p, xn, dwT, w1t, w3t, w2t, sw1t, sw3t, sw2t, g, name="m_")
        p.wait("gpsimd", (so, so.n))
        p.emit()
    return nc


def _b_u4(Tk):
    nc = bass.Bass("TRN2", target_bir_lowering=False, debug=False)
    QT = nc.dram_tensor("QT", [512, 2048], BF16, kind="ExternalInput").ap()
    KT = nc.dram_tensor("KT", [512, Tk], BF16, kind="ExternalInput").ap()
    VT = nc.dram_tensor("VT", [Tk, 512], BF16, kind="ExternalInput").ap()
    g = nc.dram_tensor("g", [2048, 2048], F32, kind="ExternalInput").ap()
    x = nc.dram_tensor("x", [2048, 2048], F32, kind="ExternalInput").ap()
    W = {"outT": nc.dram_tensor("outT", [512, 2048], F32, kind="ExternalInput").ap(),
         "out_b": nc.dram_tensor("out_b", [2048], F32, kind="ExternalInput").ap(),
         "mask": nc.dram_tensor("mask", [128, 128], F32, kind="ExternalInput").ap(),
         "ident": nc.dram_tensor("ident", [128, 128], F32, kind="ExternalInput").ap()}
    x1 = nc.dram_tensor("x1", [2048, 2048], F32, kind="ExternalOutput").ap()
    with _ES() as ctx:
        p = P(nc, ctx)
        so = plan_l0attn(p, QT, KT, VT, g, x, W, x1, Tk, name="a_")
        p.wait("scalar", (so, so.n))
        p.emit()
    return nc


def _b_u6():
    nc = bass.Bass("TRN2", target_bir_lowering=False, debug=False)
    xn = nc.dram_tensor("xn", [2048, 2048], BF16, kind="ExternalInput").ap()
    w1T = nc.dram_tensor("w1T", [8, 2048], F32, kind="ExternalInput").ap()
    gateT = nc.dram_tensor("gateT", [8, 2048, 256], F32, kind="ExternalInput").ap()
    upT = nc.dram_tensor("upT", [8, 2048, 256], F32, kind="ExternalInput").ap()
    downT = nc.dram_tensor("downT", [8, 256, 2048], F32, kind="ExternalInput").ap()
    loraAT = nc.dram_tensor("loraAT", [8, 2048, 16], F32, kind="ExternalInput").ap()
    loraBT = nc.dram_tensor("loraBT", [8, 16, 2048], F32, kind="ExternalInput").ap()
    g1 = nc.dram_tensor("g1", [2048, 2048], F32, kind="ExternalOutput").ap()
    with _ES() as ctx:
        p = P(nc, ctx)
        so = plan_l1ffn(p, xn, w1T, gateT, upT, downT, loraAT, loraBT, g1, name="f_")
        p.wait("scalar", (so, so.n))
        p.emit()
    return nc


def _b_u7():
    nc = bass.Bass("TRN2", target_bir_lowering=False, debug=False)
    xn = nc.dram_tensor("xn", [2048, 2048], BF16, kind="ExternalInput").ap()
    Wv = nc.dram_tensor("Wv", [2048, 2048], F32, kind="ExternalInput").ap()
    Wg = nc.dram_tensor("Wg", [2048, 2048], F32, kind="ExternalInput").ap()
    Wq = nc.dram_tensor("Wq", [2048, 2048], F32, kind="ExternalInput").ap()
    Wo = nc.dram_tensor("Wo", [2048, 2048], F32, kind="ExternalInput").ap()
    Bg = nc.dram_tensor("Bg", [2048], F32, kind="ExternalInput").ap()
    Bq = nc.dram_tensor("Bq", [2048], F32, kind="ExternalInput").ap()
    dpow = nc.dram_tensor("dpow", [128, 2048], F32, kind="ExternalInput").ap()
    g1 = nc.dram_tensor("g1", [2048, 2048], F32, kind="ExternalInput").ap()
    x1 = nc.dram_tensor("x1", [2048, 2048], F32, kind="ExternalInput").ap()
    ident = nc.dram_tensor("ident", [128, 128], F32, kind="ExternalInput").ap()
    mem = nc.dram_tensor("memscr", [2048, 2048], F32, kind="Internal").ap()
    cc_in = nc.dram_tensor("cc_in", [1, 2048], F32, kind="Internal").ap()
    cc_out = nc.dram_tensor("cc_out", [2, 2048], F32, kind="Internal").ap()
    x2 = nc.dram_tensor("x2", [2048, 2048], F32, kind="ExternalOutput").ap()
    with _ES() as ctx:
        p = P(nc, ctx)
        so = plan_l1attn(p, xn, Wv, Wg, Wq, Wo, Bg, Bq, dpow, g1, x1,
                         mem, cc_in, cc_out, x2, ident, name="r_")
        p.wait("scalar", (so, so.n))
        p.emit()
    return nc


def kernel_multi(**inp):
    ident = np.eye(128, dtype=np.float32)
    inv_freq = 1.0 / 10000.0 ** (np.arange(0, 128, 2, dtype=np.float32) / 128)
    t_ = np.arange(4096, dtype=np.float32)
    emb = np.concatenate([np.outer(t_, inv_freq)] * 2, axis=1)
    cosT = np.ascontiguousarray(np.cos(emb).T.astype(np.float32))
    sinT = np.sin(emb).T.astype(np.float32)
    sinT[0:64] = -sinT[0:64]
    sinT = np.ascontiguousarray(sinT)
    cos_sh = [cosT, np.ascontiguousarray(np.roll(cosT, -2048, axis=1))]
    sin_sh = [sinT, np.ascontiguousarray(np.roll(sinT, -2048, axis=1))]
    mask = np.where(np.arange(128)[None, :] <= np.arange(128)[:, None],
                    np.float32(0), np.float32(-1e30))
    dpow1 = np.tile((0.95 ** (np.arange(2048, dtype=np.float64) + 1)
                     ).astype(np.float32), (128, 1))
    dpow0 = np.zeros((128, 2048), np.float32)
    cores = list(range(8))

    nc = _build("fA", _b_fusedA)
    gwT = _tr(inp["L0_ffn_gate_w"])
    w1t = _tr3(inp["L0_ffn_w1"]); w3t = _tr3(inp["L0_ffn_w3"]); w2t = _tr3(inp["L0_ffn_w2"])
    sw1t = np.ascontiguousarray(np.asarray(inp["L0_ffn_sw1"]).T)
    sw3t = np.ascontiguousarray(np.asarray(inp["L0_ffn_sw3"]).T)
    sw2t = np.ascontiguousarray(np.asarray(inp["L0_ffn_sw2"]).T)
    Wm = dict(cqT=_tr(inp["L0_attn_CQ_w"]), cqcT=_tr(inp["L0_attn_CQC_w"]),
              cqrT=_tr(inp["L0_attn_CQR_w"]), ckvT=_tr(inp["L0_attn_CKV_w"]),
              ckrT=_tr(inp["L0_attn_CKR_w"]), ckcT=_tr(inp["L0_attn_CKC_w"]),
              cvT=_tr(inp["L0_attn_CV_w"]))
    Bm = dict(cq_b=inp["L0_attn_CQ_b"], cqc_b=inp["L0_attn_CQC_b"],
              cqr_b=inp["L0_attn_CQR_b"], ckv_b=inp["L0_attn_CKV_b"],
              ckr_b=inp["L0_attn_CKR_b"], ckc_b=inp["L0_attn_CKC_b"],
              cv_b=inp["L0_attn_CV_b"])
    cosA = [np.ascontiguousarray(np.roll(cos_sh[0], 2048, axis=1)), cos_sh[0]]
    sinA = [np.ascontiguousarray(np.roll(sin_sh[0], 2048, axis=1)), sin_sh[0]]
    x_sh = [_cs(inp["L0_attn_x"], c) for c in cores]
    xfull = []
    for c in cores:
        b, h = c // 2, c % 2
        if h == 1:
            xf = np.ascontiguousarray(np.asarray(inp["L0_attn_x"][b]))
        else:
            xf = np.ascontiguousarray(np.concatenate([x_sh[c], x_sh[c]], axis=0))
        xfull.append(xf)
    ims = []
    for c in cores:
        im = {"x": xfull[c], "gwT": gwT, "ident": ident,
              "cos": cosA[c % 2], "sin": sinA[c % 2],
              "w1t": w1t, "w3t": w3t, "w2t": w2t,
              "sw1t": sw1t, "sw3t": sw3t, "sw2t": sw2t}
        im.update(Wm); im.update(Bm)
        ims.append(im)
    r = run_bass_kernel_spmd(nc, ims, core_ids=cores)
    QT = [r.results[c]["QT"] for c in cores]
    KTf = [r.results[c]["KT"] for c in cores]
    VTf = [r.results[c]["VT"] for c in cores]
    g0 = [r.results[c]["g"] for c in cores]

    outT = _tr(inp["L0_attn_OUT_w"])
    out_b = np.asarray(inp["L0_attn_OUT_b"])
    x1 = [None] * 8
    for h, cset in ((0, [0, 2, 4, 6]), (1, [1, 3, 5, 7])):
        Tk = 2048 * (h + 1)
        nc = _build(("u4", Tk), lambda Tk=Tk: _b_u4(Tk))
        ims = []
        for c in cset:
            if h == 0:
                Kf = np.ascontiguousarray(KTf[c][:, 2048:])
                Vf = np.ascontiguousarray(VTf[c][2048:, :])
            else:
                Kf, Vf = KTf[c], VTf[c]
            ims.append({"QT": QT[c], "KT": Kf, "VT": Vf, "g": g0[c], "x": x_sh[c],
                        "outT": outT, "out_b": out_b, "mask": mask, "ident": ident})
        r = run_bass_kernel_spmd(nc, ims, core_ids=cset)
        for i, c in enumerate(cset):
            x1[c] = r.results[i]["x1"]

    nc = _build("u1l1", lambda: _b_u1("l1", 8))
    GT = np.ascontiguousarray(np.asarray(inp["L1_ffn_G"], np.float32))
    r = run_bass_kernel_spmd(nc, [{"x": x1[c], "gwT": GT, "ident": ident}
                                  for c in cores], core_ids=cores)
    xn1 = [r.results[c]["xn"] for c in cores]
    w1T = [r.results[c]["dwT"] for c in cores]

    nc = _build("u6", _b_u6)
    gateT = _tr3(inp["L1_ffn_gate_w"]); upT = _tr3(inp["L1_ffn_up_w"])
    downT = _tr3(inp["L1_ffn_down_w"])
    loraAT = _tr3(inp["L1_ffn_lora_A"]); loraBT = _tr3(inp["L1_ffn_lora_B"])
    r = run_bass_kernel_spmd(nc, [
        {"xn": xn1[c], "w1T": w1T[c], "gateT": gateT, "upT": upT, "downT": downT,
         "loraAT": loraAT, "loraBT": loraBT} for c in cores], core_ids=cores)
    g1 = [r.results[c]["g1"] for c in cores]

    nc = _build("u7", _b_u7)
    com = {"Wv": np.asarray(inp["L1_attn_Wv"]), "Wg": np.asarray(inp["L1_attn_Wg"]),
           "Wq": np.asarray(inp["L1_attn_Wq"]), "Wo": np.asarray(inp["L1_attn_Wo"]),
           "Bg": np.asarray(inp["L1_attn_bg"]), "Bq": np.asarray(inp["L1_attn_bq"]),
           "ident": ident}
    ims = []
    for c in cores:
        im = {"xn": xn1[c], "g1": g1[c], "x1": x1[c],
              "dpow": dpow1 if c % 2 else dpow0}
        im.update(com)
        ims.append(im)
    r = run_bass_kernel_spmd(nc, ims, core_ids=cores)
    out = np.empty((4, 4096, 2048), np.float32)
    for c in cores:
        b, h = c // 2, c % 2
        out[b, h * 2048:(h + 1) * 2048] = r.results[c]["x2"]
    return out


def _b_fusedA():
    nc = bass.Bass("TRN2", target_bir_lowering=False, debug=False)
    x = nc.dram_tensor("x", [4096, 2048], F32, kind="ExternalInput").ap()
    gwT = nc.dram_tensor("gwT", [2048, 16], F32, kind="ExternalInput").ap()
    ident = nc.dram_tensor("ident", [128, 128], F32, kind="ExternalInput").ap()
    Wn = dict(cqT=[2048, 512], cqcT=[512, 384], cqrT=[2048, 128], ckvT=[2048, 512],
              ckrT=[2048, 128], ckcT=[512, 384], cvT=[512, 512])
    W = {k: nc.dram_tensor(k, s, F32, kind="ExternalInput").ap() for k, s in Wn.items()}
    Bn = dict(cq_b=512, cqc_b=384, cqr_b=128, ckv_b=512, ckr_b=128, ckc_b=384, cv_b=512)
    B = {k: nc.dram_tensor(k, [s], F32, kind="ExternalInput").ap() for k, s in Bn.items()}
    cos = nc.dram_tensor("cos", [128, 4096], F32, kind="ExternalInput").ap()
    sin = nc.dram_tensor("sin", [128, 4096], F32, kind="ExternalInput").ap()
    w1t = nc.dram_tensor("w1t", [16, 2048, 1408], BF16, kind="ExternalInput").ap()
    w3t = nc.dram_tensor("w3t", [16, 2048, 1408], BF16, kind="ExternalInput").ap()
    w2t = nc.dram_tensor("w2t", [16, 1408, 2048], BF16, kind="ExternalInput").ap()
    sw1t = nc.dram_tensor("sw1t", [2048, 2816], BF16, kind="ExternalInput").ap()
    sw3t = nc.dram_tensor("sw3t", [2048, 2816], BF16, kind="ExternalInput").ap()
    sw2t = nc.dram_tensor("sw2t", [2816, 2048], BF16, kind="ExternalInput").ap()
    xn = nc.dram_tensor("xn", [2048, 4096], BF16, kind="Internal").ap()
    dwT = nc.dram_tensor("dwT", [16, 4096], F32, kind="Internal").ap()
    QT = nc.dram_tensor("QT", [512, 2048], BF16, kind="ExternalOutput").ap()
    KT = nc.dram_tensor("KT", [512, 4096], BF16, kind="ExternalOutput").ap()
    VT = nc.dram_tensor("VT", [4096, 512], BF16, kind="ExternalOutput").ap()
    g = nc.dram_tensor("g", [2048, 2048], F32, kind="ExternalOutput").ap()
    with _ES() as ctx:
        p = P(nc, ctx)
        p.push_scope()
        plan_ln_router(p, x, gwT, xn, dwT, ident, "l0", 16, T=4096, name="u1_")
        p.pop_scope()
        p.barrier()
        p.push_scope()
        plan_l0proj(p, xn, W, B, QT, KT, VT, cos, sin, ident,
                    q_tiles=[4, 5, 6, 7], kv_tiles=[0, 1, 2, 3, 4, 5, 6, 7],
                    pos_off=0, name="p_")
        p.pop_scope()
        p.barrier()
        p.push_scope()
        so = plan_moe(p, xn, dwT, w1t, w3t, w2t, sw1t, sw3t, sw2t, g,
                      name="m_", t_off=2048)
        p.pop_scope()
        p.barrier()
        p.emit()
    return nc


def _b_fused():
    nc = bass.Bass("TRN2", target_bir_lowering=False, debug=False)
    x = nc.dram_tensor("x", [4096, 2048], F32, kind="ExternalInput").ap()
    gwT = nc.dram_tensor("gwT", [2048, 16], F32, kind="ExternalInput").ap()
    ident = nc.dram_tensor("ident", [128, 128], F32, kind="ExternalInput").ap()
    Wn = dict(cqT=[2048, 512], cqcT=[512, 384], cqrT=[2048, 128], ckvT=[2048, 512],
              ckrT=[2048, 128], ckcT=[512, 384], cvT=[512, 512])
    W = {k: nc.dram_tensor(k, s, F32, kind="ExternalInput").ap() for k, s in Wn.items()}
    Bn = dict(cq_b=512, cqc_b=384, cqr_b=128, ckv_b=512, ckr_b=128, ckc_b=384, cv_b=512)
    B = {k: nc.dram_tensor(k, [s], F32, kind="ExternalInput").ap() for k, s in Bn.items()}
    cos = nc.dram_tensor("cos", [128, 4096], F32, kind="ExternalInput").ap()
    sin = nc.dram_tensor("sin", [128, 4096], F32, kind="ExternalInput").ap()
    w1t = nc.dram_tensor("w1t", [16, 2048, 1408], BF16, kind="ExternalInput").ap()
    w3t = nc.dram_tensor("w3t", [16, 2048, 1408], BF16, kind="ExternalInput").ap()
    w2t = nc.dram_tensor("w2t", [16, 1408, 2048], BF16, kind="ExternalInput").ap()
    sw1t = nc.dram_tensor("sw1t", [2048, 2816], BF16, kind="ExternalInput").ap()
    sw3t = nc.dram_tensor("sw3t", [2048, 2816], BF16, kind="ExternalInput").ap()
    sw2t = nc.dram_tensor("sw2t", [2816, 2048], BF16, kind="ExternalInput").ap()
    outTw = nc.dram_tensor("outT", [512, 2048], F32, kind="ExternalInput").ap()
    out_b = nc.dram_tensor("out_b", [2048], F32, kind="ExternalInput").ap()
    maskT = nc.dram_tensor("mask", [128, 128], F32, kind="ExternalInput").ap()
    pmask = nc.dram_tensor("pmask", [1, 4096], F32, kind="ExternalInput").ap()
    GT = nc.dram_tensor("GT", [2048, 8], F32, kind="ExternalInput").ap()
    gateT = nc.dram_tensor("gateT", [8, 2048, 256], F32, kind="ExternalInput").ap()
    upT = nc.dram_tensor("upT", [8, 2048, 256], F32, kind="ExternalInput").ap()
    downT = nc.dram_tensor("downT", [8, 256, 2048], F32, kind="ExternalInput").ap()
    loraAT = nc.dram_tensor("loraAT", [8, 2048, 16], F32, kind="ExternalInput").ap()
    loraBT = nc.dram_tensor("loraBT", [8, 16, 2048], F32, kind="ExternalInput").ap()
    Wv = nc.dram_tensor("Wv", [2048, 2048], F32, kind="ExternalInput").ap()
    Wg = nc.dram_tensor("Wg", [2048, 2048], F32, kind="ExternalInput").ap()
    Wq = nc.dram_tensor("Wq", [2048, 2048], F32, kind="ExternalInput").ap()
    Wo = nc.dram_tensor("Wo", [2048, 2048], F32, kind="ExternalInput").ap()
    Bg = nc.dram_tensor("Bg", [2048], F32, kind="ExternalInput").ap()
    Bq = nc.dram_tensor("Bq", [2048], F32, kind="ExternalInput").ap()
    dpow = nc.dram_tensor("dpow", [128, 2048], F32, kind="ExternalInput").ap()
    xn = nc.dram_tensor("xn", [2048, 4096], BF16, kind="Internal").ap()
    dwT = nc.dram_tensor("dwT", [16, 4096], F32, kind="Internal").ap()
    QT = nc.dram_tensor("QT", [512, 2048], BF16, kind="Internal").ap()
    KT = nc.dram_tensor("KT", [512, 4096], BF16, kind="Internal").ap()
    VT = nc.dram_tensor("VT", [4096, 512], BF16, kind="Internal").ap()
    g0 = nc.dram_tensor("g0", [2048, 2048], F32, kind="Internal").ap()
    x1 = nc.dram_tensor("x1", [2048, 2048], F32, kind="Internal").ap()
    xn1 = nc.dram_tensor("xn1", [2048, 2048], BF16, kind="Internal").ap()
    w1T = nc.dram_tensor("w1T", [8, 2048], F32, kind="Internal").ap()
    g1 = nc.dram_tensor("g1", [2048, 2048], F32, kind="Internal").ap()
    mem = nc.dram_tensor("memscr", [2048, 2048], F32, kind="Internal").ap()
    cc_in = nc.dram_tensor("cc_in", [1, 2048], F32, kind="Internal").ap()
    cc_out = nc.dram_tensor("cc_out", [2, 2048], F32, kind="Internal").ap()
    x2 = nc.dram_tensor("x2", [2048, 2048], F32, kind="ExternalOutput").ap()
    with _ES() as ctx:
        p = P(nc, ctx)
        p.push_scope()
        plan_ln_router(p, x, gwT, xn, dwT, ident, "l0", 16, T=4096, name="u1_")
        p.pop_scope(); p.barrier()
        p.push_scope()
        plan_l0proj(p, xn, W, B, QT, KT, VT, cos, sin, ident,
                    q_tiles=[4, 5, 6, 7], kv_tiles=[0, 1, 2, 3, 4, 5, 6, 7],
                    pos_off=0, name="p_")
        p.pop_scope(); p.barrier()
        p.push_scope()
        plan_moe(p, xn, dwT, w1t, w3t, w2t, sw1t, sw3t, sw2t, g0, name="m_", t_off=2048)
        p.pop_scope(); p.barrier()
        p.push_scope()
        plan_l0attn(p, QT, KT, VT, g0, x[2048:4096, :],
                    {"outT": outTw, "out_b": out_b, "mask": maskT, "ident": ident},
                    x1, 4096, name="a_", pmask_hbm=pmask)
        p.pop_scope(); p.barrier()
        p.push_scope()
        plan_ln_router(p, x1, GT, xn1, w1T, ident, "l1", 8, T=2048, name="u5_")
        p.pop_scope(); p.barrier()
        p.push_scope()
        plan_l1ffn(p, xn1, w1T, gateT, upT, downT, loraAT, loraBT, g1, name="f_")
        p.pop_scope(); p.barrier()
        p.push_scope()
        so = plan_l1attn(p, xn1, Wv, Wg, Wq, Wo, Bg, Bq, dpow, g1, x1,
                         mem, cc_in, cc_out, x2, ident, name="r_")
        p.pop_scope()
        p.wait("scalar", (so, so.n))
        p.barrier()
        p.emit()
    return nc


def kernel(**inp):
    ident = np.eye(128, dtype=np.float32)
    inv_freq = 1.0 / 10000.0 ** (np.arange(0, 128, 2, dtype=np.float32) / 128)
    t_ = np.arange(4096, dtype=np.float32)
    emb = np.concatenate([np.outer(t_, inv_freq)] * 2, axis=1)
    cosT = np.ascontiguousarray(np.cos(emb).T.astype(np.float32))
    sinT = np.sin(emb).T.astype(np.float32)
    sinT[0:64] = -sinT[0:64]
    sinT = np.ascontiguousarray(sinT)
    cosA = [np.ascontiguousarray(np.roll(cosT, 2048, axis=1)), cosT]
    sinA = [np.ascontiguousarray(np.roll(sinT, 2048, axis=1)), sinT]
    mask = np.where(np.arange(128)[None, :] <= np.arange(128)[:, None],
                    np.float32(0), np.float32(-1e30))
    pmask0 = np.concatenate([np.full((1, 2048), -1e30, np.float32),
                             np.zeros((1, 2048), np.float32)], axis=1)
    pmask1 = np.zeros((1, 4096), np.float32)
    dpow1 = np.tile((0.95 ** (np.arange(2048, dtype=np.float64) + 1)
                     ).astype(np.float32), (128, 1))
    dpow0 = np.zeros((128, 2048), np.float32)
    cores = list(range(8))
    nc = _build("fused", _b_fused)
    com = {"gwT": _tr(inp["L0_ffn_gate_w"]), "ident": ident,
           "w1t": _tr3(inp["L0_ffn_w1"]), "w3t": _tr3(inp["L0_ffn_w3"]),
           "w2t": _tr3(inp["L0_ffn_w2"]),
           "sw1t": np.ascontiguousarray(np.asarray(inp["L0_ffn_sw1"]).T),
           "sw3t": np.ascontiguousarray(np.asarray(inp["L0_ffn_sw3"]).T),
           "sw2t": np.ascontiguousarray(np.asarray(inp["L0_ffn_sw2"]).T),
           "cqT": _tr(inp["L0_attn_CQ_w"]), "cqcT": _tr(inp["L0_attn_CQC_w"]),
           "cqrT": _tr(inp["L0_attn_CQR_w"]), "ckvT": _tr(inp["L0_attn_CKV_w"]),
           "ckrT": _tr(inp["L0_attn_CKR_w"]), "ckcT": _tr(inp["L0_attn_CKC_w"]),
           "cvT": _tr(inp["L0_attn_CV_w"]),
           "cq_b": inp["L0_attn_CQ_b"], "cqc_b": inp["L0_attn_CQC_b"],
           "cqr_b": inp["L0_attn_CQR_b"], "ckv_b": inp["L0_attn_CKV_b"],
           "ckr_b": inp["L0_attn_CKR_b"], "ckc_b": inp["L0_attn_CKC_b"],
           "cv_b": inp["L0_attn_CV_b"],
           "outT": _tr(inp["L0_attn_OUT_w"]), "out_b": np.asarray(inp["L0_attn_OUT_b"]),
           "mask": mask,
           "GT": np.ascontiguousarray(np.asarray(inp["L1_ffn_G"], np.float32)),
           "gateT": _tr3(inp["L1_ffn_gate_w"]), "upT": _tr3(inp["L1_ffn_up_w"]),
           "downT": _tr3(inp["L1_ffn_down_w"]),
           "loraAT": _tr3(inp["L1_ffn_lora_A"]), "loraBT": _tr3(inp["L1_ffn_lora_B"]),
           "Wv": np.asarray(inp["L1_attn_Wv"]), "Wg": np.asarray(inp["L1_attn_Wg"]),
           "Wq": np.asarray(inp["L1_attn_Wq"]), "Wo": np.asarray(inp["L1_attn_Wo"]),
           "Bg": np.asarray(inp["L1_attn_bg"]), "Bq": np.asarray(inp["L1_attn_bq"])}
    x_sh = [_cs(inp["L0_attn_x"], c) for c in cores]
    ims = []
    for c in cores:
        b, h = c // 2, c % 2
        if h == 1:
            xf = np.ascontiguousarray(np.asarray(inp["L0_attn_x"][b]))
        else:
            xf = np.ascontiguousarray(np.concatenate([x_sh[c], x_sh[c]], axis=0))
        im = {"x": xf, "cos": cosA[h], "sin": sinA[h],
              "pmask": pmask1 if h else pmask0, "dpow": dpow1 if h else dpow0}
        im.update(com)
        ims.append(im)
    r = run_bass_kernel_spmd(nc, ims, core_ids=cores)
    out = np.empty((4, 4096, 2048), np.float32)
    for c in cores:
        b, h = c // 2, c % 2
        out[b, h * 2048:(h + 1) * 2048] = r.results[c]["x2"]
    return out
